# version 20
# baseline (speedup 1.0000x reference)
import numpy as np
import ml_dtypes
import concourse.bass as bass
import concourse.mybir as mybir
from concourse.bass_utils import run_bass_kernel_spmd

F32 = mybir.dt.float32
BF16 = mybir.dt.bfloat16
AF = mybir.ActivationFunctionType
ALU = mybir.AluOpType
AX = mybir.AxisListType

T = 2048
D = 1024
NCH = 8
NTT = 16
KV_RANK = 256
MLA_SCALE = 1.0 / float(np.sqrt(96.0))
HID = 2730
NHC = 22
WINDOW = 512
QK_SCALE = 8.0
NEG = -1.0e30

_DEV_UPTO = "full"
_DEV_NTT = 16
_DEV_P0_LEVEL = 4


def _host_constants():
    c = {}
    c["c_id_bf"] = np.eye(128, dtype=ml_dtypes.bfloat16)
    c["c_id_f32"] = np.eye(128, dtype=np.float32)
    low = np.where(np.arange(128)[None, :] < np.arange(128)[:, None], NEG, 0.0).astype(np.float32)
    c["c_mask_low"] = low.astype(ml_dtypes.bfloat16)
    c["c_mask_up"] = np.ascontiguousarray(low.T).astype(ml_dtypes.bfloat16)
    c["c_ones_row"] = np.ones((1, 128), dtype=np.float32)
    c["c_ones1"] = np.ones((128, 1), dtype=np.float32)
    bo2 = np.zeros((128, 2), dtype=np.float32)
    bo2[0:64, 0] = 1.0
    bo2[64:128, 1] = 1.0
    c["c_bones2"] = bo2.astype(ml_dtypes.bfloat16)
    bo4 = np.zeros((128, 4), dtype=np.float32)
    for g in range(4):
        bo4[g * 32:(g + 1) * 32, g] = 1.0
    c["c_bones4"] = bo4.astype(ml_dtypes.bfloat16)

    def tokmaj(tab, d):
        return np.ascontiguousarray(
            tab.reshape(T // 128, 128, d).transpose(1, 0, 2).reshape(128, -1)
        ).astype(np.float32)

    t = np.arange(T, dtype=np.float64)
    dim = 32
    inv = 1.0 / (10000.0 ** (np.arange(0, dim, 2, dtype=np.float64) / dim))
    fr = np.concatenate([np.outer(t, inv)] * 2, axis=1)
    sgn = np.ones(dim)
    sgn[: dim // 2] = -1.0
    c["c_rope_cos"] = tokmaj(np.cos(fr), dim)
    c["c_rope_sin"] = tokmaj(np.sin(fr) * sgn[None, :], dim)
    dh = 64
    half = np.arange(0, dh, 2, dtype=np.float64)
    invf = 1.0 / (10000.0 ** (half / dh))
    fr = np.concatenate([np.outer(t, invf)] * 2, axis=1)
    base = (half + 0.4 * dh) / (1.4 * dh)
    power = (t - T // 2) / float(WINDOW // 2)
    scale = np.concatenate([base[None, :] ** power[:, None]] * 2, axis=1)
    cosx, sinx = np.cos(fr), np.sin(fr)
    sgn2 = np.ones(dh)
    sgn2[: dh // 2] = -1.0
    c["c_xp_cq"] = tokmaj(QK_SCALE * cosx * scale, dh)
    c["c_xp_sq"] = tokmaj(QK_SCALE * sinx * scale * sgn2[None, :], dh)
    c["c_xp_ck"] = tokmaj(cosx / scale, dh)
    c["c_xp_sk"] = tokmaj(sinx / scale * sgn2[None, :], dh)
    return c


def _flat(waits):
    out = []
    for w in waits:
        if w is None:
            continue
        if isinstance(w, list):
            out.extend(_flat(w))
        else:
            out.append(w)
    return out


class Prog:
    ENGINES = ("tensor", "vector", "scalar", "gpsimd", "sync")

    def __init__(self):
        self.items = {e: [] for e in self.ENGINES}
        self.semv = {}
        self.waited = {e: {} for e in self.ENGINES}
        self.sem_names = set()

    def emit(self, eng, fn, waits=(), inc=None, amt=1):
        for s, v in _flat(list(waits)):
            if self.waited[eng].get(s, 0) < v:
                self.waited[eng][s] = v
                self.items[eng].append(("w", s, v))
        tok = None
        if inc is not None:
            self.sem_names.add(inc)
            self.semv[inc] = self.semv.get(inc, 0) + amt
            tok = (inc, self.semv[inc])
        self.items[eng].append(("i", fn, inc, amt))
        return tok

    def t(self, fn, waits=(), inc="pe"):
        return self.emit("tensor", fn, waits, inc)

    def v(self, fn, waits=(), inc="dve"):
        return self.emit("vector", fn, waits, inc)

    def s(self, fn, waits=(), inc="act"):
        return self.emit("scalar", fn, waits, inc)

    def g(self, fn, waits=(), inc="pool"):
        return self.emit("gpsimd", fn, waits, inc)

    def gdma(self, fn, waits=(), chain="g"):
        return self.emit("gpsimd", fn, waits, "D_" + chain, amt=16)

    def ydma(self, fn, waits=(), chain="y"):
        return self.emit("sync", fn, waits, "D_" + chain, amt=16)

    def barrier(self):
        toks = [(s, v) for s, v in self.semv.items() if v > 0]
        for eng in self.ENGINES:
            for s, v in toks:
                if self.waited[eng].get(s, 0) < v:
                    self.waited[eng][s] = v
                    self.items[eng].append(("w", s, v))

    def chain_total(self, chain):
        s = "D_" + chain
        v = self.semv.get(s, 0)
        return (s, v) if v > 0 else None

    def replay(self, nc, block):
        sems = {n: nc.alloc_semaphore(name=n) for n in sorted(self.sem_names)}

        def runner(name):
            def run(e):
                for it in self.items[name]:
                    if it[0] == "w":
                        e.wait_ge(sems[it[1]], it[2])
                    else:
                        _, fn, inc, amt = it
                        inst = fn(e)
                        if inc is not None:
                            inst.then_inc(sems[inc], amt)
            return run

        block.tensor(runner("tensor"))
        block.vector(runner("vector"))
        block.scalar(runner("scalar"))
        block.gpsimd(runner("gpsimd"))
        block.sync(runner("sync"))


class Arena:
    def __init__(self, nc):
        self.nc = nc
        base = (nc.sbuf_base + 63) & ~63
        self.free_list = [(base, nc.sbuf_top)]
        self.live = {}
        self.peak = 0

    def alloc(self, name, shape, dtype, side="left"):
        per = int(np.prod(shape[1:])) * mybir.dt.size(dtype)
        per = (per + 63) & ~63
        fits = [(hi - lo, i) for i, (lo, hi) in enumerate(self.free_list) if hi - lo >= per]
        fits.sort()
        if side == "right" and fits:
            fits = [max(fits, key=lambda t: self.free_list[t[1]][1])]
        for _, i in fits[:1]:
            lo, hi = self.free_list[i]
            if side == "right":
                base = hi - per
                if base == lo:
                    del self.free_list[i]
                else:
                    self.free_list[i] = (lo, base)
            else:
                base = lo
                if lo + per == hi:
                    del self.free_list[i]
                else:
                    self.free_list[i] = (lo + per, hi)
            h = self.nc.alloc_sbuf_tensor_at(name, list(shape), dtype, offset=base)
            self.live[name] = (base, base + per)
            self.peak = max(self.peak, self.used())
            return h
        raise MemoryError(f"SBUF arena: no room for {name} ({per}B/part); free={self.free_list}")

    def used(self):
        return sum(b - a for a, b in self.live.values())

    def free(self, h):
        name = h if isinstance(h, str) else h.name
        lo, hi = self.live.pop(name)
        self.free_list.append((lo, hi))
        self.free_list.sort()
        m = []
        for a, b in self.free_list:
            if m and m[-1][1] == a:
                m[-1] = (m[-1][0], b)
            else:
                m.append((a, b))
        self.free_list = m


class _EngShim:

    def __init__(self, e):
        self._e = e

    def __getattr__(self, k):
        return getattr(self._e, k)

    def tensor_copy(self, out, in_):
        if hasattr(self._e, "tensor_copy"):
            return self._e.tensor_copy(out=out, in_=in_)
        return self._e.copy(out=out, in_=in_)


def vap(h, col_off, dims):
    a = h if isinstance(h, bass.AP) else h.ap()
    return bass.AP(a.tensor, a.offset + col_off, [list(a.ap[0])] + [list(d) for d in dims])



def build_program(upto="full"):
    nc = bass.Bass("TRN2", target_bir_lowering=False)
    p = Prog()
    ar = Arena(nc)

    x_in = nc.dram_tensor("x", [T, D], F32, kind="ExternalInput").ap()
    out_t = nc.dram_tensor("out", [T, D], F32, kind="ExternalOutput").ap()

    wshapes = {
        "W_kv": [D, 256], "g_kv": [1, 256],
        "W_kdec": [256, D], "W_vdec": [256, D],
        "W_q": [D, 512], "W_qdec": [512, D],
        "W_krope": [D, 512], "W_qrope": [512, 512],
        "W_o": [D, D],
        "wqkv": [2, D, 1536], "q_scale": [2, 64], "k_scale": [2, 64],
        "wo": [2, 512, D], "ln_g": [2, D], "ln_b": [2, D],
        "w1": [2, D, 2 * HID], "b1": [2, 2 * HID], "w2": [2, HID, D], "b2": [2, D],
    }
    W = {k: nc.dram_tensor(k, v, F32, kind="ExternalInput").ap() for k, v in wshapes.items()}

    cdefs = _host_constants()
    C = {}
    for k, arr in cdefs.items():
        dt = BF16 if arr.dtype == ml_dtypes.bfloat16 else F32
        C[k] = nc.dram_tensor(k, list(arr.shape), dt, kind="ExternalInput").ap()

    s_hbm = nc.dram_tensor("s_scratch", [D, T], F32, kind="Internal").ap()
    kh_hbm = nc.dram_tensor("kh_scratch", [16, 96, T], BF16, kind="Internal").ap()
    qh_hbm = nc.dram_tensor("qh_scratch", [16, 96, T], BF16, kind="Internal").ap()

    P1 = nc.alloc_psum_tensor("P1", [128, 2048], F32)
    P2 = nc.alloc_psum_tensor("P2", [128, 1024], F32)
    P3 = nc.alloc_psum_tensor("P3", [128, 1024], F32)

    psum_wr = {b: None for b in range(8)}
    psum_rd = {b: [] for b in range(8)}

    def bank_of(pt, off):
        basebank = {id(P1): 0, id(P2): 4, id(P3): 6}[id(pt)]
        return basebank + off // 512

    def pw_waits(banks):
        w = []
        for b in banks:
            w.extend(psum_rd[b])
            w.append(psum_wr[b])
        return w

    def pw_done(banks, tok):
        for b in banks:
            psum_wr[b] = tok
            psum_rd[b] = []

    def pr_waits(banks):
        return [psum_wr[b] for b in banks]

    def pr_done(banks, tok):
        for b in banks:
            psum_rd[b].append(tok)

    id_bf = ar.alloc("id_bf", [128, 128], BF16)
    id_f32 = ar.alloc("id_f32", [128, 128], F32)
    mlow = ar.alloc("mlow", [128, 128], BF16)
    mup = ar.alloc("mup", [128, 128], BF16)
    ones_row = ar.alloc("ones_row", [1, 128], F32)
    ones1 = ar.alloc("ones1", [128, 1], F32)
    bones2 = ar.alloc("bones2", [128, 2], BF16)
    bones4 = ar.alloc("bones4", [128, 4], BF16)
    rope_cos = ar.alloc("rope_cos", [128, NTT * 32], F32)
    rope_sin = ar.alloc("rope_sin", [128, NTT * 32], F32)
    misc = ar.alloc("misc", [128, 48], F32)
    qs_t_g = ar.alloc("qs_t_g", [128, 128], F32)
    lt_g = ar.alloc("lt_g", [128, 16 + 2 * NHC], F32)
    b2_t_g = ar.alloc("b2_t_g", [128, 8], F32)
    rowbuf2_free_tok = [None]
    rowbuf_free_tok = [None]

    tok_c = []
    for n, h in [("c_id_bf", id_bf), ("c_id_f32", id_f32), ("c_mask_low", mlow),
                 ("c_mask_up", mup), ("c_ones_row", ones_row), ("c_ones1", ones1),
                 ("c_bones2", bones2), ("c_bones4", bones4),
                 ("c_rope_cos", rope_cos), ("c_rope_sin", rope_sin)]:
        tok_c.append(p.ydma(lambda e, n=n, h=h: e.dma_start(out=h[:, :], in_=C[n]),
                            chain="cst"))
    tok_consts = p.chain_total("cst")
    tok_idbf = tok_idf = tok_consts

    y = ar.alloc("y", [128, NCH * (T + 1)], F32)
    xn_bf = ar.alloc("xn_bf", [128, NCH * (T + 1)], BF16)

    def ych(c, lo=0, n=T):
        return y[:, c * (T + 1) + 1 + lo: c * (T + 1) + 1 + lo + n]

    def ych_sh(c, lo=0, n=T):
        return y[:, c * (T + 1) + lo: c * (T + 1) + lo + n]

    def xbch(c, lo=0, n=T):
        return xn_bf[:, c * (T + 1) + 1 + lo: c * (T + 1) + 1 + lo + n]

    tok_eps = [p.g(lambda e: e.memset(misc[:, 40:41], 1e-5)),
               p.g(lambda e: e.memset(misc[:, 41:42], 1e-6)),
               p.g(lambda e: e.memset(misc[:, 42:43], 1e-30))]
    tok_guards = [p.g(lambda e, c=c: e.memset(y[:, c * (T + 1): c * (T + 1) + 1], 0.0))
                  for c in range(NCH)]
    tok_guards += [p.g(lambda e, c=c: e.memset(xn_bf[:, c * (T + 1): c * (T + 1) + 1], 0.0))
                   for c in range(NCH)]

    xstage = ar.alloc("xstage", [128, 2 * D], F32)
    stat = ar.alloc("stat", [128, 16], F32)
    sq_scr0 = ar.alloc("sq_scr0", [128, D], F32)

    last_tile_tr = [None, None]
    xn_tile_tok = {}
    _lvl = globals().get("_DEV_P0_LEVEL", 4)
    _ntt = globals().get("_DEV_NTT", NTT)
    for tt in range(_ntt):
        par = tt % 2
        b0 = par * D
        xs = xstage[:, b0:b0 + D]
        xn_tile_tok.setdefault(tt, [])
        tok_xl = p.ydma(lambda e, tt=tt, b0=b0: e.dma_start(
            out=xstage[:, b0:b0 + D], in_=x_in[tt * 128:(tt + 1) * 128, :]),
            waits=[last_tile_tr[par], tok_consts], chain=f"xs{par}")
        sc = par * 8
        if _lvl < 1:
            last_tile_tr[par] = tok_xl
            continue
        t_sq = p.s(lambda e, xs=xs, sc=sc: e.activation(out=sq_scr0[:, :], in_=xs,
                                                        func=AF.Square,
                                                        accum_out=stat[:, sc + 1:sc + 2]),
                   waits=[tok_xl])
        t_sum = p.v(lambda e, xs=xs, sc=sc: e.reduce_sum(out=stat[:, sc:sc + 1], in_=xs,
                                                         axis=AX.X), waits=[tok_xl])
        t_mean = p.v(lambda e, sc=sc: e.tensor_scalar(out=stat[:, sc + 2:sc + 3],
                                                      in0=stat[:, sc:sc + 1],
                                                      scalar1=1.0 / D, scalar2=None,
                                                      op0=ALU.mult), waits=[t_sum])
        t_m2 = p.v(lambda e, sc=sc: e.tensor_tensor(out=stat[:, sc + 3:sc + 4],
                                                    in0=stat[:, sc + 2:sc + 3],
                                                    in1=stat[:, sc + 2:sc + 3],
                                                    op=ALU.mult), waits=[t_mean])
        t_var = p.v(lambda e, sc=sc: e.scalar_tensor_tensor(out=stat[:, sc + 4:sc + 5],
                                                            in0=stat[:, sc + 1:sc + 2],
                                                            scalar=1.0 / D,
                                                            in1=stat[:, sc + 3:sc + 4],
                                                            op0=ALU.mult, op1=ALU.subtract),
                    waits=[t_m2, t_sq])
        t_ln = p.s(lambda e, sc=sc: e.activation(out=stat[:, sc + 5:sc + 6],
                                                 in_=stat[:, sc + 4:sc + 5], func=AF.Ln,
                                                 bias=misc[:, 40:41], scale=1.0),
                   waits=[t_var] + tok_eps)
        t_rstd = p.s(lambda e, sc=sc: e.activation(out=stat[:, sc + 6:sc + 7],
                                                   in_=stat[:, sc + 5:sc + 6], func=AF.Exp,
                                                   bias=0.0, scale=-0.5), waits=[t_ln])
        t_nmr = p.v(lambda e, sc=sc: e.scalar_tensor_tensor(out=stat[:, sc + 7:sc + 8],
                                                            in0=stat[:, sc + 2:sc + 3],
                                                            scalar=-1.0,
                                                            in1=stat[:, sc + 6:sc + 7],
                                                            op0=ALU.mult, op1=ALU.mult),
                    waits=[t_rstd])
        if _lvl < 2:
            last_tile_tr[par] = t_nmr
            continue
        t_xn_h = []
        for hf2 in range(2):
            t_xn_h.append(p.v(lambda e, xs=xs, sc=sc, hf2=hf2: e.tensor_scalar(
                out=xs[:, hf2 * 512:(hf2 + 1) * 512],
                in0=xs[:, hf2 * 512:(hf2 + 1) * 512],
                scalar1=stat[:, sc + 6:sc + 7],
                scalar2=stat[:, sc + 7:sc + 8],
                op0=ALU.mult, op1=ALU.add),
                waits=[t_nmr]))
        t_xn = t_xn_h[-1]
        if _lvl < 3:
            last_tile_tr[par] = t_xn
            continue
        tr_last = None
        for c in range(NCH):
            bk = c % 4
            pt = P1[:, bk * 512: bk * 512 + 128]
            b = bank_of(P1, bk * 512)
            t_tr = p.t(lambda e, xs=xs, c=c, pt=pt: e.transpose(
                pt, xs[:, c * 128:(c + 1) * 128], id_f32[:, :]),
                waits=[t_xn_h[c // 4], tok_idf] + pw_waits([b]) +
                      (tok_guards if tt == 0 and c == 0 else []))
            pw_done([b], t_tr)
            tr_last = t_tr
            if _lvl < 4:
                continue
            t_e1 = p.v(lambda e, c=c, tt=tt, pt=pt: e.tensor_copy(
                out=ych(c, tt * 128, 128), in_=pt), waits=[t_tr])
            t_e2 = p.v(lambda e, c=c, tt=tt, pt=pt: e.tensor_copy(
                out=xbch(c, tt * 128, 128), in_=pt), waits=[t_tr])
            pr_done([b], t_e1)
            pr_done([b], t_e2)
            xn_tile_tok[tt].append(t_e2)
        last_tile_tr[par] = tr_last

    tok_p0 = [("dve", p.semv["dve"]), ("act", p.semv["act"])]
    ar.free("xstage")
    ar.free("sq_scr0")
    ar.free("stat")
    p.barrier()

    if upto == "p0":
        dbg = nc.dram_tensor("dbg_y", [128, NCH * (T + 1)], F32, kind="ExternalOutput").ap()
        dbg2 = nc.dram_tensor("dbg_xn", [128, NCH * (T + 1)], BF16, kind="ExternalOutput").ap()
        t1_ = p.ydma(lambda e: e.dma_start(out=dbg, in_=y[:, :]), waits=tok_p0, chain="dbg")
        t2_ = p.ydma(lambda e: e.dma_start(out=dbg2, in_=xn_bf[:, :]), waits=tok_p0, chain="dbg")
        t1_ = t2_ = p.chain_total("dbg")
        p.emit("sync", lambda e: e.nop(), waits=[t1_, t2_], inc=None)
        with nc.Block() as block:
            p.replay(nc, block)
        return nc

    def wload(name, src_ap, cols, C=None, M=None):
        h = ar.alloc(name, [128, cols], BF16)
        ap = src_ap.rearrange("(c p) m -> p c m", p=128)
        C, M = ap.shape[1], ap.shape[2]
        oap = vap(h, 0, [(M, C), (1, M)])
        tok = p.gdma(lambda e, oap=oap, ap=ap: e.dma_start(out=oap, in_=ap), chain=name)
        return h, tok

    wkv, tok_wkv = wload("wkv", W["W_kv"], 8 * 256)
    wq, tok_wq = wload("wq", W["W_q"], 8 * 512)
    wkrope, tok_wkrope = wload("wkrope", W["W_krope"], 8 * 512)
    wqrope, tok_wqrope = wload("wqrope", W["W_qrope"], 4 * 512)
    wkdec, tok_wkdec = wload("wkdec", W["W_kdec"], 2 * 1024)
    wvdec, tok_wvdec = wload("wvdec", W["W_vdec"], 2 * 1024)
    wqdec, tok_wqdec = wload("wqdec", W["W_qdec"], 4 * 1024)

    gkv_row = ar.alloc("gkv_row", [1, 256], F32)
    tok_gkvr = p.ydma(lambda e: e.dma_start(out=gkv_row[0:1, :], in_=W["g_kv"]), chain="rb")
    gkv_t = ar.alloc("gkv_t", [128, 2], F32)
    tok_gkv = None
    for cc in range(2):
        b = bank_of(P3, 0)
        tt_ = p.t(lambda e, cc=cc: e.transpose(P3[0:128, 0:1], gkv_row[0:1, cc * 128:(cc + 1) * 128],
                                               id_f32[0:1, 0:1]),
                  waits=[tok_gkvr, tok_idf] + pw_waits([b]))
        pw_done([b], tt_)
        tok_gkv = p.v(lambda e, cc=cc: e.tensor_copy(out=gkv_t[:, cc:cc + 1], in_=P3[:, 0:1]),
                      waits=[tt_])
        pr_done([b], tok_gkv)

    def row_to_cols(row_ap, ncols, dst, dst_col0, widths=None, waits=()):
        b = bank_of(P3, 512)
        tt_ = None
        for ci in range(ncols):
            tt_ = p.t(lambda e, ci=ci, row_ap=row_ap,
                      start=(ci == 0), stop=(ci == ncols - 1): e.matmul(
                P3[0:128, 512 + ci:513 + ci], row_ap[0:1, ci * 128: ci * 128 + 128],
                id_f32[0:1, 0:1], start=start, stop=stop, is_transpose=True),
                waits=(list(waits) + [tok_idf] + pw_waits([b])) if ci == 0 else [])
        pw_done([b], tt_)
        ev = p.s(lambda e: e.copy(
            out=dst[:, dst_col0: dst_col0 + ncols], in_=P3[:, 512:512 + ncols]),
            waits=[tt_])
        pr_done([b], ev)
        return [ev]

    slots = [(P2, 0), (P2, 512), (P3, 0), (P3, 512)]
    slot_i = [0]

    def mm_group(lhs_fn, rhs_fn, nk, extra_waits, evac_fn, evac_eng=None, evac_waits=()):
        if evac_eng is None:
            evac_eng = "v" if slot_i[0] % 2 == 0 else "s"
        sl = slots[slot_i[0] % 4]
        slot_i[0] += 1
        pt, off = sl
        b = bank_of(pt, off)
        mm_tok = None
        for k in range(nk):
            mm_tok = p.t(lambda e, k=k, pt=pt, off=off: e.matmul(
                pt[:, off:off + 512], lhs_fn(k), rhs_fn(k),
                start=(k == 0), stop=(k == nk - 1)),
                waits=(pw_waits([b]) + list(extra_waits)) if k == 0 else [])
        pw_done([b], mm_tok)
        eng = p.v if evac_eng == "v" else p.s
        ev = eng(lambda e, pt=pt, off=off: evac_fn(_EngShim(e), pt[:, off:off + 512]),
                 waits=[mm_tok] + list(evac_waits))
        pr_done([b], ev)
        return ev

    t_minit = p.v(lambda e: e.memset(misc[0:4, 8:12], -3.0e38))
    macc = {"qc": 8, "qr": 9, "kc": 10, "kr": 11}
    macc_tok = {k: [t_minit] for k in macc}
    sqb = ar.alloc("sqb", [128, 2 * 512], BF16)
    sqb_tok = [None, None]
    sqb_i = [0]

    cstat_pending = []

    def flush_cstat_one():
        key, G, bo, par, sq = cstat_pending.pop(0)
        sl = slots[slot_i[0] % 4]
        slot_i[0] += 1
        pt, off = sl
        b = bank_of(pt, off)
        mmt = p.t(lambda e, G=G, bo=bo, pt=pt, off=off, par=par: e.matmul(
            pt[0:G, off:off + 512], bo[:, 0:G], sqb[:, par * 512:(par + 1) * 512],
            start=True, stop=True),
            waits=[sq, tok_consts] + pw_waits([b]))
        pw_done([b], mmt)
        rc = macc[key] + 16
        rm = p.v(lambda e, G=G, pt=pt, off=off, rc=rc: e.reduce_max(
            out=misc[0:G, rc:rc + 1], in_=pt[0:G, off:off + 512], axis=AX.X),
            waits=[mmt] + macc_tok[key][-1:])
        pr_done([b], rm)
        mx = p.v(lambda e, G=G, col=macc[key], rc=rc: e.tensor_tensor(
            out=misc[0:G, col:col + 1], in0=misc[0:G, col:col + 1],
            in1=misc[0:G, rc:rc + 1], op=ALU.max), waits=[rm])
        macc_tok[key].append(mx)
        sqb_tok[par] = mmt

    def flush_cstat_all():
        while cstat_pending:
            flush_cstat_one()

    def emit_cstat(key, src_ap_fn, G, bo, waits):
        if len(cstat_pending) >= 2:
            flush_cstat_one()
        par = sqb_i[0] % 2
        sqb_i[0] += 1
        sq = p.s(lambda e, src_ap_fn=src_ap_fn, par=par: e.activation(
            out=sqb[:, par * 512:(par + 1) * 512], in_=src_ap_fn(), func=AF.Square,
            bias=0.0, scale=1.0),
            waits=list(waits) + [sqb_tok[par]])
        cstat_pending.append((key, G, bo, par, sq))
        return sq

    kvc_f = ar.alloc("kvc_f", [128, 2 * T], F32)
    kvn_bf = ar.alloc("kvn_bf", [128, 2 * T], BF16)
    scr2k = ar.alloc("scr2k", [128, T], F32)
    rms_row = ar.alloc("rms_row", [1, T], F32)
    bc_scr = ar.alloc("bc_scr", [128, 512], F32)

    tok_kvc = []
    for m in range(2):
        for sp in range(4):
            ev = mm_group(
                lambda k, m=m: wkv[:, k * 256 + m * 128: k * 256 + m * 128 + 128],
                lambda k, sp=sp: xbch(k, sp * 512, 512),
                8, [tok_wkv] + sum((xn_tile_tok[4 * sp + i] for i in range(4)), []),
                lambda e, ps, m=m, sp=sp: e.tensor_copy(
                    out=kvc_f[:, m * T + sp * 512: m * T + sp * 512 + 512], in_=ps))
            tok_kvc.append(ev)
    stats_mm = []
    for m in range(2):
        sq_tok = p.s(lambda e, m=m: e.activation(out=scr2k[:, :], in_=kvc_f[:, m * T:(m + 1) * T],
                                                 func=AF.Square), waits=tok_kvc + stats_mm)
        new_mm = []
        for sp in range(4):
            b = bank_of(P1, sp * 512)
            mmt = p.t(lambda e, sp=sp, m=m: e.matmul(P1[0:1, sp * 512: sp * 512 + 512],
                                                ones1[:, 0:1], scr2k[:, sp * 512: sp * 512 + 512],
                                                start=(m == 0), stop=(m == 1)),
                      waits=([sq_tok, tok_consts] + (pw_waits([b]) if m == 0 else [])))
            if m == 0:
                pw_done([b], mmt)
            else:
                psum_wr[b] = mmt
            new_mm.append(mmt)
        stats_mm = new_mm
    tok_rmsrow = []
    for sp in range(4):
        b = bank_of(P1, sp * 512)
        ev = p.v(lambda e, sp=sp: e.tensor_copy(out=rms_row[0:1, sp * 512: sp * 512 + 512],
                                                in_=P1[0:1, sp * 512: sp * 512 + 512]),
                 waits=[stats_mm[sp]])
        pr_done([b], ev)
        tok_rmsrow.append(ev)
    t_l = p.s(lambda e: e.activation(out=rms_row[0:1, :], in_=rms_row[0:1, :], func=AF.Ln,
                                     bias=misc[0:1, 41:42], scale=1.0 / KV_RANK),
               waits=tok_rmsrow + tok_eps)
    t_rn = p.s(lambda e: e.activation(out=rms_row[0:1, :], in_=rms_row[0:1, :], func=AF.Exp,
                                      bias=0.0, scale=-0.5), waits=[t_l])
    tok_kvn = []
    for sp in range(4):
        b = bank_of(P1, 0)
        bm = p.t(lambda e, sp=sp: e.matmul(P1[:, 0:512], ones_row[0:1, :],
                                           rms_row[0:1, sp * 512: sp * 512 + 512],
                                           start=True, stop=True),
                 waits=[t_rn, tok_consts] + pw_waits([b]))
        pw_done([b], bm)
        be = p.v(lambda e: e.tensor_copy(out=bc_scr[:, :], in_=P1[:, 0:512]),
                 waits=[bm] + tok_kvn[-2:])
        pr_done([b], be)
        for m in range(2):
            tk = p.v(lambda e, m=m, sp=sp: e.scalar_tensor_tensor(
                out=kvn_bf[:, m * T + sp * 512: m * T + sp * 512 + 512],
                in0=kvc_f[:, m * T + sp * 512: m * T + sp * 512 + 512],
                scalar=gkv_t[:, m:m + 1], in1=bc_scr[:, :], op0=ALU.mult, op1=ALU.mult),
                waits=[be, tok_gkv])
            tok_kvn.append(tk)
    ar.free("kvc_f")
    ar.free("scr2k")
    ar.free("rms_row")
    ar.free("bc_scr")
    ar.free("gkv_row")
    ar.free("gkv_t")
    p.barrier()

    qc_bf = ar.alloc("qc_bf", [128, 4 * T], BF16)
    tok_qc = []
    for m in range(4):
        for sp in range(4):
            ev = mm_group(
                lambda k, m=m: wq[:, k * 512 + m * 128: k * 512 + m * 128 + 128],
                lambda k, sp=sp: xbch(k, sp * 512, 512),
                8, [tok_wq],
                lambda e, ps, m=m, sp=sp: e.tensor_copy(
                    out=qc_bf[:, m * T + sp * 512: m * T + sp * 512 + 512], in_=ps))
            tok_qc.append(ev)
    ar.free("wkv")
    ar.free("wq")
    p.barrier()

    def rope_project(prefix, nk, lhs_fn, wsb, wtok, src_toks):
        rot_tok_t = ar.alloc(prefix + "_tok", [128, NTT * 512], BF16)
        feat = ar.alloc(prefix + "_feat", [128, 4 * T], BF16)
        t1s = ar.alloc(prefix + "_t1", [128, 512], F32)
        t2s = ar.alloc(prefix + "_t2", [128, 512], F32)
        done = []
        scr_tok = None
        for tt in range(NTT):
            pt, off = (P2, 0) if tt % 2 == 0 else (P3, 0)
            bks = [bank_of(pt, off)]
            mm_tok = None
            for k in range(nk):
                mm_tok = p.t(lambda e, tt=tt, k=k, pt=pt, off=off: e.matmul(
                    pt[:, off:off + 512], lhs_fn(k, tt * 128),
                    wsb[:, k * 512: k * 512 + 512],
                    start=(k == 0), stop=(k == nk - 1)),
                    waits=([wtok] + pw_waits(bks) + src_toks) if k == 0 else [])
            pw_done(bks, mm_tok)
            co = tt * 32
            t1 = p.v(lambda e, pt=pt, off=off, co=co: e.tensor_tensor(
                out=vap(t1s, 0, [(32, 16), (1, 32)]),
                in0=vap(pt, off, [(32, 16), (1, 32)]),
                in1=vap(rope_cos, co, [(0, 16), (1, 32)]), op=ALU.mult),
                waits=[mm_tok, tok_consts, scr_tok])
            t2a = p.v(lambda e, pt=pt, off=off, co=co: e.tensor_tensor(
                out=vap(t2s, 0, [(32, 16), (1, 16)]),
                in0=vap(pt, off + 16, [(32, 16), (1, 16)]),
                in1=vap(rope_sin, co, [(0, 16), (1, 16)]), op=ALU.mult),
                waits=[mm_tok, scr_tok])
            t2b = p.v(lambda e, pt=pt, off=off, co=co: e.tensor_tensor(
                out=vap(t2s, 16, [(32, 16), (1, 16)]),
                in0=vap(pt, off, [(32, 16), (1, 16)]),
                in1=vap(rope_sin, co + 16, [(0, 16), (1, 16)]), op=ALU.mult),
                waits=[mm_tok, scr_tok])
            scr_tok = p.v(lambda e, tt=tt: e.tensor_tensor(
                out=rot_tok_t[:, tt * 512:(tt + 1) * 512], in0=t1s[:, :], in1=t2s[:, :],
                op=ALU.add), waits=[t1, t2a, t2b])
            pr_done(bks, scr_tok)
            done.append(scr_tok)
        for tt in range(NTT):
            for dc in range(4):
                bk = (tt * 4 + dc) % 2
                b = bank_of(P1, bk * 512)
                ptv = P1[:, bk * 512: bk * 512 + 256].bitcast(BF16)
                t_tr = p.t(lambda e, tt=tt, dc=dc, ptv=ptv: e.transpose(
                    ptv[:, 0:128], rot_tok_t[:, tt * 512 + dc * 128: tt * 512 + dc * 128 + 128],
                    id_bf[:, :]),
                    waits=[done[tt], tok_idbf] + pw_waits([b]))
                pw_done([b], t_tr)
                ev = p.v(lambda e, tt=tt, dc=dc, ptv=ptv, feat=feat: e.tensor_copy(
                    out=feat[:, dc * T + tt * 128: dc * T + tt * 128 + 128], in_=ptv[:, 0:128]),
                    waits=[t_tr])
                pr_done([b], ev)
                last_ev = ev
        ar.free(prefix + "_tok")
        ar.free(prefix + "_t1")
        ar.free(prefix + "_t2")
        p.barrier()
        return feat, [last_ev]

    krope_feat, tok_krope = rope_project(
        "krope", 8, lambda k, t0: xbch(k, t0, 128), wkrope, tok_wkrope, [])
    ar.free("wkrope")
    qrope_feat, tok_qrope = rope_project(
        "qrope", 4, lambda k, t0: qc_bf[:, k * T + t0: k * T + t0 + 128],
        wqrope, tok_wqrope, tok_qc)
    ar.free("wqrope")

    tok_ropespill = []
    for key, feat, toks, hb in [("kr", krope_feat, tok_krope, kh_hbm),
                                ("qr", qrope_feat, tok_qrope, qh_hbm)]:
        for ch in range(4):
            for sp in range(4):
                emit_cstat(key, lambda feat=feat, ch=ch, sp=sp:
                           feat[:, ch * T + sp * 512: ch * T + sp * 512 + 512],
                           4, bones4, toks)
        for h in range(16):
            tok_ropespill.append(p.ydma(lambda e, h=h, feat=feat, hb=hb: e.dma_start(
                out=hb[h, 64:96, :],
                in_=feat[(h % 4) * 32:(h % 4) * 32 + 32, (h // 4) * T:(h // 4 + 1) * T]),
                waits=toks, chain="cr"))

    cont_stage = ar.alloc("cont_stage", [128, 2 * 512], BF16)
    cstage_tok = [None, None]
    tok_contspill = []
    for key, wsb, wtok, nk, src_fn, src_toks, hb in [
        ("kc", wkdec, tok_wkdec, 2,
         lambda k: kvn_bf[:, k * T: (k + 1) * T], tok_kvn, kh_hbm),
        ("qc", wqdec, tok_wqdec, 4,
         lambda k: qc_bf[:, k * T: (k + 1) * T], tok_qc, qh_hbm),
    ]:
        for m in range(8):
            for sp in range(4):
                par = (m * 4 + sp) % 2
                ev = mm_group(
                    lambda k, m=m, wsb=wsb: wsb[:, k * 1024 + m * 128: k * 1024 + m * 128 + 128],
                    lambda k, sp=sp, src_fn=src_fn: src_fn(k)[:, sp * 512: sp * 512 + 512],
                    nk, [wtok] + src_toks,
                    lambda e, ps, par=par: e.tensor_copy(
                        out=cont_stage[:, par * 512:(par + 1) * 512], in_=ps),
                    evac_waits=[cstage_tok[par]])
                sqtok = emit_cstat(
                    key, lambda par=par: cont_stage[:, par * 512:(par + 1) * 512],
                    2, bones2, [ev])
                d1 = p.ydma(lambda e, m=m, sp=sp, par=par, hb=hb: e.dma_start(
                    out=hb[2 * m, 0:64, sp * 512: sp * 512 + 512],
                    in_=cont_stage[0:64, par * 512:(par + 1) * 512]), waits=[ev],
                    chain=f"cs{par}")
                d2 = p.ydma(lambda e, m=m, sp=sp, par=par, hb=hb: e.dma_start(
                    out=hb[2 * m + 1, 0:64, sp * 512: sp * 512 + 512],
                    in_=cont_stage[64:128, par * 512:(par + 1) * 512]), waits=[],
                    chain=f"cs{par}")
                cstage_tok[par] = [p.chain_total(f"cs{par}"), sqtok]
                tok_contspill += [d1, d2]
    ar.free("cont_stage")
    ar.free("wkdec")
    ar.free("wqdec")
    ar.free("krope_feat")
    ar.free("qrope_feat")
    ar.free("xn_bf")
    ar.free("qc_bf")
    p.barrier()

    v65 = ar.alloc("v65", [128, NTT * 1040], BF16)
    tok_vones = p.g(lambda e: e.memset(vap(v65, 64, [(1040, NTT), (65, 16), (1, 1)]), 1.0))
    tok_v = []
    for tt in range(NTT):
        for hf in range(2):
            ev = mm_group(
                lambda k, tt=tt: kvn_bf[:, k * T + tt * 128: k * T + tt * 128 + 128],
                lambda k, hf=hf: wvdec[:, k * 1024 + hf * 512: k * 1024 + hf * 512 + 512],
                2, [tok_wvdec] + tok_kvn,
                lambda e, ps, tt=tt, hf=hf: e.tensor_copy(
                    out=vap(v65, tt * 1040 + hf * 8 * 65, [(65, 8), (1, 64)]),
                    in_=vap(ps, 0, [(64, 8), (1, 64)])),
                evac_eng="v", evac_waits=[tok_vones])
            tok_v.append(ev)
    flush_cstat_all()
    ar.free("wvdec")
    ar.free("kvn_bf")
    ar.free("sqb")
    p.barrier()

    fin = []
    for i, key in enumerate(["qc", "qr", "kc", "kr"]):
        G = 2 if key in ("qc", "kc") else 4
        b = bank_of(P2, 512)
        tt_ = p.t(lambda e, col=macc[key], G=G: e.transpose(
            P2[0:G, 512:513], misc[0:G, col:col + 1], id_f32[0:G, 0:G]),
            waits=macc_tok[key] + pw_waits([b]) + [tok_idf]) if False else None
        tt_ = p.t(lambda e, col=macc[key], G=G: e.transpose(
            P2[0:1, 512:512 + G], misc[0:G, col:col + 1], id_f32[0:G, 0:G]),
            waits=macc_tok[key] + pw_waits([b]) + [tok_idf])
        pw_done([b], tt_)
        fin.append(p.v(lambda e, i=i, G=G: e.reduce_max(out=misc[0:1, 13 + i:14 + i],
                                                        in_=P2[0:1, 512:512 + G], axis=AX.X),
                       waits=[tt_]))
        pr_done([b], fin[-1])
    t_mq = p.v(lambda e: e.tensor_tensor(out=misc[0:1, 17:18], in0=misc[0:1, 13:14],
                                         in1=misc[0:1, 14:15], op=ALU.add), waits=fin)
    t_mk = p.v(lambda e: e.tensor_tensor(out=misc[0:1, 18:19], in0=misc[0:1, 15:16],
                                         in1=misc[0:1, 16:17], op=ALU.add), waits=[t_mq])
    t_pr = p.v(lambda e: e.tensor_tensor(out=misc[0:1, 19:20], in0=misc[0:1, 17:18],
                                         in1=misc[0:1, 18:19], op=ALU.mult), waits=[t_mk])
    t_lnp = p.s(lambda e: e.activation(out=misc[0:1, 20:21], in_=misc[0:1, 19:20], func=AF.Ln,
                                       bias=misc[0:1, 42:43], scale=1.0),
                 waits=[t_pr] + tok_eps)
    t_sqr = p.s(lambda e: e.activation(out=misc[0:1, 21:22], in_=misc[0:1, 20:21], func=AF.Exp,
                                       bias=0.0, scale=0.5), waits=[t_lnp])
    t_negc = p.v(lambda e: e.tensor_scalar(out=misc[0:1, 22:23], in0=misc[0:1, 21:22],
                                           scalar1=-MLA_SCALE, scalar2=None, op0=ALU.mult),
                 waits=[t_sqr])
    b = bank_of(P2, 512)
    t_bc = p.t(lambda e: e.matmul(P2[:, 512:513], ones_row[0:1, :], misc[0:1, 22:23],
                                  start=True, stop=True), waits=[t_negc] + pw_waits([b]))
    pw_done([b], t_bc)
    tok_negc = p.v(lambda e: e.tensor_copy(out=misc[:, 0:1], in_=P2[:, 512:513]), waits=[t_bc])
    pr_done([b], tok_negc)

    wo_mla, tok_womla = wload("wo_mla", W["W_o"], 8 * 1024)
    attnT = ar.alloc("attnT", [128, NCH * T], BF16)
    qh_sb = [ar.alloc(f"qh{j}", [96, T], BF16) for j in range(2)]
    kh_sb = [ar.alloc(f"kh{j}", [96, T], BF16) for j in range(2)]
    pt_sb = [ar.alloc(f"ptb{j}", [128, 1024], BF16) for j in range(3)]
    bcn_sb = ar.alloc("bcn_sb", [64, 1024], F32)
    rec_row = ar.alloc("rec_row", [1, 4096], F32)

    stage_tok = [None, None]
    head_done = [None, None]
    pt_last_pv = [None, None, None]
    ot_state = {0: {"wr": None, "rd": []}, 1: {"wr": None, "rd": []}}

    def stage_head(h):
        j = h % 2
        w0 = [head_done[j]]
        if h < 2:
            w0 += [p.chain_total("cs0"), p.chain_total("cs1"), p.chain_total("cr")]
        t1 = p.ydma(lambda e, h=h, j=j: e.dma_start(out=kh_sb[j][:, :], in_=kh_hbm[h]),
                    waits=w0, chain=f"st{j}")
        t2 = p.ydma(lambda e, h=h, j=j: e.dma_start(out=qh_sb[j][:, :], in_=qh_hbm[h]),
                    waits=[], chain=f"st{j}")
        return [p.chain_total(f"st{j}")]

    stage_tok[0] = stage_head(0)
    stage_tok[1] = stage_head(1)
    sc_bufs = [(P2, 0), (P3, 0)]
    mla_exp_tok = {}
    mla_qk_last = {}
    mla_ctr = [0]

    def mla_emit_qk(h, pa, kc):
        j = h % 2
        sbuf = sc_bufs[kc % 2]
        pt, off = sbuf
        bks = [bank_of(pt, off), bank_of(pt, off + 512)]
        qlo = max(1024 * pa, 128 * kc)
        lspan = 1024 * (pa + 1) - qlo
        has_mask = 128 * kc >= 1024 * pa
        tok = None
        for ns in range(0, lspan, 512):
            n = min(512, lspan - ns)
            stop = not (has_mask and ns == 0)
            tok = p.t(lambda e, kc=kc, qlo=qlo, ns=ns, n=n, pt=pt, off=off, j=j,
                      stop=stop: e.matmul(
                pt[:, off + ns: off + ns + n],
                kh_sb[j][:, kc * 128: kc * 128 + 128],
                qh_sb[j][:, qlo + ns: qlo + ns + n],
                start=True, stop=stop),
                waits=(pw_waits(bks) + stage_tok[j]) if ns == 0 else [])
        if has_mask:
            tok = p.t(lambda e, pt=pt, off=off: e.matmul(
                pt[:, off: off + 128], id_bf[:, :], mlow[:, :],
                start=False, stop=True), waits=[tok_idbf, tok_consts])
        pw_done(bks, tok)
        mla_qk_last[h] = tok
        pb = mla_ctr[0] % 3
        mla_ctr[0] += 1
        ex = p.s(lambda e, pt=pt, off=off, lspan=lspan, pb=pb: e.activation(
            out=pt_sb[pb][:, 0:lspan], in_=pt[:, off:off + lspan], func=AF.Exp,
            bias=misc[:, 0:1], scale=MLA_SCALE),
            waits=[tok, tok_negc, pt_last_pv[pb]])
        pr_done(bks, ex)
        mla_exp_tok[(h, pa, kc)] = ex
        return (h, pa, kc, qlo, lspan, pb)

    def mla_emit_pv(st):
        h, pa, kc, qlo, lspan, pb = st
        oj = h % 2
        ot_off = oj * 1024
        ot_banks = [bank_of(P1, ot_off), bank_of(P1, ot_off + 512)]
        ex = mla_exp_tok[(h, pa, kc)]
        lq = qlo - 1024 * pa
        last_kc = {0: 3 if pa == 0 else 11, 1: 8 * (pa + 1) - 1}
        segs = []
        cur = lq
        while cur < 1024:
            nxt = min(1024, (cur // 512 + 1) * 512)
            segs.append((cur, nxt - cur))
            cur = nxt
        tok = None
        firstseg = True
        for (lo, n) in segs:
            first = (kc == 0)
            bank01 = lo // 512
            stop = (kc == last_kc[bank01])
            w0 = [ex]
            if kc == 0 and firstseg:
                w0 += ot_state[oj]["rd"] + [ot_state[oj]["wr"]]
                w0 += tok_v + [tok_vones]
            firstseg = False
            tok = p.t(lambda e, kc=kc, lo=lo, n=n, lq=lq, pb=pb, first=first,
                      stop=stop, h=h, ot_off=ot_off: e.matmul(
                P1[0:65, ot_off + lo: ot_off + lo + n],
                v65[:, kc * 1040 + h * 65: kc * 1040 + h * 65 + 65],
                pt_sb[pb][:, lo - lq: lo - lq + n],
                start=first, stop=stop), waits=w0)
        pt_last_pv[pb] = tok
        ot_state[oj]["wr"] = tok
        if kc == 0:
            ot_state[oj]["rd"] = []
        return tok

    bcn_free = [None]
    rec_i = [0]
    rec_war = [None, None]

    def mla_norm_pre(h, pa, pv_last):
        j = rec_i[0] % 2
        rec_i[0] += 1
        base = j * 2048
        ot_off = (h % 2) * 1024
        rcs = []
        for half2 in range(2):
            t_ln = p.s(lambda e, ot_off=ot_off, half2=half2, base=base: e.activation(
                out=rec_row[0:1, base + 1024 + half2 * 512: base + 1536 + half2 * 512],
                in_=P1[64:65, ot_off + half2 * 512: ot_off + half2 * 512 + 512],
                func=AF.Ln, bias=0.0, scale=1.0), waits=[pv_last, rec_war[j]])
            t_rc = p.s(lambda e, half2=half2, base=base: e.activation(
                out=rec_row[0:1, base + half2 * 512: base + (half2 + 1) * 512],
                in_=rec_row[0:1, base + 1024 + half2 * 512: base + 1536 + half2 * 512],
                func=AF.Exp, bias=0.0, scale=-1.0), waits=[t_ln])
            rcs.append(t_rc)
        return (j, base, rcs)

    def mla_emit_norm(h, pa, pre, pv_last):
        j, base, rcs = pre
        oj = h % 2
        ot_off = oj * 1024
        bb = bank_of(P3, 512)
        be = []
        bms = []
        for half2 in range(2):
            bm = p.t(lambda e, half2=half2, base=base: e.matmul(
                P3[0:64, 512:1024], ones_row[0:1, 0:64],
                rec_row[0:1, base + half2 * 512: base + (half2 + 1) * 512],
                start=True, stop=True), waits=[rcs[half2]] + pw_waits([bb]))
            pw_done([bb], bm)
            bms.append(bm)
            bev = p.s(lambda e, half2=half2: e.copy(
                out=bcn_sb[0:64, half2 * 512:(half2 + 1) * 512],
                in_=P3[0:64, 512:1024]), waits=[bm, bcn_free[0]])
            pr_done([bb], bev)
            be.append(bev)
        rec_war[j] = bms[-1]
        nt = p.v(lambda e, h=h, pa=pa, ot_off=ot_off: e.tensor_tensor(
            out=attnT[(h % 2) * 64:(h % 2) * 64 + 64,
                      (h // 2) * T + pa * 1024: (h // 2) * T + pa * 1024 + 1024],
            in0=P1[0:64, ot_off: ot_off + 1024],
            in1=bcn_sb[0:64, 0:1024], op=ALU.mult), waits=be)
        bcn_free[0] = nt
        ot_state[oj]["rd"] = [nt]

    mla_items = []
    for h in range(16):
        for pa in range(2):
            for kc in range(8 * (pa + 1)):
                mla_items.append((h, pa, kc, kc == 8 * (pa + 1) - 1))
    mq = []
    norm_ready = None
    norm_age = 0
    for ii, (h, pa, kc, is_last) in enumerate(mla_items):
        mq.append(mla_emit_qk(h, pa, kc))
        if norm_ready is not None:
            norm_age += 1
            if norm_age >= 2:
                mla_emit_norm(*norm_ready)
                norm_ready = None
        if len(mq) > 2:
            st0 = mq.pop(0)
            if st0[2] == 0 and norm_ready is not None:
                mla_emit_norm(*norm_ready)
                norm_ready = None
            pv = mla_emit_pv(st0)
            if st0[2] == 8 * (st0[1] + 1) - 1:
                pre = mla_norm_pre(st0[0], st0[1], pv)
                norm_ready = (st0[0], st0[1], pre, pv)
                norm_age = 0
        head_done[h % 2] = mla_qk_last.get(h)
        if kc == 0 and pa == 0 and 1 <= h < 15:
            stage_tok[(h + 1) % 2] = stage_head(h + 1)
    while mq:
        st0 = mq.pop(0)
        pv = mla_emit_pv(st0)
        if norm_ready is not None:
            mla_emit_norm(*norm_ready)
            norm_ready = None
        if st0[2] == 8 * (st0[1] + 1) - 1:
            pre = mla_norm_pre(st0[0], st0[1], pv)
            norm_ready = (st0[0], st0[1], pre, pv)
    if norm_ready is not None:
        mla_emit_norm(*norm_ready)

    tok_attnT = [ot_state[0]["rd"], ot_state[1]["rd"]]
    for nm in ("qh0", "qh1", "kh0", "kh1", "ptb0", "ptb1", "ptb2", "v65", "bcn_sb", "rec_row"):
        ar.free(nm)
    p.barrier()

    s_stage = ar.alloc("s_stage", [128, 2 * 512], F32)
    sstage_tok = [None, None]
    tok_s_done = []
    for m in range(8):
        for sp in range(4):
            par = (m * 4 + sp) % 2
            ev = mm_group(
                lambda k, m=m: wo_mla[:, k * 1024 + m * 128: k * 1024 + m * 128 + 128],
                lambda k, sp=sp: attnT[:, k * T + sp * 512: k * T + sp * 512 + 512],
                8, [tok_womla] + tok_attnT,
                lambda e, ps, par=par: e.tensor_copy(
                    out=s_stage[:, par * 512:(par + 1) * 512], in_=ps),
                evac_waits=[sstage_tok[par]])
            dt_ = p.ydma(lambda e, m=m, sp=sp, par=par: e.dma_start(
                out=s_hbm[m * 128:(m + 1) * 128, sp * 512: sp * 512 + 512],
                in_=s_stage[:, par * 512:(par + 1) * 512]), waits=[ev], chain=f"ss{par}")
            sstage_tok[par] = dt_
            tok_s_done.append(dt_)
    tok_s_done = [p.chain_total("ss0"), p.chain_total("ss1")]
    ar.free("s_stage")
    ar.free("attnT")
    ar.free("wo_mla")
    p.barrier()

    if upto == "mla":
        dbg = nc.dram_tensor("dbg_s", [D, T], F32, kind="ExternalOutput").ap()
        tk = p.ydma(lambda e: e.dma_start(out=dbg, in_=s_hbm), waits=tok_s_done, chain="dbg")
        p.emit("sync", lambda e: e.nop(), waits=[tk], inc=None)
        with nc.Block() as block:
            p.replay(nc, block)
        return nc

    ar.free("rope_cos")
    ar.free("rope_sin")
    p.barrier()
    rowbuf_g = ar.alloc("rowbuf_g", [1, 128], F32)

    def ffn_layer(l, y_ready, dbg_stop=None):
        xp = {n: ar.alloc(n + str(l), [128, NTT * 64], F32)
              for n in ("xcq", "xsq", "xck", "xsk")}
        for n, h in [("c_xp_cq", xp["xcq"]), ("c_xp_sq", xp["xsq"]),
                     ("c_xp_ck", xp["xck"]), ("c_xp_sk", xp["xsk"])]:
            p.ydma(lambda e, n=n, h=h: e.dma_start(out=h[:, :], in_=C[n]), chain="cst")
        tok_xp = [p.chain_total("cst")]
        y_bf = ar.alloc(f"y_bf{l}", [128, NCH * (T + 1)], BF16)

        def ybch(c, lo=0, n=T):
            return y_bf[:, c * (T + 1) + 1 + lo: c * (T + 1) + 1 + lo + n]

        def ybch_sh(c, lo=0, n=T):
            return y_bf[:, c * (T + 1) + lo: c * (T + 1) + lo + n]

        tok_ybg = [p.g(lambda e, c=c, y_bf=y_bf: e.memset(
            y_bf[:, c * (T + 1): c * (T + 1) + 1], 0.0)) for c in range(NCH)]
        rowbuf = rowbuf_g
        rowbuf_free = [rowbuf_free_tok[0]]
        t_qsr = p.ydma(lambda e: e.dma_start(out=rowbuf[0:1, 0:64], in_=W["q_scale"][l:l + 1, :]),
                       waits=rowbuf_free, chain="rb")
        t_ksr = p.ydma(lambda e: e.dma_start(out=rowbuf[0:1, 64:128], in_=W["k_scale"][l:l + 1, :]),
                       waits=[], chain="rb")
        t_qsr = t_ksr = p.chain_total("rb")
        qs_t = qs_t_g
        tok_qs = []
        for i, tkr in enumerate([t_qsr, t_ksr]):
            bb = bank_of(P3, 512)
            bm = p.t(lambda e, i=i: e.matmul(P3[:, 512:512 + 64], ones_row[0:1, :],
                                             rowbuf[0:1, i * 64:(i + 1) * 64],
                                             start=True, stop=True),
                     waits=[tkr, tok_consts] + pw_waits([bb]))
            pw_done([bb], bm)
            tk = p.v(lambda e, i=i: e.tensor_copy(out=qs_t[:, i * 64:(i + 1) * 64],
                                                  in_=P3[:, 512:512 + 64]), waits=[bm])
            pr_done([bb], tk)
            tok_qs.append(tk)
        lt = lt_g
        b2_t = b2_t_g
        def stage_param_tiles(waits0=()):
            rowbuf2 = ar.alloc(f"rowbuf2_{l}", [1, 2 * HID + 128], F32)
            rowbuf3 = ar.alloc(f"rowbuf3_{l}", [1, 2 * D], F32)
            rowbuf4 = ar.alloc(f"rowbuf4_{l}", [1, D], F32)
            t_zpad = p.v(lambda e: e.memset(rowbuf2[0:1, 2 * HID: 2 * HID + 128], 0.0),
                         waits=list(waits0))
            p.ydma(lambda e: e.dma_start(out=rowbuf2[0:1, 0:2 * HID],
                                         in_=W["b1"][l:l + 1, :]), chain="rb",
                   waits=list(waits0))
            p.ydma(lambda e: e.dma_start(out=rowbuf3[0:1, 0:D],
                                         in_=W["ln_g"][l:l + 1, :]), chain="rb")
            p.ydma(lambda e: e.dma_start(out=rowbuf3[0:1, D:2 * D],
                                         in_=W["ln_b"][l:l + 1, :]), chain="rb")
            p.ydma(lambda e: e.dma_start(out=rowbuf4[0:1, 0:D],
                                         in_=W["b2"][l:l + 1, :]), chain="rb")
            tok_rows2 = [p.chain_total("rb"), t_zpad]
            tok_lng = row_to_cols(vap(rowbuf3, 0, [(1, D)]), 8, lt, 0,
                                  waits=tok_rows2)
            tok_lnb = row_to_cols(vap(rowbuf3, D, [(1, D)]), 8, lt, 8,
                                  waits=tok_rows2)
            tok_b2 = row_to_cols(vap(rowbuf4, 0, [(1, D)]), 8, b2_t, 0,
                                 waits=tok_rows2)
            wid = [128] * (NHC - 1) + [42]
            tok_b1a = row_to_cols(rowbuf2, NHC, lt, 16, widths=wid, waits=tok_rows2)
            b1g_row = vap(rowbuf2, HID, [(1, HID + 128)])
            tok_b1g = row_to_cols(b1g_row, NHC, lt, 16 + NHC, widths=wid, waits=tok_rows2)
            return tok_lng, tok_lnb, tok_b2, tok_b1a, tok_b1g

        tok_ybf = []
        for c in range(NCH):
            eng = p.v if c < 5 else p.g
            tok_ybf.append(eng(lambda e, c=c: e.tensor_copy(out=ybch(c), in_=ych(c)),
                               waits=y_ready + tok_ybg))

        cqs = ar.alloc(f"cqs{l}", [128, NTT * 64], BF16)
        sqs = ar.alloc(f"sqs{l}", [128, NTT * 64], BF16)
        cks = ar.alloc(f"cks{l}", [128, NTT * 64], BF16)
        sks = ar.alloc(f"sks{l}", [128, NTT * 64], BF16)
        tok_tab = []
        for dst, srcn, col in [(cqs, "xcq", 0), (cks, "xck", 64)]:
            tok_tab.append(p.v(lambda e, dst=dst, srcn=srcn, col=col: e.tensor_tensor(
                out=vap(dst, 0, [(64, NTT), (1, 64)]),
                in0=vap(xp[srcn], 0, [(64, NTT), (1, 64)]),
                in1=vap(qs_t, col, [(0, NTT), (1, 64)]), op=ALU.mult),
                waits=tok_qs + tok_xp))
        for dst, srcn, col in [(sqs, "xsq", 0), (sks, "xsk", 64)]:
            tok_tab.append(p.v(lambda e, dst=dst, srcn=srcn, col=col: e.tensor_tensor(
                out=vap(dst, 0, [(64, NTT), (32, 2), (1, 32)]),
                in0=vap(xp[srcn], 0, [(64, NTT), (32, 2), (1, 32)]),
                in1=vap(qs_t, col + 32, [(0, NTT), (-32, 2), (1, 32)]), op=ALU.mult),
                waits=tok_qs + tok_xp))

        wqkv_sb = ar.alloc(f"wqkv{l}", [128, 8 * 1536], BF16)
        tok_wqkv = p.gdma(lambda e: e.dma_start(
            out=vap(wqkv_sb, 0, [(1536, 4), (1, 1536)]),
            in_=W["wqkv"][l, 0:512].rearrange("(c p) m -> p c m", p=128)),
            chain=f"wqkv{l}")
        tok_wqkv2 = p.gdma(lambda e: e.dma_start(
            out=vap(wqkv_sb, 4 * 1536, [(1536, 4), (1, 1536)]),
            in_=W["wqkv"][l, 512:1024].rearrange("(c p) m -> p c m", p=128)),
            chain=f"wqkv{l}b")
        q_tok = ar.alloc(f"q_tok{l}", [128, NTT * 512], BF16)
        k_tok = ar.alloc(f"k_tok{l}", [128, NTT * 512], BF16)
        v_tok = ar.alloc(f"v_tok{l}", [128, NTT * 512], BF16)
        ss = ar.alloc(f"ss{l}", [128, NTT * 16], F32)
        ssb = ar.alloc(f"ssb{l}", [128, NTT * 16], BF16)
        sqf = ar.alloc(f"sqf{l}", [128, 2 * 512], F32)
        r1 = ar.alloc(f"r1_{l}", [128, 512], BF16)
        r2 = ar.alloc(f"r2_{l}", [128, 512], BF16)
        r3 = ar.alloc(f"r3_{l}", [128, 512], BF16)
        qk_ps = [(P2, 0), (P1, 1024)]
        v_ps = [(P1, 0), (P1, 512)]
        tok_qkv = []
        sqf_war = [None, None]
        last_a3 = [None]
        rot_tok = {}
        for tt in range(NTT):
            ptq, offq = qk_ps[tt % 2]
            ptv, offv = v_ps[tt % 2]
            bksq = [bank_of(ptq, offq), bank_of(ptq, offq + 512)]
            bksv = [bank_of(ptv, offv)]
            qm = vm = None
            for k in range(8):
                for ns in range(2):
                    qm = p.t(lambda e, k=k, tt=tt, ns=ns, ptq=ptq, offq=offq: e.matmul(
                        ptq[:, offq + ns * 512: offq + (ns + 1) * 512],
                        ybch_sh(k, tt * 128, 128) if k >= 4 else ybch(k, tt * 128, 128),
                        wqkv_sb[:, k * 1536 + ns * 512: k * 1536 + (ns + 1) * 512],
                        start=(k == 0), stop=(k == 7)),
                        waits=([tok_wqkv] + tok_ybf + pw_waits(bksq) + pw_waits(bksv))
                        if (k == 0 and ns == 0) else
                        ([tok_wqkv2] if (k == 4 and ns == 0) else []))
                vm = p.t(lambda e, k=k, tt=tt, ptv=ptv, offv=offv: e.matmul(
                    ptv[:, offv: offv + 512],
                    ybch_sh(k, tt * 128, 128) if k >= 4 else ybch(k, tt * 128, 128),
                    wqkv_sb[:, k * 1536 + 1024: k * 1536 + 1536],
                    start=(k == 0), stop=(k == 7)), waits=[])
            pw_done(bksq, qm)
            pw_done(bksv, vm)
            e1 = p.s(lambda e, tt=tt, ptq=ptq, offq=offq: e.copy(
                out=q_tok[:, tt * 512:(tt + 1) * 512], in_=ptq[:, offq:offq + 512]),
                waits=[qm])
            e2 = p.s(lambda e, tt=tt, ptq=ptq, offq=offq: e.copy(
                out=k_tok[:, tt * 512:(tt + 1) * 512], in_=ptq[:, offq + 512:offq + 1024]),
                waits=[qm])
            e3 = p.s(lambda e, tt=tt, ptv=ptv, offv=offv: e.copy(
                out=v_tok[:, tt * 512:(tt + 1) * 512], in_=ptv[:, offv:offv + 512]),
                waits=[vm])
            pr_done(bksq, e1)
            pr_done(bksq, e2)
            pr_done(bksv, e3)
            tok_qkv.append([e1, e2, e3])
            if dbg_stop == "qkv":
                continue
            sq_q = p.s(lambda e, tt=tt: e.activation(
                out=sqf[:, 0:512], in_=q_tok[:, tt * 512:(tt + 1) * 512],
                func=AF.Square, bias=0.0, scale=1.0), waits=[e1, sqf_war[0]])
            red_q = p.v(lambda e, tt=tt: e.reduce_sum(
                out=ss[:, tt * 16: tt * 16 + 8], in_=vap(sqf, 0, [(64, 8), (1, 64)]),
                axis=AX.X), waits=[sq_q])
            sq_k = p.s(lambda e, tt=tt: e.activation(
                out=sqf[:, 512:1024], in_=k_tok[:, tt * 512:(tt + 1) * 512],
                func=AF.Square, bias=0.0, scale=1.0), waits=[e2, sqf_war[1]])
            red_k = p.v(lambda e, tt=tt: e.reduce_sum(
                out=ss[:, tt * 16 + 8: tt * 16 + 16],
                in_=vap(sqf, 512, [(64, 8), (1, 64)]), axis=AX.X), waits=[sq_k])
            sqf_war = [red_q, red_k]
            t_cl = p.v(lambda e, tt=tt: e.tensor_scalar(
                out=ss[:, tt * 16: tt * 16 + 16], in0=ss[:, tt * 16: tt * 16 + 16],
                scalar1=1e-24, scalar2=None, op0=ALU.max), waits=[red_q, red_k])
            t_l1 = p.s(lambda e, tt=tt: e.activation(
                out=ss[:, tt * 16: tt * 16 + 16], in_=ss[:, tt * 16: tt * 16 + 16],
                func=AF.Ln, bias=0.0, scale=1.0), waits=[t_cl])
            t_rs = p.s(lambda e, tt=tt: e.activation(
                out=ssb[:, tt * 16: tt * 16 + 16], in_=ss[:, tt * 16: tt * 16 + 16],
                func=AF.Exp, bias=0.0, scale=-0.5), waits=[t_l1])
            for ti, (srcb, ctab, stab, sq_war) in enumerate(
                    [(q_tok, cqs, sqs, sq_q), (k_tok, cks, sks, sq_k)]):
                etok = tok_qkv[tt][ti]
                a1 = p.v(lambda e, srcb=srcb, ctab=ctab, tt=tt: e.tensor_tensor(
                    out=vap(r1, 0, [(64, 8), (1, 64)]),
                    in0=vap(srcb, tt * 512, [(64, 8), (1, 64)]),
                    in1=vap(ctab, tt * 64, [(0, 8), (1, 64)]), op=ALU.mult),
                    waits=[etok] + tok_tab)
                a2 = p.g(lambda e, srcb=srcb, stab=stab, tt=tt: e.tensor_tensor(
                    out=vap(r2, 0, [(64, 8), (32, 2), (1, 32)]),
                    in0=vap(srcb, tt * 512 + 32, [(64, 8), (-32, 2), (1, 32)]),
                    in1=vap(stab, tt * 64, [(0, 8), (32, 2), (1, 32)]), op=ALU.mult),
                    waits=[etok, last_a3[0]] + tok_tab)
                a3 = p.v(lambda e: e.tensor_tensor(out=r3[:, :], in0=r1[:, :], in1=r2[:, :],
                                                   op=ALU.add), waits=[a1, a2])
                last_a3[0] = a3
                prevt = p.v(lambda e, srcb=srcb, tt=tt, ti=ti: e.tensor_tensor(
                    out=vap(srcb, tt * 512, [(64, 8), (1, 64)]),
                    in0=vap(r3, 0, [(64, 8), (1, 64)]),
                    in1=vap(ssb, tt * 16 + ti * 8, [(1, 8), (0, 64)]), op=ALU.mult),
                    waits=[a3, t_rs, sq_war])
                rot_tok[(ti, tt)] = prevt
        if dbg_stop == "qkv":
            return ("DBG", [("dbg_q", q_tok), ("dbg_k", k_tok), ("dbg_v", v_tok)],
                    [t[0] for t in tok_qkv] + [t[1] for t in tok_qkv] + [t[2] for t in tok_qkv])
        if dbg_stop == "rot":
            return ("DBG", [("dbg_q", q_tok), ("dbg_k", k_tok)],
                    [rot_tok[kk] for kk in rot_tok])
        ar.free(f"wqkv{l}")
        ar.free(f"y_bf{l}")
        for n in ("xcq", "xsq", "xck", "xsk"):
            ar.free(n + str(l))
        q_feat = ar.alloc(f"q_feat{l}", [128, 4 * T], BF16)
        k_feat = ar.alloc(f"k_feat{l}", [128, 4 * T], BF16)
        qkf_tile = {}
        tp_last = []

        def emit_transposes(tq):
            for ti, (srcb, dstb) in enumerate([(q_tok, q_feat), (k_tok, k_feat)]):
                tpo = 512 * ((2 * tq + ti) % 2)
                b = bank_of(P3, tpo)
                ptv = P3[:, tpo: tpo + 256].bitcast(BF16)
                ttr = None
                for dc in range(4):
                    ttr = p.t(lambda e, srcb=srcb, tq=tq, dc=dc, ptv=ptv,
                              start=(dc == 0), stop=(dc == 3): e.matmul(
                        ptv[:, dc * 128:(dc + 1) * 128],
                        srcb[:, tq * 512 + dc * 128: tq * 512 + dc * 128 + 128],
                        id_bf[:, :], start=start, stop=stop, is_transpose=True),
                        waits=([rot_tok[(ti, tq)], tok_idbf] + pw_waits([b]))
                        if dc == 0 else [])
                pw_done([b], ttr)
                eng = p.s if ti == 0 else p.v
                ev = eng(lambda e, dstb=dstb, tq=tq, ptv=ptv: _EngShim(e).tensor_copy(
                    out=vap(dstb, tq * 128, [(T, 4), (1, 128)]), in_=ptv[:, 0:512]),
                    waits=[ttr])
                pr_done([b], ev)
                qkf_tile[(ti, tq)] = [ev]
                tp_last.append(ev)

        o_nrm = ar.alloc(f"o_nrm{l}", [128, 4 * T], BF16, side="right")
        pn_sb = [ar.alloc(f"pn{l}_{jj}", [128, 640], BF16, side="right") for jj in range(3)]
        pz_sb = [ar.alloc(f"pz{l}_{jj}", [128, 640], BF16, side="right") for jj in range(8)]
        ptf_sb = [ar.alloc(f"ptf{l}_{jj}", [128, 640], BF16, side="right") for jj in range(3)]
        pn_free = [None, None, None]
        pz_free = [None] * 8
        ptf_free = [None, None, None]
        sc_ps = [(P2, 0), (P3, 0)]
        pt_ps = [(P1, 1024), (P1, 1536)]
        ot_ps = [(P1, 0), (P1, 512)]
        o_evac = [None, None]


        def ffn_emit_A(idx, h, w4, r):
            ch, rr = h // 2, (h % 2) * 64
            qt = w4 * 4 + r
            scj = qt % 2
            m3 = idx % 3
            m6 = idx % 8
            scp, sco = sc_ps[scj]
            sc_b = [bank_of(scp, sco), bank_of(scp, sco + 512)]
            kstart = max(0, qt * 128 + 128 - 640)
            nk = qt * 128 + 128 - kstart
            tok = None
            for ns in range(0, nk, 512):
                n = min(512, nk - ns)
                need = [qkf_tile[(0, qt)]]
                for kt in range(kstart // 128, qt + 1):
                    need.append(qkf_tile[(1, kt)])
                tok = p.t(lambda e, ch=ch, rr=rr, qt=qt, ks=kstart, ns=ns, n=n,
                          scp=scp, sco=sco: e.matmul(
                    scp[:, sco + ns: sco + ns + n],
                    q_feat[rr:rr + 64, ch * T + qt * 128: ch * T + qt * 128 + 128],
                    k_feat[rr:rr + 64, ch * T + ks + ns: ch * T + ks + ns + n],
                    start=True, stop=False),
                    waits=(pw_waits(sc_b) + need) if ns == 0 else [])
            if qt >= 4:
                tok = p.t(lambda e, scp=scp, sco=sco: e.matmul(
                    scp[:, sco: sco + 128], id_bf[:, :], mlow[:, :],
                    start=False, stop=True), waits=[])
            tok = p.t(lambda e, scp=scp, sco=sco, nk=nk: e.matmul(
                scp[:, sco + nk - 128: sco + nk], id_bf[:, :], mup[:, :],
                start=False, stop=True), waits=[])
            pw_done(sc_b, tok)
            mc = 24 + m3 * 4
            t_mx = p.v(lambda e, scp=scp, sco=sco, nk=nk, mc=mc: e.reduce_max(
                out=misc[:, mc:mc + 1], in_=scp[:, sco:sco + nk], axis=AX.X,
                negate=True), waits=[tok])
            pr_done(sc_b, t_mx)
            t_ex = p.s(lambda e, scp=scp, sco=sco, nk=nk, m3=m3, mc=mc: e.activation(
                out=pn_sb[m3][:, 0:nk], in_=scp[:, sco:sco + nk], func=AF.Exp,
                bias=misc[:, mc:mc + 1], scale=1.0,
                accum_out=misc[:, mc + 1:mc + 2]), waits=[t_mx, pn_free[m3]])
            pr_done(sc_b, t_ex)
            t_rc = p.v(lambda e, mc=mc: e.reciprocal(out=misc[:, mc + 2:mc + 3],
                                                     in_=misc[:, mc + 1:mc + 2]),
                       waits=[t_ex])
            t_nm = p.s(lambda e, m6=m6, m3=m3, nk=nk, mc=mc: e.activation(
                out=pz_sb[m6][:, 0:nk], in_=pn_sb[m3][:, 0:nk], func=AF.Identity,
                bias=0.0, scale=misc[:, mc + 2:mc + 3]),
                waits=[t_rc, pz_free[m6]])
            pn_free[m3] = t_nm
            return (h, w4, r, nk, kstart, scj, m3, m6, t_nm)

        def ffn_emit_B1(st):
            h, w4, r, nk, kstart, scj, m3, m6, t_nm = st
            ptp, pto = pt_ps[scj]
            pt_b = [bank_of(ptp, pto)]
            ntr = nk // 128
            ttk = None
            for i in range(ntr):
                ttk = p.t(lambda e, m6=m6, i=i, ptp=ptp, pto=pto,
                          stop=(i == ntr - 1): e.matmul(
                    ptp[:, pto:pto + 512].bitcast(BF16)[:, i * 128:(i + 1) * 128],
                    pz_sb[m6][:, i * 128:(i + 1) * 128], id_bf[:, :],
                    start=(i == 0), stop=stop, is_transpose=True),
                    waits=([t_nm, tok_idbf] + pw_waits(pt_b) + [ptf_free[m3]])
                    if i == 0 else [])
            pw_done(pt_b, ttk)
            pz_free[m6] = ttk
            pe_eng = p.v if (w4 * 8 + h) % 2 == 0 else p.s
            t_pe = pe_eng(lambda e, m3=m3, ptp=ptp, pto=pto, nk=nk: _EngShim(e).tensor_copy(
                out=ptf_sb[m3][:, 0:nk],
                in_=ptp[:, pto:pto + 512].bitcast(BF16)[:, 0:nk]), waits=[ttk])
            pr_done(pt_b, t_pe)
            return (h, w4, r, nk, kstart, m3, t_pe)

        def ffn_emit_B2(st2):
            h, w4, r, nk, kstart, m3, t_pe = st2
            ch, rr = h // 2, (h % 2) * 64
            otj = (w4 * 8 + h) % 2
            otp, oto = ot_ps[otj]
            ot_b = [bank_of(otp, oto)]
            ntr = nk // 128
            pvt = None
            for i in range(ntr):
                kt = (kstart + i * 128) // 128
                first = (r == 0 and i == 0)
                w0 = [t_pe, tok_qkv[kt][2]]
                if first:
                    w0 += pw_waits(ot_b) + [o_evac[otj]]
                pvt = p.t(lambda e, h=h, m3=m3, i=i, kt=kt, r=r, otp=otp,
                          oto=oto, first=first,
                          stop=(r == 3 and i == ntr - 1): e.matmul(
                    otp[0:64, oto + r * 128: oto + r * 128 + 128],
                    v_tok[:, kt * 512 + h * 64: kt * 512 + h * 64 + 64],
                    ptf_sb[m3][:, i * 128:(i + 1) * 128],
                    start=first, stop=stop),
                    waits=w0 if i == 0 else [])
            for b_ in ot_b:
                psum_wr[b_] = pvt
                if r == 0:
                    psum_rd[b_] = []
            ptf_free[m3] = pvt
            if r == 3:
                ev = p.v(lambda e, h=h, w4=w4, otp=otp, oto=oto, ch=ch, rr=rr: e.tensor_copy(
                    out=o_nrm[rr:rr + 64, ch * T + w4 * 512: ch * T + w4 * 512 + 512],
                    in_=otp[0:64, oto:oto + 512]), waits=[pvt])
                pr_done(ot_b, ev)
                o_evac[otj] = ev

        fq = []
        gq = []
        idx = 0
        for w4 in range(4):
            for tq in range(w4 * 4, w4 * 4 + 4):
                emit_transposes(tq)
            for h in range(8):
                for r in range(4):
                    fq.append(ffn_emit_A(idx, h, w4, r))
                    idx += 1
                    if len(fq) > 6:
                        gq.append(ffn_emit_B1(fq.pop(0)))
                        if len(gq) > 2:
                            ffn_emit_B2(gq.pop(0))
        while fq:
            gq.append(ffn_emit_B1(fq.pop(0)))
            if len(gq) > 2:
                ffn_emit_B2(gq.pop(0))
        while gq:
            ffn_emit_B2(gq.pop(0))
        free_sync = qkf_tile[(0, NTT - 1)] + qkf_tile[(1, NTT - 1)]
        for nm in (f"q_tok{l}", f"k_tok{l}", f"ss{l}", f"ssb{l}", f"sqf{l}",
                   f"r1_{l}", f"r2_{l}", f"r3_{l}",
                   f"cqs{l}", f"sqs{l}", f"cks{l}", f"sks{l}"):
            ar.free(nm)
        tok_lng, tok_lnb, tok_b2, tok_b1a, tok_b1g = stage_param_tiles(free_sync)
        rowbuf_free_tok[0] = tok_b2[-1]
        tok_onrm = [o_evac[0], o_evac[1]]
        if dbg_stop == "attn":
            return ("DBG", [("dbg_o", o_nrm)], tok_onrm)
        for nm in tuple(f"pn{l}_{jj}" for jj in range(3)) + \
                  tuple(f"pz{l}_{jj}" for jj in range(8)) + \
                  tuple(f"ptf{l}_{jj}" for jj in range(3)) + (f"rowbuf2_{l}", f"rowbuf3_{l}", f"rowbuf4_{l}") + (
                   f"q_feat{l}", f"k_feat{l}", f"v_tok{l}"):
            ar.free(nm)
        p.barrier()

        wo_sb = ar.alloc(f"wo_sb{l}", [128, 4 * 1024], BF16)
        tok_wo = p.gdma(lambda e: e.dma_start(
            out=vap(wo_sb, 0, [(1024, wo_sb.shape[1] // 1024), (1, 1024)]),
            in_=W["wo"][l].rearrange("(c p) m -> p c m", p=128)), chain=f"wo{l}")
        tok_res1 = []
        for m in range(8):
            for sp in range(4):
                ev = mm_group(
                    lambda k, m=m: wo_sb[:, k * 1024 + m * 128: k * 1024 + m * 128 + 128],
                    lambda k, sp=sp: o_nrm[:, k * T + sp * 512: k * T + sp * 512 + 512],
                    4, [tok_wo] + tok_onrm,
                    lambda e, ps, m=m, sp=sp: e.tensor_tensor(
                        out=ych(m, sp * 512, 512), in0=ps, in1=ych(m, sp * 512, 512),
                        op=ALU.add),
                    evac_eng="v", evac_waits=y_ready)
                tok_res1.append(ev)
        if dbg_stop == "wores":
            return ("DBG", [("dbg_y", y)], tok_res1)
        ar.free(f"wo_sb{l}")
        ar.free(f"o_nrm{l}")
        p.barrier()

        hln = ar.alloc(f"hln{l}", [128, NCH * T], BF16, side="right")
        hsc = ar.alloc(f"hsc{l}", [128, 1024], F32, side="right")
        srow = ar.alloc(f"srow{l}", [1, 2 * 1024], F32, side="right")
        sqs2 = ar.alloc(f"sqs2_{l}", [128, 1024], F32, side="right")
        rmr = ar.alloc(f"rmr{l}", [128, 2 * 1024], F32, side="right")
        tok_y2 = tok_res1

        tok_hln_all = []
        for hh in range(2):
            hlo = hh * 1024
            sum_mm, sq_mm = [], []
            prev_sqa = None
            prev_m2 = []
            for c in range(NCH):
                ysrc = (lambda lo, n, c=c, hlo=hlo: ych_sh(c, hlo + lo, n)) if c >= 4 else \
                       (lambda lo, n, c=c, hlo=hlo: ych(c, hlo + lo, n))
                prev_sqa = p.s(lambda e, ysrc=ysrc: e.activation(
                    out=sqs2[:, :], in_=ysrc(0, 1024), func=AF.Square),
                    waits=tok_y2 + [prev_sqa] + prev_m2)
                prev_m2 = []
                for sp in range(2):
                    b1_ = bank_of(P2, sp * 512)
                    m1 = p.t(lambda e, c=c, sp=sp, ysrc=ysrc: e.matmul(
                        P2[0:1, sp * 512: sp * 512 + 512], ones1[:, 0:1],
                        ysrc(sp * 512, 512), start=(c == 0), stop=(c == NCH - 1)),
                        waits=((pw_waits([b1_]) if c == 0 else []) + tok_y2 + [tok_consts]))
                    b2_ = bank_of(P3, sp * 512)
                    m2 = p.t(lambda e, sp=sp, c=c: e.matmul(
                        P3[0:1, sp * 512: sp * 512 + 512], ones1[:, 0:1],
                        sqs2[:, sp * 512: sp * 512 + 512], start=(c == 0), stop=(c == NCH - 1)),
                        waits=([prev_sqa] + (pw_waits([b2_]) if c == 0 else [])))
                    if c == 0:
                        pw_done([b1_], m1)
                        pw_done([b2_], m2)
                    else:
                        psum_wr[b1_] = m1
                        psum_wr[b2_] = m2
                    prev_m2.append(m2)
                    if c == NCH - 1:
                        sum_mm.append(m1)
                        sq_mm.append(m2)
            ev_rows = []
            for sp in range(2):
                e1 = p.v(lambda e, sp=sp: e.tensor_copy(
                    out=srow[0:1, sp * 512: sp * 512 + 512],
                    in_=P2[0:1, sp * 512: sp * 512 + 512]), waits=[sum_mm[sp]])
                pr_done([bank_of(P2, sp * 512)], e1)
                e2 = p.v(lambda e, sp=sp: e.tensor_copy(
                    out=srow[0:1, 1024 + sp * 512: 1024 + sp * 512 + 512],
                    in_=P3[0:1, sp * 512: sp * 512 + 512]), waits=[sq_mm[sp]])
                pr_done([bank_of(P3, sp * 512)], e2)
                ev_rows += [e1, e2]
            t_mu = p.v(lambda e: e.tensor_scalar(out=srow[0:1, 0:1024], in0=srow[0:1, 0:1024],
                                                 scalar1=1.0 / D, scalar2=None, op0=ALU.mult),
                       waits=ev_rows)
            t_msq = p.v(lambda e: e.tensor_tensor(out=hsc[0:1, 0:1024], in0=srow[0:1, 0:1024],
                                                  in1=srow[0:1, 0:1024], op=ALU.mult),
                        waits=[t_mu] + tok_hln_all[-1:])
            t_vv = p.v(lambda e: e.scalar_tensor_tensor(out=srow[0:1, 1024:2048],
                                                        in0=srow[0:1, 1024:2048],
                                                        scalar=1.0 / D, in1=hsc[0:1, 0:1024],
                                                        op0=ALU.mult, op1=ALU.subtract),
                       waits=[t_msq])
            t_lnv = p.s(lambda e: e.activation(out=srow[0:1, 1024:2048], in_=srow[0:1, 1024:2048],
                                               func=AF.Ln, bias=misc[0:1, 40:41], scale=1.0),
                        waits=[t_vv] + tok_eps)
            t_rsd = p.s(lambda e: e.activation(out=srow[0:1, 1024:2048], in_=srow[0:1, 1024:2048],
                                               func=AF.Exp, bias=0.0, scale=-0.5), waits=[t_lnv])
            t_nmu = p.v(lambda e: e.scalar_tensor_tensor(out=srow[0:1, 0:1024],
                                                         in0=srow[0:1, 0:1024], scalar=-1.0,
                                                         in1=srow[0:1, 1024:2048],
                                                         op0=ALU.mult, op1=ALU.mult),
                        waits=[t_rsd])
            tok_bc = []
            for roff, dcol in [(1024, 0), (0, 1024)]:
                for sp in range(2):
                    b = bank_of(P2, sp * 512)
                    bm = p.t(lambda e, roff=roff, sp=sp: e.matmul(
                        P2[:, sp * 512: sp * 512 + 512], ones_row[0:1, :],
                        srow[0:1, roff + sp * 512: roff + sp * 512 + 512],
                        start=True, stop=True), waits=[t_nmu] + pw_waits([b]))
                    pw_done([b], bm)
                    ev = p.v(lambda e, dcol=dcol, sp=sp: e.tensor_copy(
                        out=rmr[:, dcol + sp * 512: dcol + sp * 512 + 512],
                        in_=P2[:, sp * 512: sp * 512 + 512]), waits=[bm])
                    pr_done([b], ev)
                    tok_bc.append(ev)
            tok_hln = []
            prev_h = None
            for c in range(NCH):
                ysrc = ych_sh(c, hlo, 1024) if c >= 4 else ych(c, hlo, 1024)
                a1 = p.v(lambda e, ysrc=ysrc: e.tensor_tensor(out=hsc[:, :], in0=ysrc,
                                                              in1=rmr[:, 0:1024], op=ALU.mult),
                         waits=tok_bc + tok_y2 + [prev_h])
                a2 = p.v(lambda e: e.tensor_tensor(out=hsc[:, :], in0=hsc[:, :],
                                                   in1=rmr[:, 1024:2048], op=ALU.add),
                         waits=[a1])
                prev_h = p.v(lambda e, c=c, hlo=hlo: e.tensor_scalar(
                    out=hln[:, c * T + hlo: c * T + hlo + 1024], in0=hsc[:, :],
                    scalar1=lt[:, c:c + 1], scalar2=lt[:, 8 + c:9 + c],
                    op0=ALU.mult, op1=ALU.add), waits=[a2] + tok_lng + tok_lnb)
                tok_hln.append(prev_h)
            if dbg_stop == "hln" + str(hh):
                return ("DBG", [("dbg_h", hln)], tok_hln)
            tok_hln_all += tok_hln
        ar.free(f"srow{l}")
        ar.free(f"sqs2_{l}")
        ar.free(f"rmr{l}")
        ar.free(f"hsc{l}")
        p.barrier()

        for hh in range(2):
            hlo = hh * 1024
            tok_hln = tok_hln_all

            mid = ar.alloc(f"mid{l}_{hh}", [128, NHC * 1024], BF16)
            w2a_sb = ar.alloc(f"w2a_{l}{hh}", [128, 11 * 1024], BF16)
            w1a = [ar.alloc(f"w1a{l}{hh}{jj}", [128, 8 * 512], BF16) for jj in range(2)]
            w1g = [ar.alloc(f"w1g{l}{hh}{jj}", [128, 8 * 512], BF16) for jj in range(2)]
            w1_rd = [None, None]
            pre_w1 = {}
            for gi2 in (0, 1):
                g0b, gwb = (0, 128) if gi2 == 0 else (128, 128)
                ta2 = p.gdma(lambda e, g0b=g0b, gwb=gwb, gi2=gi2, w1a=w1a: e.dma_start(
                    out=vap(w1a[gi2], 0, [(gwb, 8), (1, gwb)]),
                    in_=W["w1"][l, :, g0b:g0b + gwb].rearrange("(c p) m -> p c m", p=128)),
                    chain=f"w1a{gi2}")
                tg2 = p.gdma(lambda e, g0b=g0b, gwb=gwb, gi2=gi2, w1g=w1g: e.dma_start(
                    out=vap(w1g[gi2], 0, [(gwb, 8), (1, gwb)]),
                    in_=W["w1"][l, :, HID + g0b: HID + g0b + gwb].rearrange("(c p) m -> p c m", p=128)),
                    chain=f"w1g{gi2}")
                pre_w1[gi2] = (ta2, tg2)
            p.gdma(lambda e, w2a_sb=w2a_sb: e.dma_start(
                out=vap(w2a_sb, 0, [(1024, 11), (1, 1024)]),
                in_=W["w2"][l, 0:1408, :].rearrange("(c p) m -> p c m", p=128)),
                chain="w2")
            glu_scr = ar.alloc(f"glu{l}{hh}", [128, 512], BF16, side="right")
            asc_scr = ar.alloc(f"asc{l}{hh}", [128, 2 * 512], BF16, side="right")
            asc_free = [None, None]
            groups = [(0, 128), (128, 128), (256, 256), (512, 512), (1024, 512),
                      (1536, 512), (2048, 512), (2560, 170)]
            gl_ps = [(P2, 0), (P2, 512), (P3, 0), (P3, 512)]
            gl_i = [0]
            tok_mid = []
            prev_glu = None
            for gi, (g0, gw) in enumerate(groups):
                jj = gi % 2
                if gi in pre_w1:
                    tok_wa, tok_wg = pre_w1[gi]
                else:
                    tok_wa = None
                tok_wa = tok_wa or p.gdma(lambda e, g0=g0, gw=gw, jj=jj, w1a=w1a: e.dma_start(
                    out=vap(w1a[jj], 0, [(gw, 8), (1, gw)]),
                    in_=W["w1"][l, :, g0:g0 + gw].rearrange("(c p) m -> p c m", p=128)),
                    waits=[w1_rd[jj]], chain=f"w1a{jj}")
                if gi not in pre_w1:
                    tok_wg = p.gdma(lambda e, g0=g0, gw=gw, jj=jj, w1g=w1g: e.dma_start(
                        out=vap(w1g[jj], 0, [(gw, 8), (1, gw)]),
                        in_=W["w1"][l, :, HID + g0: HID + g0 + gw].rearrange("(c p) m -> p c m", p=128)),
                        waits=[w1_rd[jj]], chain=f"w1g{jj}")
                mw = []
                off = 0
                while off < gw:
                    mw.append(min(128, gw - off))
                    off += 128
                last_mm = None
                for mi, wdt in enumerate(mw):
                    mcg = (g0 + mi * 128) // 128
                    for sp in range(2):
                        sla = gl_ps[gl_i[0] % 4]
                        slg = gl_ps[(gl_i[0] + 1) % 4]
                        gl_i[0] += 2
                        ba = [bank_of(*sla)]
                        bg = [bank_of(*slg)]
                        am = gm = None
                        for k in range(8):
                            am = p.t(lambda e, k=k, mi=mi, wdt=wdt, sp=sp, jj=jj, gw=gw,
                                     sla=sla, hlo=hlo, w1a=w1a: e.matmul(
                                sla[0][0:wdt, sla[1]:sla[1] + 512],
                                w1a[jj][:, k * gw + mi * 128: k * gw + mi * 128 + wdt],
                                hln[:, k * T + hlo + sp * 512: k * T + hlo + sp * 512 + 512],
                                start=(k == 0), stop=(k == 7)),
                                waits=([tok_wa] + tok_hln + pw_waits(ba)) if k == 0 else [])
                        for k in range(8):
                            gm = p.t(lambda e, k=k, mi=mi, wdt=wdt, sp=sp, jj=jj, gw=gw,
                                     slg=slg, hlo=hlo, w1g=w1g: e.matmul(
                                slg[0][0:wdt, slg[1]:slg[1] + 512],
                                w1g[jj][:, k * gw + mi * 128: k * gw + mi * 128 + wdt],
                                hln[:, k * T + hlo + sp * 512: k * T + hlo + sp * 512 + 512],
                                start=(k == 0), stop=(k == 7)),
                                waits=([tok_wg] + pw_waits(bg)) if k == 0 else [])
                        pw_done(ba, am)
                        pw_done(bg, gm)
                        last_mm = gm
                        jj2 = (gl_i[0] // 2) % 2
                        t_ga = p.s(lambda e, sla=sla, wdt=wdt, mcg=mcg, jj2=jj2,
                                   asc_scr=asc_scr: e.activation(
                            out=asc_scr[0:wdt, jj2 * 512:(jj2 + 1) * 512],
                            in_=sla[0][0:wdt, sla[1]:sla[1] + 512],
                            func=AF.Identity, bias=lt[0:wdt, 16 + mcg: 17 + mcg],
                            scale=1.0), waits=[am, asc_free[jj2]] + tok_b1a)
                        pr_done(ba, t_ga)
                        t_gl = p.s(lambda e, slg=slg, wdt=wdt, mcg=mcg,
                                   glu_scr=glu_scr: e.activation(
                            out=glu_scr[0:wdt, :], in_=slg[0][0:wdt, slg[1]:slg[1] + 512],
                            func=AF.Gelu, bias=lt[0:wdt, 16 + NHC + mcg: 17 + NHC + mcg],
                            scale=1.0), waits=[gm, prev_glu] + tok_b1g + tok_mid[-1:])
                        prev_glu = t_gl
                        pr_done(bg, t_gl)
                        t_md = p.v(lambda e, wdt=wdt, mcg=mcg, sp=sp, jj2=jj2,
                                   mid=mid, asc_scr=asc_scr, glu_scr=glu_scr: e.tensor_tensor(
                            out=mid[0:wdt, mcg * 1024 + sp * 512: mcg * 1024 + sp * 512 + 512],
                            in0=asc_scr[0:wdt, jj2 * 512:(jj2 + 1) * 512],
                            in1=glu_scr[0:wdt, :], op=ALU.mult), waits=[t_gl, t_ga])
                        asc_free[jj2] = t_md
                        tok_mid.append(t_md)
                w1_rd[jj] = last_mm
            for nm in (f"w1a{l}{hh}0", f"w1a{l}{hh}1", f"w1g{l}{hh}0", f"w1g{l}{hh}1",
                       f"glu{l}{hh}", f"asc{l}{hh}"):
                ar.free(nm)
            p.barrier()

            w2b_sb = ar.alloc(f"w2b_{l}{hh}", [128, 11 * 1024], BF16)
            p.gdma(lambda e, w2b_sb=w2b_sb: e.dma_start(
                out=vap(w2b_sb, 0, [(1024, 10), (1, 1024)]),
                in_=W["w2"][l, 1408:2688, :].rearrange("(c p) m -> p c m", p=128)),
                chain="w2")
            p.gdma(lambda e, w2b_sb=w2b_sb: e.dma_start(
                out=w2b_sb[0:42, 10 * 1024:11 * 1024], in_=W["w2"][l, 2688:2730, :]),
                chain="w2")
            tok_w2 = [p.chain_total("w2")]
            new_y2 = []
            for sp in range(2):
                for m in range(8):
                    sl = gl_ps[gl_i[0] % 4]
                    gl_i[0] += 1
                    bb2 = [bank_of(*sl)]
                    mmt = None
                    for kc in range(NHC):
                        wk = 128 if kc < 21 else 42
                        wt = w2a_sb if kc < 11 else w2b_sb
                        ko = kc if kc < 11 else kc - 11
                        mmt = p.t(lambda e, kc=kc, ko=ko, m=m, sp=sp, sl=sl, wk=wk,
                                  wt=wt, mid=mid: e.matmul(
                            sl[0][:, sl[1]:sl[1] + 512],
                            wt[0:wk, ko * 1024 + m * 128: ko * 1024 + m * 128 + 128],
                            mid[0:wk, kc * 1024 + sp * 512: kc * 1024 + sp * 512 + 512],
                            start=(kc == 0), stop=(kc == NHC - 1)),
                            waits=(tok_w2 + tok_mid + pw_waits(bb2)) if kc == 0 else [])
                    pw_done(bb2, mmt)
                    ev = p.v(lambda e, m=m, sp=sp, hlo=hlo, sl=sl: e.scalar_tensor_tensor(
                        out=ych(m, hlo + sp * 512, 512), in0=sl[0][:, sl[1]:sl[1] + 512],
                        scalar=b2_t[:, m:m + 1], in1=ych(m, hlo + sp * 512, 512),
                        op0=ALU.add, op1=ALU.add), waits=[mmt] + tok_b2 + tok_y2)
                    pr_done(bb2, ev)
                    new_y2.append(ev)
            tok_y2 = tok_y2 + new_y2
            if dbg_stop == "glu" + str(hh):
                return ("DBG", [("dbg_y", y)], tok_y2)
            ar.free(f"w2a_{l}{hh}")
            ar.free(f"w2b_{l}{hh}")
            ar.free(f"mid{l}_{hh}")
            p.barrier()

        ar.free(f"hln{l}")
        p.barrier()
        return tok_y2

    if upto.startswith("ffn1_"):
        r = ffn_layer(0, tok_p0, dbg_stop=upto.split("_")[1])
        _, tens, toks = r
        dbg_toks = []
        for nm, h in tens:
            d = nc.dram_tensor(nm, [128, h.shape[1]], h.dtype, kind="ExternalOutput").ap()
            p.ydma(lambda e, d=d, h=h: e.dma_start(out=d, in_=h[:, :]),
                   waits=toks, chain="dbg")
        p.emit("sync", lambda e: e.nop(), waits=[p.chain_total("dbg")], inc=None)
        with nc.Block() as block:
            p.replay(nc, block)
        return nc

    y_ready = ffn_layer(0, tok_p0)
    if upto == "ffn1":
        dbg = nc.dram_tensor("dbg_y1", [128, NCH * (T + 1)], F32, kind="ExternalOutput").ap()
        tk = p.ydma(lambda e: e.dma_start(out=dbg, in_=y[:, :]), waits=y_ready, chain="dbg")
        p.emit("sync", lambda e: e.nop(), waits=[tk], inc=None)
        with nc.Block() as block:
            p.replay(nc, block)
        return nc
    y_ready = ffn_layer(1, y_ready)
    ar.free("rowbuf_g")
    p.barrier()

    sg_scr = ar.alloc("sg_scr", [128, 2 * T], F32)
    s_ld = ar.alloc("s_ld", [128, 2 * T], F32)
    x_ld = ar.alloc("x_ld", [128, 4 * D], F32)
    o_st = ar.alloc("o_st", [128, 4 * D], F32)

    s_rd = [None, None]
    sg_rd = [None, None]
    tok_sg = []
    for c in range(NCH):
        par = c % 2
        t_s = p.ydma(lambda e, c=c, par=par: e.dma_start(
            out=s_ld[:, par * T:(par + 1) * T], in_=s_hbm[c * 128:(c + 1) * 128, :]),
            waits=[s_rd[par]] + tok_s_done, chain=f"sr{par}")
        t_si = p.s(lambda e, c=c, par=par: e.activation(
            out=sg_scr[:, par * T:(par + 1) * T], in_=ych(c), func=AF.Silu,
            bias=0.0, scale=1.0),
            waits=y_ready + [sg_rd[par]])
        t_m = (p.g if c % 2 == 0 else p.v)(lambda e, c=c, par=par: e.tensor_tensor(
            out=ych(c), in0=sg_scr[:, par * T:(par + 1) * T],
            in1=s_ld[:, par * T:(par + 1) * T], op=ALU.mult),
            waits=[t_si, t_s])
        s_rd[par] = t_m
        sg_rd[par] = t_m
        tok_sg.append(t_m)

    x_rd = [None] * 4
    o_wr = [None] * 4
    out_toks = []
    for tt in range(NTT):
        par = tt % 4
        t_x = p.ydma(lambda e, tt=tt, par=par: e.dma_start(
            out=x_ld[:, par * D:(par + 1) * D], in_=x_in[tt * 128:(tt + 1) * 128, :]),
            waits=[x_rd[par]], chain=f"xs{par % 2}")
        x_rd[par] = t_x
        ev_toks = []
        for c in range(NCH):
            bk = c % 4
            b = bank_of(P1, bk * 512)
            pt = P1[:, bk * 512: bk * 512 + 128]
            t_tr = p.t(lambda e, c=c, tt=tt, pt=pt: e.transpose(
                pt, ych(c, tt * 128, 128), id_f32[:, :]),
                waits=[tok_sg[c]] + pw_waits([b]) + [o_wr[par]])
            pw_done([b], t_tr)
            eng = p.v if c % 2 == 0 else p.v
            ev = eng(lambda e, c=c, par=par, pt=pt: e.tensor_tensor(
                out=o_st[:, par * D + c * 128: par * D + c * 128 + 128], in0=pt,
                in1=x_ld[:, par * D + c * 128: par * D + c * 128 + 128], op=ALU.add),
                waits=[t_tr, t_x])
            pr_done([b], ev)
            ev_toks.append(ev)
        o_wr[par] = p.ydma(lambda e, tt=tt, par=par: e.dma_start(
            out=out_t[tt * 128:(tt + 1) * 128, :], in_=o_st[:, par * D:(par + 1) * D]),
            waits=ev_toks, chain=f"ow{par}")
        out_toks.append(o_wr[par])
    p.emit("sync", lambda e: e.nop(), waits=out_toks, inc=None)

    with nc.Block() as block:
        p.replay(nc, block)
    return nc



_REN = {
    "W_kv": "L0_attn_W_kv", "g_kv": "L0_attn_g_kv", "W_kdec": "L0_attn_W_kdec",
    "W_vdec": "L0_attn_W_vdec", "W_q": "L0_attn_W_q", "W_qdec": "L0_attn_W_qdec",
    "W_krope": "L0_attn_W_krope", "W_qrope": "L0_attn_W_qrope", "W_o": "L0_attn_W_o",
    "wqkv": "L0_ffn_wqkv", "q_scale": "L0_ffn_q_scale", "k_scale": "L0_ffn_k_scale",
    "wo": "L0_ffn_wo", "ln_g": "L0_ffn_ln_g", "ln_b": "L0_ffn_ln_b",
    "w1": "L0_ffn_w1", "b1": "L0_ffn_b1", "w2": "L0_ffn_w2", "b2": "L0_ffn_b2",
}


def kernel(**inputs):
    consts = _host_constants()
    nc = build_program(_DEV_UPTO)
    in_maps = []
    for core in range(8):
        m = {"x": np.ascontiguousarray(np.asarray(inputs["L0_attn_x"][core], dtype=np.float32))}
        for k, src in _REN.items():
            a = np.asarray(inputs[src], dtype=np.float32)
            if k == "g_kv":
                a = a.reshape(1, -1)
            m[k] = np.ascontiguousarray(a)
        m.update(consts)
        in_maps.append(m)
    res = run_bass_kernel_spmd(nc, in_maps, core_ids=list(range(8)))
    return np.stack([np.asarray(res.results[c]["out"]) for c in range(8)], axis=0)


# version 22 (final)
# speedup vs baseline: 1.0124x; 1.0124x over previous
import numpy as np
import ml_dtypes
import concourse.bass as bass
import concourse.mybir as mybir
from concourse.bass_utils import run_bass_kernel_spmd

F32 = mybir.dt.float32
BF16 = mybir.dt.bfloat16
AF = mybir.ActivationFunctionType
ALU = mybir.AluOpType
AX = mybir.AxisListType

T = 2048
D = 1024
NCH = 8
NTT = 16
KV_RANK = 256
MLA_SCALE = 1.0 / float(np.sqrt(96.0))
HID = 2730
NHC = 22
WINDOW = 512
QK_SCALE = 8.0
NEG = -1.0e30

_DEV_UPTO = "full"
_DEV_NTT = 16
_DEV_P0_LEVEL = 4


def _host_constants():
    c = {}
    c["c_id_bf"] = np.eye(128, dtype=ml_dtypes.bfloat16)
    c["c_id_f32"] = np.eye(128, dtype=np.float32)
    low = np.where(np.arange(128)[None, :] < np.arange(128)[:, None], NEG, 0.0).astype(np.float32)
    c["c_mask_low"] = low.astype(ml_dtypes.bfloat16)
    c["c_mask_up"] = np.ascontiguousarray(low.T).astype(ml_dtypes.bfloat16)
    c["c_ones_row"] = np.ones((1, 128), dtype=np.float32)
    c["c_ones1"] = np.ones((128, 1), dtype=np.float32)
    bo2 = np.zeros((128, 2), dtype=np.float32)
    bo2[0:64, 0] = 1.0
    bo2[64:128, 1] = 1.0
    c["c_bones2"] = bo2.astype(ml_dtypes.bfloat16)
    bo4 = np.zeros((128, 4), dtype=np.float32)
    for g in range(4):
        bo4[g * 32:(g + 1) * 32, g] = 1.0
    c["c_bones4"] = bo4.astype(ml_dtypes.bfloat16)

    def tokmaj(tab, d):
        return np.ascontiguousarray(
            tab.reshape(T // 128, 128, d).transpose(1, 0, 2).reshape(128, -1)
        ).astype(np.float32)

    t = np.arange(T, dtype=np.float64)
    dim = 32
    inv = 1.0 / (10000.0 ** (np.arange(0, dim, 2, dtype=np.float64) / dim))
    fr = np.concatenate([np.outer(t, inv)] * 2, axis=1)
    sgn = np.ones(dim)
    sgn[: dim // 2] = -1.0
    c["c_rope_cos"] = tokmaj(np.cos(fr), dim)
    c["c_rope_sin"] = tokmaj(np.sin(fr) * sgn[None, :], dim)
    dh = 64
    half = np.arange(0, dh, 2, dtype=np.float64)
    invf = 1.0 / (10000.0 ** (half / dh))
    fr = np.concatenate([np.outer(t, invf)] * 2, axis=1)
    base = (half + 0.4 * dh) / (1.4 * dh)
    power = (t - T // 2) / float(WINDOW // 2)
    scale = np.concatenate([base[None, :] ** power[:, None]] * 2, axis=1)
    cosx, sinx = np.cos(fr), np.sin(fr)
    sgn2 = np.ones(dh)
    sgn2[: dh // 2] = -1.0
    c["c_xp_cq"] = tokmaj(QK_SCALE * cosx * scale, dh)
    c["c_xp_sq"] = tokmaj(QK_SCALE * sinx * scale * sgn2[None, :], dh)
    c["c_xp_ck"] = tokmaj(cosx / scale, dh)
    c["c_xp_sk"] = tokmaj(sinx / scale * sgn2[None, :], dh)
    return c


def _flat(waits):
    out = []
    for w in waits:
        if w is None:
            continue
        if isinstance(w, list):
            out.extend(_flat(w))
        else:
            out.append(w)
    return out


class Prog:
    ENGINES = ("tensor", "vector", "scalar", "gpsimd", "sync")

    def __init__(self):
        self.items = {e: [] for e in self.ENGINES}
        self.semv = {}
        self.waited = {e: {} for e in self.ENGINES}
        self.sem_names = set()

    def emit(self, eng, fn, waits=(), inc=None, amt=1):
        for s, v in _flat(list(waits)):
            if self.waited[eng].get(s, 0) < v:
                self.waited[eng][s] = v
                self.items[eng].append(("w", s, v))
        tok = None
        if inc is not None:
            self.sem_names.add(inc)
            self.semv[inc] = self.semv.get(inc, 0) + amt
            tok = (inc, self.semv[inc])
        self.items[eng].append(("i", fn, inc, amt))
        return tok

    def t(self, fn, waits=(), inc="pe"):
        return self.emit("tensor", fn, waits, inc)

    def v(self, fn, waits=(), inc="dve"):
        return self.emit("vector", fn, waits, inc)

    def s(self, fn, waits=(), inc="act"):
        return self.emit("scalar", fn, waits, inc)

    def g(self, fn, waits=(), inc="pool"):
        return self.emit("gpsimd", fn, waits, inc)

    def gdma(self, fn, waits=(), chain="g"):
        return self.emit("gpsimd", fn, waits, "D_" + chain, amt=16)

    def ydma(self, fn, waits=(), chain="y"):
        return self.emit("sync", fn, waits, "D_" + chain, amt=16)

    def barrier(self):
        toks = [(s, v) for s, v in self.semv.items() if v > 0]
        for eng in self.ENGINES:
            for s, v in toks:
                if self.waited[eng].get(s, 0) < v:
                    self.waited[eng][s] = v
                    self.items[eng].append(("w", s, v))

    def chain_total(self, chain):
        s = "D_" + chain
        v = self.semv.get(s, 0)
        return (s, v) if v > 0 else None

    def replay(self, nc, block):
        sems = {n: nc.alloc_semaphore(name=n) for n in sorted(self.sem_names)}

        def runner(name):
            def run(e):
                for it in self.items[name]:
                    if it[0] == "w":
                        e.wait_ge(sems[it[1]], it[2])
                    else:
                        _, fn, inc, amt = it
                        inst = fn(e)
                        if inc is not None:
                            inst.then_inc(sems[inc], amt)
            return run

        block.tensor(runner("tensor"))
        block.vector(runner("vector"))
        block.scalar(runner("scalar"))
        block.gpsimd(runner("gpsimd"))
        block.sync(runner("sync"))


class Arena:
    def __init__(self, nc):
        self.nc = nc
        base = (nc.sbuf_base + 63) & ~63
        self.free_list = [(base, nc.sbuf_top)]
        self.live = {}
        self.peak = 0

    def alloc(self, name, shape, dtype, side="left"):
        per = int(np.prod(shape[1:])) * mybir.dt.size(dtype)
        per = (per + 63) & ~63
        fits = [(hi - lo, i) for i, (lo, hi) in enumerate(self.free_list) if hi - lo >= per]
        fits.sort()
        if side == "right" and fits:
            fits = [max(fits, key=lambda t: self.free_list[t[1]][1])]
        for _, i in fits[:1]:
            lo, hi = self.free_list[i]
            if side == "right":
                base = hi - per
                if base == lo:
                    del self.free_list[i]
                else:
                    self.free_list[i] = (lo, base)
            else:
                base = lo
                if lo + per == hi:
                    del self.free_list[i]
                else:
                    self.free_list[i] = (lo + per, hi)
            h = self.nc.alloc_sbuf_tensor_at(name, list(shape), dtype, offset=base)
            self.live[name] = (base, base + per)
            self.peak = max(self.peak, self.used())
            return h
        raise MemoryError(f"SBUF arena: no room for {name} ({per}B/part); free={self.free_list}")

    def used(self):
        return sum(b - a for a, b in self.live.values())

    def free(self, h):
        name = h if isinstance(h, str) else h.name
        lo, hi = self.live.pop(name)
        self.free_list.append((lo, hi))
        self.free_list.sort()
        m = []
        for a, b in self.free_list:
            if m and m[-1][1] == a:
                m[-1] = (m[-1][0], b)
            else:
                m.append((a, b))
        self.free_list = m


class _EngShim:

    def __init__(self, e):
        self._e = e

    def __getattr__(self, k):
        return getattr(self._e, k)

    def tensor_copy(self, out, in_):
        if hasattr(self._e, "tensor_copy"):
            return self._e.tensor_copy(out=out, in_=in_)
        return self._e.copy(out=out, in_=in_)


def vap(h, col_off, dims):
    a = h if isinstance(h, bass.AP) else h.ap()
    return bass.AP(a.tensor, a.offset + col_off, [list(a.ap[0])] + [list(d) for d in dims])



def build_program(upto="full"):
    nc = bass.Bass("TRN2", target_bir_lowering=False)
    p = Prog()
    ar = Arena(nc)

    x_in = nc.dram_tensor("x", [T, D], F32, kind="ExternalInput").ap()
    out_t = nc.dram_tensor("out", [T, D], F32, kind="ExternalOutput").ap()

    wshapes = {
        "W_kv": [D, 256], "g_kv": [1, 256],
        "W_kdec": [256, D], "W_vdec": [256, D],
        "W_q": [D, 512], "W_qdec": [512, D],
        "W_krope": [D, 512], "W_qrope": [512, 512],
        "W_o": [D, D],
        "wqkv": [2, D, 1536], "q_scale": [2, 64], "k_scale": [2, 64],
        "wo": [2, 512, D], "ln_g": [2, D], "ln_b": [2, D],
        "w1": [2, D, 2 * HID], "b1": [2, 2 * HID], "w2": [2, HID, D], "b2": [2, D],
    }
    W = {k: nc.dram_tensor(k, v, F32, kind="ExternalInput").ap() for k, v in wshapes.items()}

    cdefs = _host_constants()
    C = {}
    for k, arr in cdefs.items():
        dt = BF16 if arr.dtype == ml_dtypes.bfloat16 else F32
        C[k] = nc.dram_tensor(k, list(arr.shape), dt, kind="ExternalInput").ap()

    s_hbm = nc.dram_tensor("s_scratch", [D, T], F32, kind="Internal").ap()
    kh_hbm = nc.dram_tensor("kh_scratch", [16, 96, T], BF16, kind="Internal").ap()
    qh_hbm = nc.dram_tensor("qh_scratch", [16, 96, T], BF16, kind="Internal").ap()

    P1 = nc.alloc_psum_tensor("P1", [128, 2048], F32)
    P2 = nc.alloc_psum_tensor("P2", [128, 1024], F32)
    P3 = nc.alloc_psum_tensor("P3", [128, 1024], F32)

    psum_wr = {b: None for b in range(8)}
    psum_rd = {b: [] for b in range(8)}

    def bank_of(pt, off):
        basebank = {id(P1): 0, id(P2): 4, id(P3): 6}[id(pt)]
        return basebank + off // 512

    def pw_waits(banks):
        w = []
        for b in banks:
            w.extend(psum_rd[b])
            w.append(psum_wr[b])
        return w

    def pw_done(banks, tok):
        for b in banks:
            psum_wr[b] = tok
            psum_rd[b] = []

    def pr_waits(banks):
        return [psum_wr[b] for b in banks]

    def pr_done(banks, tok):
        for b in banks:
            psum_rd[b].append(tok)

    id_bf = ar.alloc("id_bf", [128, 128], BF16)
    id_f32 = ar.alloc("id_f32", [128, 128], F32)
    mlow = ar.alloc("mlow", [128, 128], BF16)
    mup = ar.alloc("mup", [128, 128], BF16)
    ones_row = ar.alloc("ones_row", [1, 128], F32)
    ones1 = ar.alloc("ones1", [128, 1], F32)
    bones2 = ar.alloc("bones2", [128, 2], BF16)
    bones4 = ar.alloc("bones4", [128, 4], BF16)
    rope_cos = ar.alloc("rope_cos", [128, NTT * 32], F32)
    rope_sin = ar.alloc("rope_sin", [128, NTT * 32], F32)
    misc = ar.alloc("misc", [128, 48], F32)
    qs_t_g = ar.alloc("qs_t_g", [128, 128], F32)
    lt_g = ar.alloc("lt_g", [128, 16 + 2 * NHC], F32)
    b2_t_g = ar.alloc("b2_t_g", [128, 8], F32)
    rowbuf2_free_tok = [None]
    rowbuf_free_tok = [None]

    tok_c = []
    for n, h in [("c_id_bf", id_bf), ("c_id_f32", id_f32), ("c_mask_low", mlow),
                 ("c_mask_up", mup), ("c_ones_row", ones_row), ("c_ones1", ones1),
                 ("c_bones2", bones2), ("c_bones4", bones4),
                 ("c_rope_cos", rope_cos), ("c_rope_sin", rope_sin)]:
        tok_c.append(p.ydma(lambda e, n=n, h=h: e.dma_start(out=h[:, :], in_=C[n]),
                            chain="cst"))
    tok_consts = p.chain_total("cst")
    tok_idbf = tok_idf = tok_consts

    y = ar.alloc("y", [128, NCH * (T + 1)], F32)
    xn_bf = ar.alloc("xn_bf", [128, NCH * (T + 1)], BF16)

    def ych(c, lo=0, n=T):
        return y[:, c * (T + 1) + 1 + lo: c * (T + 1) + 1 + lo + n]

    def ych_sh(c, lo=0, n=T):
        return y[:, c * (T + 1) + lo: c * (T + 1) + lo + n]

    def xbch(c, lo=0, n=T):
        return xn_bf[:, c * (T + 1) + 1 + lo: c * (T + 1) + 1 + lo + n]

    tok_eps = [p.g(lambda e: e.memset(misc[:, 40:41], 1e-5)),
               p.g(lambda e: e.memset(misc[:, 41:42], 1e-6)),
               p.g(lambda e: e.memset(misc[:, 42:43], 1e-30))]
    tok_guards = [p.g(lambda e, c=c: e.memset(y[:, c * (T + 1): c * (T + 1) + 1], 0.0))
                  for c in range(NCH)]
    tok_guards += [p.g(lambda e, c=c: e.memset(xn_bf[:, c * (T + 1): c * (T + 1) + 1], 0.0))
                   for c in range(NCH)]

    xstage = ar.alloc("xstage", [128, 2 * D], F32)
    stat = ar.alloc("stat", [128, 16], F32)
    sq_scr0 = ar.alloc("sq_scr0", [128, D], F32)

    last_tile_tr = [None, None]
    xn_tile_tok = {}
    _lvl = globals().get("_DEV_P0_LEVEL", 4)
    _ntt = globals().get("_DEV_NTT", NTT)
    for tt in range(_ntt):
        par = tt % 2
        b0 = par * D
        xs = xstage[:, b0:b0 + D]
        xn_tile_tok.setdefault(tt, [])
        tok_xl = p.ydma(lambda e, tt=tt, b0=b0: e.dma_start(
            out=xstage[:, b0:b0 + D], in_=x_in[tt * 128:(tt + 1) * 128, :]),
            waits=[last_tile_tr[par], tok_consts], chain=f"xs{par}")
        sc = par * 8
        if _lvl < 1:
            last_tile_tr[par] = tok_xl
            continue
        t_sq = p.s(lambda e, xs=xs, sc=sc: e.activation(out=sq_scr0[:, :], in_=xs,
                                                        func=AF.Square,
                                                        accum_out=stat[:, sc + 1:sc + 2]),
                   waits=[tok_xl])
        t_sum = p.v(lambda e, xs=xs, sc=sc: e.reduce_sum(out=stat[:, sc:sc + 1], in_=xs,
                                                         axis=AX.X), waits=[tok_xl])
        t_mean = p.v(lambda e, sc=sc: e.tensor_scalar(out=stat[:, sc + 2:sc + 3],
                                                      in0=stat[:, sc:sc + 1],
                                                      scalar1=1.0 / D, scalar2=None,
                                                      op0=ALU.mult), waits=[t_sum])
        t_m2 = p.v(lambda e, sc=sc: e.tensor_tensor(out=stat[:, sc + 3:sc + 4],
                                                    in0=stat[:, sc + 2:sc + 3],
                                                    in1=stat[:, sc + 2:sc + 3],
                                                    op=ALU.mult), waits=[t_mean])
        t_var = p.v(lambda e, sc=sc: e.scalar_tensor_tensor(out=stat[:, sc + 4:sc + 5],
                                                            in0=stat[:, sc + 1:sc + 2],
                                                            scalar=1.0 / D,
                                                            in1=stat[:, sc + 3:sc + 4],
                                                            op0=ALU.mult, op1=ALU.subtract),
                    waits=[t_m2, t_sq])
        t_ln = p.s(lambda e, sc=sc: e.activation(out=stat[:, sc + 5:sc + 6],
                                                 in_=stat[:, sc + 4:sc + 5], func=AF.Ln,
                                                 bias=misc[:, 40:41], scale=1.0),
                   waits=[t_var] + tok_eps)
        t_rstd = p.s(lambda e, sc=sc: e.activation(out=stat[:, sc + 6:sc + 7],
                                                   in_=stat[:, sc + 5:sc + 6], func=AF.Exp,
                                                   bias=0.0, scale=-0.5), waits=[t_ln])
        t_nmr = p.v(lambda e, sc=sc: e.scalar_tensor_tensor(out=stat[:, sc + 7:sc + 8],
                                                            in0=stat[:, sc + 2:sc + 3],
                                                            scalar=-1.0,
                                                            in1=stat[:, sc + 6:sc + 7],
                                                            op0=ALU.mult, op1=ALU.mult),
                    waits=[t_rstd])
        if _lvl < 2:
            last_tile_tr[par] = t_nmr
            continue
        t_xn_h = []
        for hf2 in range(2):
            t_xn_h.append(p.v(lambda e, xs=xs, sc=sc, hf2=hf2: e.tensor_scalar(
                out=xs[:, hf2 * 512:(hf2 + 1) * 512],
                in0=xs[:, hf2 * 512:(hf2 + 1) * 512],
                scalar1=stat[:, sc + 6:sc + 7],
                scalar2=stat[:, sc + 7:sc + 8],
                op0=ALU.mult, op1=ALU.add),
                waits=[t_nmr]))
        t_xn = t_xn_h[-1]
        if _lvl < 3:
            last_tile_tr[par] = t_xn
            continue
        tr_last = None
        for c in range(NCH):
            bk = c % 4
            pt = P1[:, bk * 512: bk * 512 + 128]
            b = bank_of(P1, bk * 512)
            t_tr = p.t(lambda e, xs=xs, c=c, pt=pt: e.transpose(
                pt, xs[:, c * 128:(c + 1) * 128], id_f32[:, :]),
                waits=[t_xn_h[c // 4], tok_idf] + pw_waits([b]) +
                      (tok_guards if tt == 0 and c == 0 else []))
            pw_done([b], t_tr)
            tr_last = t_tr
            if _lvl < 4:
                continue
            t_e1 = p.v(lambda e, c=c, tt=tt, pt=pt: e.tensor_copy(
                out=ych(c, tt * 128, 128), in_=pt), waits=[t_tr])
            t_e2 = p.v(lambda e, c=c, tt=tt, pt=pt: e.tensor_copy(
                out=xbch(c, tt * 128, 128), in_=pt), waits=[t_tr])
            pr_done([b], t_e1)
            pr_done([b], t_e2)
            xn_tile_tok[tt].append(t_e2)
        last_tile_tr[par] = tr_last

    tok_p0 = [("dve", p.semv["dve"]), ("act", p.semv["act"])]
    ar.free("xstage")
    ar.free("sq_scr0")
    ar.free("stat")
    p.barrier()

    if upto == "p0":
        dbg = nc.dram_tensor("dbg_y", [128, NCH * (T + 1)], F32, kind="ExternalOutput").ap()
        dbg2 = nc.dram_tensor("dbg_xn", [128, NCH * (T + 1)], BF16, kind="ExternalOutput").ap()
        t1_ = p.ydma(lambda e: e.dma_start(out=dbg, in_=y[:, :]), waits=tok_p0, chain="dbg")
        t2_ = p.ydma(lambda e: e.dma_start(out=dbg2, in_=xn_bf[:, :]), waits=tok_p0, chain="dbg")
        t1_ = t2_ = p.chain_total("dbg")
        p.emit("sync", lambda e: e.nop(), waits=[t1_, t2_], inc=None)
        with nc.Block() as block:
            p.replay(nc, block)
        return nc

    def wload(name, src_ap, cols, C=None, M=None):
        h = ar.alloc(name, [128, cols], BF16)
        ap = src_ap.rearrange("(c p) m -> p c m", p=128)
        C, M = ap.shape[1], ap.shape[2]
        oap = vap(h, 0, [(M, C), (1, M)])
        tok = p.gdma(lambda e, oap=oap, ap=ap: e.dma_start(out=oap, in_=ap), chain=name)
        return h, tok

    wkv, tok_wkv = wload("wkv", W["W_kv"], 8 * 256)
    wq, tok_wq = wload("wq", W["W_q"], 8 * 512)
    wkrope, tok_wkrope = wload("wkrope", W["W_krope"], 8 * 512)
    wqrope, tok_wqrope = wload("wqrope", W["W_qrope"], 4 * 512)
    wkdec, tok_wkdec = wload("wkdec", W["W_kdec"], 2 * 1024)
    wvdec, tok_wvdec = wload("wvdec", W["W_vdec"], 2 * 1024)
    wqdec, tok_wqdec = wload("wqdec", W["W_qdec"], 4 * 1024)

    gkv_row = ar.alloc("gkv_row", [1, 256], F32)
    tok_gkvr = p.ydma(lambda e: e.dma_start(out=gkv_row[0:1, :], in_=W["g_kv"]), chain="rb")
    gkv_t = ar.alloc("gkv_t", [128, 2], F32)
    tok_gkv = None
    for cc in range(2):
        b = bank_of(P3, 0)
        tt_ = p.t(lambda e, cc=cc: e.transpose(P3[0:128, 0:1], gkv_row[0:1, cc * 128:(cc + 1) * 128],
                                               id_f32[0:1, 0:1]),
                  waits=[tok_gkvr, tok_idf] + pw_waits([b]))
        pw_done([b], tt_)
        tok_gkv = p.v(lambda e, cc=cc: e.tensor_copy(out=gkv_t[:, cc:cc + 1], in_=P3[:, 0:1]),
                      waits=[tt_])
        pr_done([b], tok_gkv)

    def row_to_cols(row_ap, ncols, dst, dst_col0, widths=None, waits=()):
        b = bank_of(P3, 512)
        tt_ = None
        for ci in range(ncols):
            tt_ = p.t(lambda e, ci=ci, row_ap=row_ap,
                      start=(ci == 0), stop=(ci == ncols - 1): e.matmul(
                P3[0:128, 512 + ci:513 + ci], row_ap[0:1, ci * 128: ci * 128 + 128],
                id_f32[0:1, 0:1], start=start, stop=stop, is_transpose=True),
                waits=(list(waits) + [tok_idf] + pw_waits([b])) if ci == 0 else [])
        pw_done([b], tt_)
        ev = p.s(lambda e: e.copy(
            out=dst[:, dst_col0: dst_col0 + ncols], in_=P3[:, 512:512 + ncols]),
            waits=[tt_])
        pr_done([b], ev)
        return [ev]

    slots = [(P2, 0), (P2, 512), (P3, 0), (P3, 512)]
    slot_i = [0]

    def mm_group(lhs_fn, rhs_fn, nk, extra_waits, evac_fn, evac_eng=None, evac_waits=()):
        if evac_eng is None:
            evac_eng = "v" if slot_i[0] % 2 == 0 else "s"
        sl = slots[slot_i[0] % 4]
        slot_i[0] += 1
        pt, off = sl
        b = bank_of(pt, off)
        mm_tok = None
        for k in range(nk):
            mm_tok = p.t(lambda e, k=k, pt=pt, off=off: e.matmul(
                pt[:, off:off + 512], lhs_fn(k), rhs_fn(k),
                start=(k == 0), stop=(k == nk - 1)),
                waits=(pw_waits([b]) + list(extra_waits)) if k == 0 else [])
        pw_done([b], mm_tok)
        eng = p.v if evac_eng == "v" else p.s
        ev = eng(lambda e, pt=pt, off=off: evac_fn(_EngShim(e), pt[:, off:off + 512]),
                 waits=[mm_tok] + list(evac_waits))
        pr_done([b], ev)
        return ev

    t_minit = p.v(lambda e: e.memset(misc[0:4, 8:12], -3.0e38))
    macc = {"qc": 8, "qr": 9, "kc": 10, "kr": 11}
    macc_tok = {k: [t_minit] for k in macc}
    sqb = ar.alloc("sqb", [128, 2 * 512], BF16)
    sqb_tok = [None, None]
    sqb_i = [0]

    cstat_pending = []

    def flush_cstat_one():
        key, G, bo, par, sq = cstat_pending.pop(0)
        sl = slots[slot_i[0] % 4]
        slot_i[0] += 1
        pt, off = sl
        b = bank_of(pt, off)
        mmt = p.t(lambda e, G=G, bo=bo, pt=pt, off=off, par=par: e.matmul(
            pt[0:G, off:off + 512], bo[:, 0:G], sqb[:, par * 512:(par + 1) * 512],
            start=True, stop=True),
            waits=[sq, tok_consts] + pw_waits([b]))
        pw_done([b], mmt)
        rc = macc[key] + 16
        rm = p.v(lambda e, G=G, pt=pt, off=off, rc=rc: e.reduce_max(
            out=misc[0:G, rc:rc + 1], in_=pt[0:G, off:off + 512], axis=AX.X),
            waits=[mmt] + macc_tok[key][-1:])
        pr_done([b], rm)
        mx = p.v(lambda e, G=G, col=macc[key], rc=rc: e.tensor_tensor(
            out=misc[0:G, col:col + 1], in0=misc[0:G, col:col + 1],
            in1=misc[0:G, rc:rc + 1], op=ALU.max), waits=[rm])
        macc_tok[key].append(mx)
        sqb_tok[par] = mmt

    def flush_cstat_all():
        while cstat_pending:
            flush_cstat_one()

    def emit_cstat(key, src_ap_fn, G, bo, waits):
        if len(cstat_pending) >= 2:
            flush_cstat_one()
        par = sqb_i[0] % 2
        sqb_i[0] += 1
        sq = p.s(lambda e, src_ap_fn=src_ap_fn, par=par: e.activation(
            out=sqb[:, par * 512:(par + 1) * 512], in_=src_ap_fn(), func=AF.Square,
            bias=0.0, scale=1.0),
            waits=list(waits) + [sqb_tok[par]])
        cstat_pending.append((key, G, bo, par, sq))
        return sq

    kvc_f = ar.alloc("kvc_f", [128, 2 * T], F32)
    kvn_bf = ar.alloc("kvn_bf", [128, 2 * T], BF16)
    scr2k = ar.alloc("scr2k", [128, T], F32)
    rms_row = ar.alloc("rms_row", [1, T], F32)
    bc_scr = ar.alloc("bc_scr", [128, 512], F32)

    tok_kvc = []
    for m in range(2):
        for sp in range(4):
            ev = mm_group(
                lambda k, m=m: wkv[:, k * 256 + m * 128: k * 256 + m * 128 + 128],
                lambda k, sp=sp: xbch(k, sp * 512, 512),
                8, [tok_wkv] + sum((xn_tile_tok[4 * sp + i] for i in range(4)), []),
                lambda e, ps, m=m, sp=sp: e.tensor_copy(
                    out=kvc_f[:, m * T + sp * 512: m * T + sp * 512 + 512], in_=ps))
            tok_kvc.append(ev)
    stats_mm = []
    for m in range(2):
        sq_tok = p.s(lambda e, m=m: e.activation(out=scr2k[:, :], in_=kvc_f[:, m * T:(m + 1) * T],
                                                 func=AF.Square), waits=tok_kvc + stats_mm)
        new_mm = []
        for sp in range(4):
            b = bank_of(P1, sp * 512)
            mmt = p.t(lambda e, sp=sp, m=m: e.matmul(P1[0:1, sp * 512: sp * 512 + 512],
                                                ones1[:, 0:1], scr2k[:, sp * 512: sp * 512 + 512],
                                                start=(m == 0), stop=(m == 1)),
                      waits=([sq_tok, tok_consts] + (pw_waits([b]) if m == 0 else [])))
            if m == 0:
                pw_done([b], mmt)
            else:
                psum_wr[b] = mmt
            new_mm.append(mmt)
        stats_mm = new_mm
    tok_rmsrow = []
    for sp in range(4):
        b = bank_of(P1, sp * 512)
        ev = p.v(lambda e, sp=sp: e.tensor_copy(out=rms_row[0:1, sp * 512: sp * 512 + 512],
                                                in_=P1[0:1, sp * 512: sp * 512 + 512]),
                 waits=[stats_mm[sp]])
        pr_done([b], ev)
        tok_rmsrow.append(ev)
    t_l = p.s(lambda e: e.activation(out=rms_row[0:1, :], in_=rms_row[0:1, :], func=AF.Ln,
                                     bias=misc[0:1, 41:42], scale=1.0 / KV_RANK),
               waits=tok_rmsrow + tok_eps)
    t_rn = p.s(lambda e: e.activation(out=rms_row[0:1, :], in_=rms_row[0:1, :], func=AF.Exp,
                                      bias=0.0, scale=-0.5), waits=[t_l])
    tok_kvn = []
    for sp in range(4):
        b = bank_of(P1, 0)
        bm = p.t(lambda e, sp=sp: e.matmul(P1[:, 0:512], ones_row[0:1, :],
                                           rms_row[0:1, sp * 512: sp * 512 + 512],
                                           start=True, stop=True),
                 waits=[t_rn, tok_consts] + pw_waits([b]))
        pw_done([b], bm)
        be = p.v(lambda e: e.tensor_copy(out=bc_scr[:, :], in_=P1[:, 0:512]),
                 waits=[bm] + tok_kvn[-2:])
        pr_done([b], be)
        for m in range(2):
            tk = p.v(lambda e, m=m, sp=sp: e.scalar_tensor_tensor(
                out=kvn_bf[:, m * T + sp * 512: m * T + sp * 512 + 512],
                in0=kvc_f[:, m * T + sp * 512: m * T + sp * 512 + 512],
                scalar=gkv_t[:, m:m + 1], in1=bc_scr[:, :], op0=ALU.mult, op1=ALU.mult),
                waits=[be, tok_gkv])
            tok_kvn.append(tk)
    ar.free("kvc_f")
    ar.free("scr2k")
    ar.free("rms_row")
    ar.free("bc_scr")
    ar.free("gkv_row")
    ar.free("gkv_t")
    p.barrier()

    qc_bf = ar.alloc("qc_bf", [128, 4 * T], BF16)
    tok_qc = []
    for m in range(4):
        for sp in range(4):
            ev = mm_group(
                lambda k, m=m: wq[:, k * 512 + m * 128: k * 512 + m * 128 + 128],
                lambda k, sp=sp: xbch(k, sp * 512, 512),
                8, [tok_wq],
                lambda e, ps, m=m, sp=sp: e.tensor_copy(
                    out=qc_bf[:, m * T + sp * 512: m * T + sp * 512 + 512], in_=ps))
            tok_qc.append(ev)
    ar.free("wkv")
    ar.free("wq")
    p.barrier()

    def rope_project(prefix, nk, lhs_fn, wsb, wtok, src_toks):
        rot_tok_t = ar.alloc(prefix + "_tok", [128, NTT * 512], BF16)
        feat = ar.alloc(prefix + "_feat", [128, 4 * T], BF16)
        t1s = ar.alloc(prefix + "_t1", [128, 512], F32)
        t2s = ar.alloc(prefix + "_t2", [128, 512], F32)
        done = []
        scr_tok = None
        for tt in range(NTT):
            pt, off = (P2, 0) if tt % 2 == 0 else (P3, 0)
            bks = [bank_of(pt, off)]
            mm_tok = None
            for k in range(nk):
                mm_tok = p.t(lambda e, tt=tt, k=k, pt=pt, off=off: e.matmul(
                    pt[:, off:off + 512], lhs_fn(k, tt * 128),
                    wsb[:, k * 512: k * 512 + 512],
                    start=(k == 0), stop=(k == nk - 1)),
                    waits=([wtok] + pw_waits(bks) + src_toks) if k == 0 else [])
            pw_done(bks, mm_tok)
            co = tt * 32
            t1 = p.v(lambda e, pt=pt, off=off, co=co: e.tensor_tensor(
                out=vap(t1s, 0, [(32, 16), (1, 32)]),
                in0=vap(pt, off, [(32, 16), (1, 32)]),
                in1=vap(rope_cos, co, [(0, 16), (1, 32)]), op=ALU.mult),
                waits=[mm_tok, tok_consts, scr_tok])
            t2a = p.v(lambda e, pt=pt, off=off, co=co: e.tensor_tensor(
                out=vap(t2s, 0, [(32, 16), (1, 16)]),
                in0=vap(pt, off + 16, [(32, 16), (1, 16)]),
                in1=vap(rope_sin, co, [(0, 16), (1, 16)]), op=ALU.mult),
                waits=[mm_tok, scr_tok])
            t2b = p.v(lambda e, pt=pt, off=off, co=co: e.tensor_tensor(
                out=vap(t2s, 16, [(32, 16), (1, 16)]),
                in0=vap(pt, off, [(32, 16), (1, 16)]),
                in1=vap(rope_sin, co + 16, [(0, 16), (1, 16)]), op=ALU.mult),
                waits=[mm_tok, scr_tok])
            scr_tok = p.v(lambda e, tt=tt: e.tensor_tensor(
                out=rot_tok_t[:, tt * 512:(tt + 1) * 512], in0=t1s[:, :], in1=t2s[:, :],
                op=ALU.add), waits=[t1, t2a, t2b])
            pr_done(bks, scr_tok)
            done.append(scr_tok)
        for tt in range(NTT):
            for dc in range(4):
                bk = (tt * 4 + dc) % 2
                b = bank_of(P1, bk * 512)
                ptv = P1[:, bk * 512: bk * 512 + 256].bitcast(BF16)
                t_tr = p.t(lambda e, tt=tt, dc=dc, ptv=ptv: e.transpose(
                    ptv[:, 0:128], rot_tok_t[:, tt * 512 + dc * 128: tt * 512 + dc * 128 + 128],
                    id_bf[:, :]),
                    waits=[done[tt], tok_idbf] + pw_waits([b]))
                pw_done([b], t_tr)
                ev = p.v(lambda e, tt=tt, dc=dc, ptv=ptv, feat=feat: e.tensor_copy(
                    out=feat[:, dc * T + tt * 128: dc * T + tt * 128 + 128], in_=ptv[:, 0:128]),
                    waits=[t_tr])
                pr_done([b], ev)
                last_ev = ev
        ar.free(prefix + "_tok")
        ar.free(prefix + "_t1")
        ar.free(prefix + "_t2")
        p.barrier()
        return feat, [last_ev]

    krope_feat, tok_krope = rope_project(
        "krope", 8, lambda k, t0: xbch(k, t0, 128), wkrope, tok_wkrope, [])
    ar.free("wkrope")
    qrope_feat, tok_qrope = rope_project(
        "qrope", 4, lambda k, t0: qc_bf[:, k * T + t0: k * T + t0 + 128],
        wqrope, tok_wqrope, tok_qc)
    ar.free("wqrope")

    tok_ropespill = []
    for key, feat, toks, hb in [("kr", krope_feat, tok_krope, kh_hbm),
                                ("qr", qrope_feat, tok_qrope, qh_hbm)]:
        for ch in range(4):
            for sp in range(4):
                emit_cstat(key, lambda feat=feat, ch=ch, sp=sp:
                           feat[:, ch * T + sp * 512: ch * T + sp * 512 + 512],
                           4, bones4, toks)
        for h in range(16):
            tok_ropespill.append(p.ydma(lambda e, h=h, feat=feat, hb=hb: e.dma_start(
                out=hb[h, 64:96, :],
                in_=feat[(h % 4) * 32:(h % 4) * 32 + 32, (h // 4) * T:(h // 4 + 1) * T]),
                waits=toks, chain="cr"))

    cont_stage = ar.alloc("cont_stage", [128, 2 * 512], BF16)
    cstage_tok = [None, None]
    tok_contspill = []
    for key, wsb, wtok, nk, src_fn, src_toks, hb in [
        ("kc", wkdec, tok_wkdec, 2,
         lambda k: kvn_bf[:, k * T: (k + 1) * T], tok_kvn, kh_hbm),
        ("qc", wqdec, tok_wqdec, 4,
         lambda k: qc_bf[:, k * T: (k + 1) * T], tok_qc, qh_hbm),
    ]:
        for m in range(8):
            for sp in range(4):
                par = (m * 4 + sp) % 2
                ev = mm_group(
                    lambda k, m=m, wsb=wsb: wsb[:, k * 1024 + m * 128: k * 1024 + m * 128 + 128],
                    lambda k, sp=sp, src_fn=src_fn: src_fn(k)[:, sp * 512: sp * 512 + 512],
                    nk, [wtok] + src_toks,
                    lambda e, ps, par=par: e.tensor_copy(
                        out=cont_stage[:, par * 512:(par + 1) * 512], in_=ps),
                    evac_waits=[cstage_tok[par]])
                sqtok = emit_cstat(
                    key, lambda par=par: cont_stage[:, par * 512:(par + 1) * 512],
                    2, bones2, [ev])
                d1 = p.ydma(lambda e, m=m, sp=sp, par=par, hb=hb: e.dma_start(
                    out=hb[2 * m, 0:64, sp * 512: sp * 512 + 512],
                    in_=cont_stage[0:64, par * 512:(par + 1) * 512]), waits=[ev],
                    chain=f"cs{par}")
                d2 = p.ydma(lambda e, m=m, sp=sp, par=par, hb=hb: e.dma_start(
                    out=hb[2 * m + 1, 0:64, sp * 512: sp * 512 + 512],
                    in_=cont_stage[64:128, par * 512:(par + 1) * 512]), waits=[],
                    chain=f"cs{par}")
                cstage_tok[par] = [p.chain_total(f"cs{par}"), sqtok]
                tok_contspill += [d1, d2]
    ar.free("cont_stage")
    ar.free("wkdec")
    ar.free("wqdec")
    ar.free("krope_feat")
    ar.free("qrope_feat")
    ar.free("xn_bf")
    ar.free("qc_bf")
    p.barrier()

    v65 = ar.alloc("v65", [128, NTT * 1040], BF16)
    tok_vones = p.g(lambda e: e.memset(vap(v65, 64, [(1040, NTT), (65, 16), (1, 1)]), 1.0))
    tok_v = []
    for tt in range(NTT):
        for hf in range(2):
            ev = mm_group(
                lambda k, tt=tt: kvn_bf[:, k * T + tt * 128: k * T + tt * 128 + 128],
                lambda k, hf=hf: wvdec[:, k * 1024 + hf * 512: k * 1024 + hf * 512 + 512],
                2, [tok_wvdec] + tok_kvn,
                lambda e, ps, tt=tt, hf=hf: e.tensor_copy(
                    out=vap(v65, tt * 1040 + hf * 8 * 65, [(65, 8), (1, 64)]),
                    in_=vap(ps, 0, [(64, 8), (1, 64)])),
                evac_eng="v", evac_waits=[tok_vones])
            tok_v.append(ev)
    flush_cstat_all()
    ar.free("wvdec")
    ar.free("kvn_bf")
    ar.free("sqb")
    p.barrier()

    fin = []
    for i, key in enumerate(["qc", "qr", "kc", "kr"]):
        G = 2 if key in ("qc", "kc") else 4
        b = bank_of(P2, 512)
        tt_ = p.t(lambda e, col=macc[key], G=G: e.transpose(
            P2[0:G, 512:513], misc[0:G, col:col + 1], id_f32[0:G, 0:G]),
            waits=macc_tok[key] + pw_waits([b]) + [tok_idf]) if False else None
        tt_ = p.t(lambda e, col=macc[key], G=G: e.transpose(
            P2[0:1, 512:512 + G], misc[0:G, col:col + 1], id_f32[0:G, 0:G]),
            waits=macc_tok[key] + pw_waits([b]) + [tok_idf])
        pw_done([b], tt_)
        fin.append(p.v(lambda e, i=i, G=G: e.reduce_max(out=misc[0:1, 13 + i:14 + i],
                                                        in_=P2[0:1, 512:512 + G], axis=AX.X),
                       waits=[tt_]))
        pr_done([b], fin[-1])
    t_mq = p.v(lambda e: e.tensor_tensor(out=misc[0:1, 17:18], in0=misc[0:1, 13:14],
                                         in1=misc[0:1, 14:15], op=ALU.add), waits=fin)
    t_mk = p.v(lambda e: e.tensor_tensor(out=misc[0:1, 18:19], in0=misc[0:1, 15:16],
                                         in1=misc[0:1, 16:17], op=ALU.add), waits=[t_mq])
    t_pr = p.v(lambda e: e.tensor_tensor(out=misc[0:1, 19:20], in0=misc[0:1, 17:18],
                                         in1=misc[0:1, 18:19], op=ALU.mult), waits=[t_mk])
    t_lnp = p.s(lambda e: e.activation(out=misc[0:1, 20:21], in_=misc[0:1, 19:20], func=AF.Ln,
                                       bias=misc[0:1, 42:43], scale=1.0),
                 waits=[t_pr] + tok_eps)
    t_sqr = p.s(lambda e: e.activation(out=misc[0:1, 21:22], in_=misc[0:1, 20:21], func=AF.Exp,
                                       bias=0.0, scale=0.5), waits=[t_lnp])
    t_negc = p.v(lambda e: e.tensor_scalar(out=misc[0:1, 22:23], in0=misc[0:1, 21:22],
                                           scalar1=-MLA_SCALE, scalar2=None, op0=ALU.mult),
                 waits=[t_sqr])
    b = bank_of(P2, 512)
    t_bc = p.t(lambda e: e.matmul(P2[:, 512:513], ones_row[0:1, :], misc[0:1, 22:23],
                                  start=True, stop=True), waits=[t_negc] + pw_waits([b]))
    pw_done([b], t_bc)
    tok_negc = p.v(lambda e: e.tensor_copy(out=misc[:, 0:1], in_=P2[:, 512:513]), waits=[t_bc])
    pr_done([b], tok_negc)

    wo_mla, tok_womla = wload("wo_mla", W["W_o"], 8 * 1024)
    attnT = ar.alloc("attnT", [128, NCH * T], BF16)
    qh_sb = [ar.alloc(f"qh{j}", [96, T], BF16) for j in range(2)]
    kh_sb = [ar.alloc(f"kh{j}", [96, T], BF16) for j in range(2)]
    pt_sb = [ar.alloc(f"ptb{j}", [128, 1024], BF16) for j in range(3)]
    bcn_sb = ar.alloc("bcn_sb", [64, 1024], F32)
    rec_row = ar.alloc("rec_row", [1, 4096], F32)

    stage_tok = [None, None]
    head_done = [None, None]
    pt_last_pv = [None, None, None]
    ot_state = {0: {"wr": None, "rd": []}, 1: {"wr": None, "rd": []}}

    def stage_head(h):
        j = h % 2
        w0 = [head_done[j]]
        if h < 2:
            w0 += [p.chain_total("cs0"), p.chain_total("cs1"), p.chain_total("cr")]
        t1 = p.ydma(lambda e, h=h, j=j: e.dma_start(out=kh_sb[j][:, :], in_=kh_hbm[h]),
                    waits=w0, chain=f"st{j}")
        t2 = p.ydma(lambda e, h=h, j=j: e.dma_start(out=qh_sb[j][:, :], in_=qh_hbm[h]),
                    waits=[], chain=f"st{j}")
        return [p.chain_total(f"st{j}")]

    stage_tok[0] = stage_head(0)
    stage_tok[1] = stage_head(1)
    sc_bufs = [(P2, 0), (P3, 0)]
    mla_exp_tok = {}
    mla_qk_last = {}
    mla_ctr = [0]

    def mla_emit_qk(h, pa, kc):
        j = h % 2
        sbuf = sc_bufs[kc % 2]
        pt, off = sbuf
        bks = [bank_of(pt, off), bank_of(pt, off + 512)]
        qlo = max(1024 * pa, 128 * kc)
        lspan = 1024 * (pa + 1) - qlo
        has_mask = 128 * kc >= 1024 * pa
        tok = None
        for ns in range(0, lspan, 512):
            n = min(512, lspan - ns)
            stop = not (has_mask and ns == 0)
            tok = p.t(lambda e, kc=kc, qlo=qlo, ns=ns, n=n, pt=pt, off=off, j=j,
                      stop=stop: e.matmul(
                pt[:, off + ns: off + ns + n],
                kh_sb[j][:, kc * 128: kc * 128 + 128],
                qh_sb[j][:, qlo + ns: qlo + ns + n],
                start=True, stop=stop),
                waits=(pw_waits(bks) + stage_tok[j]) if ns == 0 else [])
        if has_mask:
            tok = p.t(lambda e, pt=pt, off=off: e.matmul(
                pt[:, off: off + 128], id_bf[:, :], mlow[:, :],
                start=False, stop=True), waits=[tok_idbf, tok_consts])
        pw_done(bks, tok)
        mla_qk_last[h] = tok
        pb = mla_ctr[0] % 3
        mla_ctr[0] += 1
        ex = p.s(lambda e, pt=pt, off=off, lspan=lspan, pb=pb: e.activation(
            out=pt_sb[pb][:, 0:lspan], in_=pt[:, off:off + lspan], func=AF.Exp,
            bias=misc[:, 0:1], scale=MLA_SCALE),
            waits=[tok, tok_negc, pt_last_pv[pb]])
        pr_done(bks, ex)
        mla_exp_tok[(h, pa, kc)] = ex
        return (h, pa, kc, qlo, lspan, pb)

    def mla_emit_pv(st):
        h, pa, kc, qlo, lspan, pb = st
        oj = h % 2
        ot_off = oj * 1024
        ot_banks = [bank_of(P1, ot_off), bank_of(P1, ot_off + 512)]
        ex = mla_exp_tok[(h, pa, kc)]
        lq = qlo - 1024 * pa
        last_kc = {0: 3 if pa == 0 else 11, 1: 8 * (pa + 1) - 1}
        segs = []
        cur = lq
        while cur < 1024:
            nxt = min(1024, (cur // 512 + 1) * 512)
            segs.append((cur, nxt - cur))
            cur = nxt
        tok = None
        firstseg = True
        for (lo, n) in segs:
            first = (kc == 0)
            bank01 = lo // 512
            stop = (kc == last_kc[bank01])
            w0 = [ex]
            if kc == 0 and firstseg:
                w0 += ot_state[oj]["rd"] + [ot_state[oj]["wr"]]
                w0 += tok_v + [tok_vones]
            firstseg = False
            tok = p.t(lambda e, kc=kc, lo=lo, n=n, lq=lq, pb=pb, first=first,
                      stop=stop, h=h, ot_off=ot_off: e.matmul(
                P1[0:65, ot_off + lo: ot_off + lo + n],
                v65[:, kc * 1040 + h * 65: kc * 1040 + h * 65 + 65],
                pt_sb[pb][:, lo - lq: lo - lq + n],
                start=first, stop=stop), waits=w0)
        pt_last_pv[pb] = tok
        ot_state[oj]["wr"] = tok
        if kc == 0:
            ot_state[oj]["rd"] = []
        return tok

    bcn_free = [None]
    rec_i = [0]
    rec_war = [None, None]

    def mla_norm_pre(h, pa, pv_last):
        j = rec_i[0] % 2
        rec_i[0] += 1
        base = j * 2048
        ot_off = (h % 2) * 1024
        rcs = []
        for half2 in range(2):
            t_ln = p.s(lambda e, ot_off=ot_off, half2=half2, base=base: e.activation(
                out=rec_row[0:1, base + 1024 + half2 * 512: base + 1536 + half2 * 512],
                in_=P1[64:65, ot_off + half2 * 512: ot_off + half2 * 512 + 512],
                func=AF.Ln, bias=0.0, scale=1.0), waits=[pv_last, rec_war[j]])
            t_rc = p.s(lambda e, half2=half2, base=base: e.activation(
                out=rec_row[0:1, base + half2 * 512: base + (half2 + 1) * 512],
                in_=rec_row[0:1, base + 1024 + half2 * 512: base + 1536 + half2 * 512],
                func=AF.Exp, bias=0.0, scale=-1.0), waits=[t_ln])
            rcs.append(t_rc)
        return (j, base, rcs)

    def mla_emit_norm(h, pa, pre, pv_last):
        j, base, rcs = pre
        oj = h % 2
        ot_off = oj * 1024
        bb = bank_of(P3, 512)
        be = []
        bms = []
        for half2 in range(2):
            bm = p.t(lambda e, half2=half2, base=base: e.matmul(
                P3[0:64, 512:1024], ones_row[0:1, 0:64],
                rec_row[0:1, base + half2 * 512: base + (half2 + 1) * 512],
                start=True, stop=True), waits=[rcs[half2]] + pw_waits([bb]))
            pw_done([bb], bm)
            bms.append(bm)
            bev = p.s(lambda e, half2=half2: e.copy(
                out=bcn_sb[0:64, half2 * 512:(half2 + 1) * 512],
                in_=P3[0:64, 512:1024]), waits=[bm, bcn_free[0]])
            pr_done([bb], bev)
            be.append(bev)
        rec_war[j] = bms[-1]
        nt = p.v(lambda e, h=h, pa=pa, ot_off=ot_off: e.tensor_tensor(
            out=attnT[(h % 2) * 64:(h % 2) * 64 + 64,
                      (h // 2) * T + pa * 1024: (h // 2) * T + pa * 1024 + 1024],
            in0=P1[0:64, ot_off: ot_off + 1024],
            in1=bcn_sb[0:64, 0:1024], op=ALU.mult), waits=be)
        bcn_free[0] = nt
        ot_state[oj]["rd"] = [nt]

    mla_items = []
    for h in range(16):
        for pa in range(2):
            for kc in range(8 * (pa + 1)):
                mla_items.append((h, pa, kc, kc == 8 * (pa + 1) - 1))
    mq = []
    norm_ready = None
    norm_age = 0
    for ii, (h, pa, kc, is_last) in enumerate(mla_items):
        mq.append(mla_emit_qk(h, pa, kc))
        if norm_ready is not None:
            norm_age += 1
            if norm_age >= 2:
                mla_emit_norm(*norm_ready)
                norm_ready = None
        if len(mq) > 2:
            st0 = mq.pop(0)
            if st0[2] == 0 and norm_ready is not None:
                mla_emit_norm(*norm_ready)
                norm_ready = None
            pv = mla_emit_pv(st0)
            if st0[2] == 8 * (st0[1] + 1) - 1:
                pre = mla_norm_pre(st0[0], st0[1], pv)
                norm_ready = (st0[0], st0[1], pre, pv)
                norm_age = 0
        head_done[h % 2] = mla_qk_last.get(h)
        if kc == 0 and pa == 0 and 1 <= h < 15:
            stage_tok[(h + 1) % 2] = stage_head(h + 1)
    while mq:
        st0 = mq.pop(0)
        pv = mla_emit_pv(st0)
        if norm_ready is not None:
            mla_emit_norm(*norm_ready)
            norm_ready = None
        if st0[2] == 8 * (st0[1] + 1) - 1:
            pre = mla_norm_pre(st0[0], st0[1], pv)
            norm_ready = (st0[0], st0[1], pre, pv)
    if norm_ready is not None:
        mla_emit_norm(*norm_ready)

    tok_attnT = [ot_state[0]["rd"], ot_state[1]["rd"]]
    for nm in ("qh0", "qh1", "kh0", "kh1", "ptb0", "ptb1", "ptb2", "v65", "bcn_sb", "rec_row"):
        ar.free(nm)
    p.barrier()

    s_stage = ar.alloc("s_stage", [128, 2 * 512], F32)
    sstage_tok = [None, None]
    tok_s_done = []
    for m in range(8):
        for sp in range(4):
            par = (m * 4 + sp) % 2
            ev = mm_group(
                lambda k, m=m: wo_mla[:, k * 1024 + m * 128: k * 1024 + m * 128 + 128],
                lambda k, sp=sp: attnT[:, k * T + sp * 512: k * T + sp * 512 + 512],
                8, [tok_womla] + tok_attnT,
                lambda e, ps, par=par: e.tensor_copy(
                    out=s_stage[:, par * 512:(par + 1) * 512], in_=ps),
                evac_waits=[sstage_tok[par]])
            dt_ = p.ydma(lambda e, m=m, sp=sp, par=par: e.dma_start(
                out=s_hbm[m * 128:(m + 1) * 128, sp * 512: sp * 512 + 512],
                in_=s_stage[:, par * 512:(par + 1) * 512]), waits=[ev], chain=f"ss{par}")
            sstage_tok[par] = dt_
            tok_s_done.append(dt_)
    tok_s_done = [p.chain_total("ss0"), p.chain_total("ss1")]
    ar.free("s_stage")
    ar.free("attnT")
    ar.free("wo_mla")
    p.barrier()

    if upto == "mla":
        dbg = nc.dram_tensor("dbg_s", [D, T], F32, kind="ExternalOutput").ap()
        tk = p.ydma(lambda e: e.dma_start(out=dbg, in_=s_hbm), waits=tok_s_done, chain="dbg")
        p.emit("sync", lambda e: e.nop(), waits=[tk], inc=None)
        with nc.Block() as block:
            p.replay(nc, block)
        return nc

    ar.free("rope_cos")
    ar.free("rope_sin")
    p.barrier()
    rowbuf_g = ar.alloc("rowbuf_g", [1, 128], F32)

    def ffn_layer(l, y_ready, dbg_stop=None):
        xp = {n: ar.alloc(n + str(l), [128, NTT * 64], F32)
              for n in ("xcq", "xsq", "xck", "xsk")}
        for n, h in [("c_xp_cq", xp["xcq"]), ("c_xp_sq", xp["xsq"]),
                     ("c_xp_ck", xp["xck"]), ("c_xp_sk", xp["xsk"])]:
            p.ydma(lambda e, n=n, h=h: e.dma_start(out=h[:, :], in_=C[n]), chain="cst")
        tok_xp = [p.chain_total("cst")]
        y_bf = ar.alloc(f"y_bf{l}", [128, NCH * (T + 1)], BF16)

        def ybch(c, lo=0, n=T):
            return y_bf[:, c * (T + 1) + 1 + lo: c * (T + 1) + 1 + lo + n]

        def ybch_sh(c, lo=0, n=T):
            return y_bf[:, c * (T + 1) + lo: c * (T + 1) + lo + n]

        tok_ybg = [p.g(lambda e, c=c, y_bf=y_bf: e.memset(
            y_bf[:, c * (T + 1): c * (T + 1) + 1], 0.0)) for c in range(NCH)]
        rowbuf = rowbuf_g
        rowbuf_free = [rowbuf_free_tok[0]]
        t_qsr = p.ydma(lambda e: e.dma_start(out=rowbuf[0:1, 0:64], in_=W["q_scale"][l:l + 1, :]),
                       waits=rowbuf_free, chain="rb")
        t_ksr = p.ydma(lambda e: e.dma_start(out=rowbuf[0:1, 64:128], in_=W["k_scale"][l:l + 1, :]),
                       waits=[], chain="rb")
        t_qsr = t_ksr = p.chain_total("rb")
        qs_t = qs_t_g
        tok_qs = []
        for i, tkr in enumerate([t_qsr, t_ksr]):
            bb = bank_of(P3, 512)
            bm = p.t(lambda e, i=i: e.matmul(P3[:, 512:512 + 64], ones_row[0:1, :],
                                             rowbuf[0:1, i * 64:(i + 1) * 64],
                                             start=True, stop=True),
                     waits=[tkr, tok_consts] + pw_waits([bb]))
            pw_done([bb], bm)
            tk = p.v(lambda e, i=i: e.tensor_copy(out=qs_t[:, i * 64:(i + 1) * 64],
                                                  in_=P3[:, 512:512 + 64]), waits=[bm])
            pr_done([bb], tk)
            tok_qs.append(tk)
        lt = lt_g
        b2_t = b2_t_g
        def stage_param_tiles(waits0=()):
            rowbuf2 = ar.alloc(f"rowbuf2_{l}", [1, 2 * HID + 128], F32)
            rowbuf3 = ar.alloc(f"rowbuf3_{l}", [1, 3 * D], F32)
            t_zpad = p.v(lambda e: e.memset(rowbuf2[0:1, 2 * HID: 2 * HID + 128], 0.0),
                         waits=list(waits0))
            p.ydma(lambda e: e.dma_start(out=rowbuf2[0:1, 0:2 * HID],
                                         in_=W["b1"][l:l + 1, :]), chain="rb",
                   waits=list(waits0))
            p.ydma(lambda e: e.dma_start(out=rowbuf3[0:1, 0:D],
                                         in_=W["ln_g"][l:l + 1, :]), chain="rb")
            p.ydma(lambda e: e.dma_start(out=rowbuf3[0:1, D:2 * D],
                                         in_=W["ln_b"][l:l + 1, :]), chain="rb")
            p.ydma(lambda e: e.dma_start(out=rowbuf3[0:1, 2 * D:3 * D],
                                         in_=W["b2"][l:l + 1, :]), chain="rb")
            tok_rows2 = [p.chain_total("rb"), t_zpad]
            tok_lng = row_to_cols(vap(rowbuf3, 0, [(1, D)]), 8, lt, 0,
                                  waits=tok_rows2)
            tok_lnb = row_to_cols(vap(rowbuf3, D, [(1, D)]), 8, lt, 8,
                                  waits=tok_rows2)
            tok_b2 = row_to_cols(vap(rowbuf3, 2 * D, [(1, D)]), 8, b2_t, 0,
                                 waits=tok_rows2)
            wid = [128] * (NHC - 1) + [42]
            tok_b1a = row_to_cols(rowbuf2, NHC, lt, 16, widths=wid, waits=tok_rows2)
            b1g_row = vap(rowbuf2, HID, [(1, HID + 128)])
            tok_b1g = row_to_cols(b1g_row, NHC, lt, 16 + NHC, widths=wid, waits=tok_rows2)
            return tok_lng, tok_lnb, tok_b2, tok_b1a, tok_b1g

        tok_ybf = []
        for c in range(NCH):
            eng = p.v if c < 5 else p.g
            tok_ybf.append(eng(lambda e, c=c: e.tensor_copy(out=ybch(c), in_=ych(c)),
                               waits=y_ready + tok_ybg))

        cqs = ar.alloc(f"cqs{l}", [128, NTT * 64], BF16)
        sqs = ar.alloc(f"sqs{l}", [128, NTT * 64], BF16)
        cks = ar.alloc(f"cks{l}", [128, NTT * 64], BF16)
        sks = ar.alloc(f"sks{l}", [128, NTT * 64], BF16)
        tok_tab = []
        for dst, srcn, col in [(cqs, "xcq", 0), (cks, "xck", 64)]:
            tok_tab.append(p.v(lambda e, dst=dst, srcn=srcn, col=col: e.tensor_tensor(
                out=vap(dst, 0, [(64, NTT), (1, 64)]),
                in0=vap(xp[srcn], 0, [(64, NTT), (1, 64)]),
                in1=vap(qs_t, col, [(0, NTT), (1, 64)]), op=ALU.mult),
                waits=tok_qs + tok_xp))
        for dst, srcn, col in [(sqs, "xsq", 0), (sks, "xsk", 64)]:
            tok_tab.append(p.v(lambda e, dst=dst, srcn=srcn, col=col: e.tensor_tensor(
                out=vap(dst, 0, [(64, NTT), (32, 2), (1, 32)]),
                in0=vap(xp[srcn], 0, [(64, NTT), (32, 2), (1, 32)]),
                in1=vap(qs_t, col + 32, [(0, NTT), (-32, 2), (1, 32)]), op=ALU.mult),
                waits=tok_qs + tok_xp))

        wqkv_sb = ar.alloc(f"wqkv{l}", [128, 8 * 1536], BF16)
        tok_wqkv = p.gdma(lambda e: e.dma_start(
            out=vap(wqkv_sb, 0, [(1536, 4), (1, 1536)]),
            in_=W["wqkv"][l, 0:512].rearrange("(c p) m -> p c m", p=128)),
            chain=f"wqkv{l}")
        tok_wqkv2 = p.gdma(lambda e: e.dma_start(
            out=vap(wqkv_sb, 4 * 1536, [(1536, 4), (1, 1536)]),
            in_=W["wqkv"][l, 512:1024].rearrange("(c p) m -> p c m", p=128)),
            chain=f"wqkv{l}b")
        q_tok = ar.alloc(f"q_tok{l}", [128, NTT * 512], BF16)
        k_tok = ar.alloc(f"k_tok{l}", [128, NTT * 512], BF16)
        v_tok = ar.alloc(f"v_tok{l}", [128, NTT * 512], BF16)
        ss = ar.alloc(f"ss{l}", [128, NTT * 16], F32)
        ssb = ar.alloc(f"ssb{l}", [128, NTT * 16], BF16)
        sqf = ar.alloc(f"sqf{l}", [128, 2 * 512], F32)
        r1 = ar.alloc(f"r1_{l}", [128, 512], BF16)
        r2 = ar.alloc(f"r2_{l}", [128, 512], BF16)
        r3 = ar.alloc(f"r3_{l}", [128, 512], BF16)
        qk_ps = [(P2, 0), (P1, 1024)]
        v_ps = [(P1, 0), (P1, 512)]
        tok_qkv = []
        sqf_war = [None, None]
        last_a3 = [None]
        rot_tok = {}
        for tt in range(NTT):
            ptq, offq = qk_ps[tt % 2]
            ptv, offv = v_ps[tt % 2]
            bksq = [bank_of(ptq, offq), bank_of(ptq, offq + 512)]
            bksv = [bank_of(ptv, offv)]
            qm = vm = None
            for k in range(8):
                for ns in range(2):
                    qm = p.t(lambda e, k=k, tt=tt, ns=ns, ptq=ptq, offq=offq: e.matmul(
                        ptq[:, offq + ns * 512: offq + (ns + 1) * 512],
                        ybch_sh(k, tt * 128, 128) if k >= 4 else ybch(k, tt * 128, 128),
                        wqkv_sb[:, k * 1536 + ns * 512: k * 1536 + (ns + 1) * 512],
                        start=(k == 0), stop=(k == 7)),
                        waits=([tok_wqkv] + tok_ybf + pw_waits(bksq) + pw_waits(bksv))
                        if (k == 0 and ns == 0) else
                        ([tok_wqkv2] if (k == 4 and ns == 0) else []))
                vm = p.t(lambda e, k=k, tt=tt, ptv=ptv, offv=offv: e.matmul(
                    ptv[:, offv: offv + 512],
                    ybch_sh(k, tt * 128, 128) if k >= 4 else ybch(k, tt * 128, 128),
                    wqkv_sb[:, k * 1536 + 1024: k * 1536 + 1536],
                    start=(k == 0), stop=(k == 7)), waits=[])
            pw_done(bksq, qm)
            pw_done(bksv, vm)
            e1 = p.s(lambda e, tt=tt, ptq=ptq, offq=offq: e.copy(
                out=q_tok[:, tt * 512:(tt + 1) * 512], in_=ptq[:, offq:offq + 512]),
                waits=[qm])
            e2 = p.s(lambda e, tt=tt, ptq=ptq, offq=offq: e.copy(
                out=k_tok[:, tt * 512:(tt + 1) * 512], in_=ptq[:, offq + 512:offq + 1024]),
                waits=[qm])
            e3 = p.s(lambda e, tt=tt, ptv=ptv, offv=offv: e.copy(
                out=v_tok[:, tt * 512:(tt + 1) * 512], in_=ptv[:, offv:offv + 512]),
                waits=[vm])
            pr_done(bksq, e1)
            pr_done(bksq, e2)
            pr_done(bksv, e3)
            tok_qkv.append([e1, e2, e3])
            if dbg_stop == "qkv":
                continue
            sq_q = p.s(lambda e, tt=tt: e.activation(
                out=sqf[:, 0:512], in_=q_tok[:, tt * 512:(tt + 1) * 512],
                func=AF.Square, bias=0.0, scale=1.0), waits=[e1, sqf_war[0]])
            red_q = p.v(lambda e, tt=tt: e.reduce_sum(
                out=ss[:, tt * 16: tt * 16 + 8], in_=vap(sqf, 0, [(64, 8), (1, 64)]),
                axis=AX.X), waits=[sq_q])
            sq_k = p.s(lambda e, tt=tt: e.activation(
                out=sqf[:, 512:1024], in_=k_tok[:, tt * 512:(tt + 1) * 512],
                func=AF.Square, bias=0.0, scale=1.0), waits=[e2, sqf_war[1]])
            red_k = p.v(lambda e, tt=tt: e.reduce_sum(
                out=ss[:, tt * 16 + 8: tt * 16 + 16],
                in_=vap(sqf, 512, [(64, 8), (1, 64)]), axis=AX.X), waits=[sq_k])
            sqf_war = [red_q, red_k]
            t_cl = p.v(lambda e, tt=tt: e.tensor_scalar(
                out=ss[:, tt * 16: tt * 16 + 16], in0=ss[:, tt * 16: tt * 16 + 16],
                scalar1=1e-24, scalar2=None, op0=ALU.max), waits=[red_q, red_k])
            t_l1 = p.s(lambda e, tt=tt: e.activation(
                out=ss[:, tt * 16: tt * 16 + 16], in_=ss[:, tt * 16: tt * 16 + 16],
                func=AF.Ln, bias=0.0, scale=1.0), waits=[t_cl])
            t_rs = p.s(lambda e, tt=tt: e.activation(
                out=ssb[:, tt * 16: tt * 16 + 16], in_=ss[:, tt * 16: tt * 16 + 16],
                func=AF.Exp, bias=0.0, scale=-0.5), waits=[t_l1])
            for ti, (srcb, ctab, stab, sq_war) in enumerate(
                    [(q_tok, cqs, sqs, sq_q), (k_tok, cks, sks, sq_k)]):
                etok = tok_qkv[tt][ti]
                a1 = p.v(lambda e, srcb=srcb, ctab=ctab, tt=tt: e.tensor_tensor(
                    out=vap(r1, 0, [(64, 8), (1, 64)]),
                    in0=vap(srcb, tt * 512, [(64, 8), (1, 64)]),
                    in1=vap(ctab, tt * 64, [(0, 8), (1, 64)]), op=ALU.mult),
                    waits=[etok] + tok_tab)
                a2 = p.g(lambda e, srcb=srcb, stab=stab, tt=tt: e.tensor_tensor(
                    out=vap(r2, 0, [(64, 8), (32, 2), (1, 32)]),
                    in0=vap(srcb, tt * 512 + 32, [(64, 8), (-32, 2), (1, 32)]),
                    in1=vap(stab, tt * 64, [(0, 8), (32, 2), (1, 32)]), op=ALU.mult),
                    waits=[etok, last_a3[0]] + tok_tab)
                a3 = p.v(lambda e: e.tensor_tensor(out=r3[:, :], in0=r1[:, :], in1=r2[:, :],
                                                   op=ALU.add), waits=[a1, a2])
                last_a3[0] = a3
                prevt = p.v(lambda e, srcb=srcb, tt=tt, ti=ti: e.tensor_tensor(
                    out=vap(srcb, tt * 512, [(64, 8), (1, 64)]),
                    in0=vap(r3, 0, [(64, 8), (1, 64)]),
                    in1=vap(ssb, tt * 16 + ti * 8, [(1, 8), (0, 64)]), op=ALU.mult),
                    waits=[a3, t_rs, sq_war])
                rot_tok[(ti, tt)] = prevt
        if dbg_stop == "qkv":
            return ("DBG", [("dbg_q", q_tok), ("dbg_k", k_tok), ("dbg_v", v_tok)],
                    [t[0] for t in tok_qkv] + [t[1] for t in tok_qkv] + [t[2] for t in tok_qkv])
        if dbg_stop == "rot":
            return ("DBG", [("dbg_q", q_tok), ("dbg_k", k_tok)],
                    [rot_tok[kk] for kk in rot_tok])
        ar.free(f"wqkv{l}")
        ar.free(f"y_bf{l}")
        for n in ("xcq", "xsq", "xck", "xsk"):
            ar.free(n + str(l))
        q_feat = ar.alloc(f"q_feat{l}", [128, 4 * T], BF16)
        k_feat = ar.alloc(f"k_feat{l}", [128, 4 * T], BF16)
        qkf_tile = {}
        tp_last = []

        def emit_transposes(tq):
            for ti, (srcb, dstb) in enumerate([(q_tok, q_feat), (k_tok, k_feat)]):
                tpo = 512 * ((2 * tq + ti) % 2)
                b = bank_of(P3, tpo)
                ptv = P3[:, tpo: tpo + 256].bitcast(BF16)
                ttr = None
                for dc in range(4):
                    ttr = p.t(lambda e, srcb=srcb, tq=tq, dc=dc, ptv=ptv,
                              start=(dc == 0), stop=(dc == 3): e.matmul(
                        ptv[:, dc * 128:(dc + 1) * 128],
                        srcb[:, tq * 512 + dc * 128: tq * 512 + dc * 128 + 128],
                        id_bf[:, :], start=start, stop=stop, is_transpose=True),
                        waits=([rot_tok[(ti, tq)], tok_idbf] + pw_waits([b]))
                        if dc == 0 else [])
                pw_done([b], ttr)
                eng = p.s if ti == 0 else p.v
                ev = eng(lambda e, dstb=dstb, tq=tq, ptv=ptv: _EngShim(e).tensor_copy(
                    out=vap(dstb, tq * 128, [(T, 4), (1, 128)]), in_=ptv[:, 0:512]),
                    waits=[ttr])
                pr_done([b], ev)
                qkf_tile[(ti, tq)] = [ev]
                tp_last.append(ev)

        o_nrm = ar.alloc(f"o_nrm{l}", [128, 4 * T], BF16, side="right")
        pn_sb = [ar.alloc(f"pn{l}_{jj}", [128, 640], BF16, side="right") for jj in range(3)]
        pz_sb = [ar.alloc(f"pz{l}_{jj}", [128, 640], BF16, side="right") for jj in range(6)]
        ptf_sb = [ar.alloc(f"ptf{l}_{jj}", [128, 640], BF16, side="right") for jj in range(3)]
        pn_free = [None, None, None]
        pz_free = [None] * 6
        ptf_free = [None, None, None]
        sc_ps = [(P2, 0), (P3, 0)]
        pt_ps = [(P1, 1024), (P1, 1536)]
        ot_ps = [(P1, 0), (P1, 512)]
        o_evac = [None, None]


        def ffn_emit_A(idx, h, w4, r):
            ch, rr = h // 2, (h % 2) * 64
            qt = w4 * 4 + r
            scj = qt % 2
            m3 = idx % 3
            m6 = idx % 6
            scp, sco = sc_ps[scj]
            sc_b = [bank_of(scp, sco), bank_of(scp, sco + 512)]
            kstart = max(0, qt * 128 + 128 - 640)
            nk = qt * 128 + 128 - kstart
            tok = None
            for ns in range(0, nk, 512):
                n = min(512, nk - ns)
                need = [qkf_tile[(0, qt)]]
                for kt in range(kstart // 128, qt + 1):
                    need.append(qkf_tile[(1, kt)])
                tok = p.t(lambda e, ch=ch, rr=rr, qt=qt, ks=kstart, ns=ns, n=n,
                          scp=scp, sco=sco: e.matmul(
                    scp[:, sco + ns: sco + ns + n],
                    q_feat[rr:rr + 64, ch * T + qt * 128: ch * T + qt * 128 + 128],
                    k_feat[rr:rr + 64, ch * T + ks + ns: ch * T + ks + ns + n],
                    start=True, stop=False),
                    waits=(pw_waits(sc_b) + need) if ns == 0 else [])
            if qt >= 4:
                tok = p.t(lambda e, scp=scp, sco=sco: e.matmul(
                    scp[:, sco: sco + 128], id_bf[:, :], mlow[:, :],
                    start=False, stop=True), waits=[])
            tok = p.t(lambda e, scp=scp, sco=sco, nk=nk: e.matmul(
                scp[:, sco + nk - 128: sco + nk], id_bf[:, :], mup[:, :],
                start=False, stop=True), waits=[])
            pw_done(sc_b, tok)
            mc = 24 + m3 * 4
            t_mx = p.v(lambda e, scp=scp, sco=sco, nk=nk, mc=mc: e.reduce_max(
                out=misc[:, mc:mc + 1], in_=scp[:, sco:sco + nk], axis=AX.X,
                negate=True), waits=[tok])
            pr_done(sc_b, t_mx)
            t_ex = p.s(lambda e, scp=scp, sco=sco, nk=nk, m3=m3, mc=mc: e.activation(
                out=pn_sb[m3][:, 0:nk], in_=scp[:, sco:sco + nk], func=AF.Exp,
                bias=misc[:, mc:mc + 1], scale=1.0,
                accum_out=misc[:, mc + 1:mc + 2]), waits=[t_mx, pn_free[m3]])
            pr_done(sc_b, t_ex)
            t_rc = p.v(lambda e, mc=mc: e.reciprocal(out=misc[:, mc + 2:mc + 3],
                                                     in_=misc[:, mc + 1:mc + 2]),
                       waits=[t_ex])
            t_nm = p.s(lambda e, m6=m6, m3=m3, nk=nk, mc=mc: e.activation(
                out=pz_sb[m6][:, 0:nk], in_=pn_sb[m3][:, 0:nk], func=AF.Identity,
                bias=0.0, scale=misc[:, mc + 2:mc + 3]),
                waits=[t_rc, pz_free[m6]])
            pn_free[m3] = t_nm
            return (h, w4, r, nk, kstart, scj, m3, m6, t_nm)

        def ffn_emit_B1(st):
            h, w4, r, nk, kstart, scj, m3, m6, t_nm = st
            ptp, pto = pt_ps[scj]
            pt_b = [bank_of(ptp, pto)]
            ntr = nk // 128
            ttk = None
            for i in range(ntr):
                ttk = p.t(lambda e, m6=m6, i=i, ptp=ptp, pto=pto,
                          stop=(i == ntr - 1): e.matmul(
                    ptp[:, pto:pto + 512].bitcast(BF16)[:, i * 128:(i + 1) * 128],
                    pz_sb[m6][:, i * 128:(i + 1) * 128], id_bf[:, :],
                    start=(i == 0), stop=stop, is_transpose=True),
                    waits=([t_nm, tok_idbf] + pw_waits(pt_b) + [ptf_free[m3]])
                    if i == 0 else [])
            pw_done(pt_b, ttk)
            pz_free[m6] = ttk
            pe_eng = p.v if (w4 * 8 + h) % 2 == 0 else p.s
            t_pe = pe_eng(lambda e, m3=m3, ptp=ptp, pto=pto, nk=nk: _EngShim(e).tensor_copy(
                out=ptf_sb[m3][:, 0:nk],
                in_=ptp[:, pto:pto + 512].bitcast(BF16)[:, 0:nk]), waits=[ttk])
            pr_done(pt_b, t_pe)
            return (h, w4, r, nk, kstart, m3, t_pe)

        def ffn_emit_B2(st2):
            h, w4, r, nk, kstart, m3, t_pe = st2
            ch, rr = h // 2, (h % 2) * 64
            otj = (w4 * 8 + h) % 2
            otp, oto = ot_ps[otj]
            ot_b = [bank_of(otp, oto)]
            ntr = nk // 128
            pvt = None
            for i in range(ntr):
                kt = (kstart + i * 128) // 128
                first = (r == 0 and i == 0)
                w0 = [t_pe, tok_qkv[kt][2]]
                if first:
                    w0 += pw_waits(ot_b) + [o_evac[otj]]
                pvt = p.t(lambda e, h=h, m3=m3, i=i, kt=kt, r=r, otp=otp,
                          oto=oto, first=first,
                          stop=(r == 3 and i == ntr - 1): e.matmul(
                    otp[0:64, oto + r * 128: oto + r * 128 + 128],
                    v_tok[:, kt * 512 + h * 64: kt * 512 + h * 64 + 64],
                    ptf_sb[m3][:, i * 128:(i + 1) * 128],
                    start=first, stop=stop),
                    waits=w0 if i == 0 else [])
            for b_ in ot_b:
                psum_wr[b_] = pvt
                if r == 0:
                    psum_rd[b_] = []
            ptf_free[m3] = pvt
            if r == 3:
                ev = p.v(lambda e, h=h, w4=w4, otp=otp, oto=oto, ch=ch, rr=rr: e.tensor_copy(
                    out=o_nrm[rr:rr + 64, ch * T + w4 * 512: ch * T + w4 * 512 + 512],
                    in_=otp[0:64, oto:oto + 512]), waits=[pvt])
                pr_done(ot_b, ev)
                o_evac[otj] = ev

        fq = []
        gq = []
        idx = 0
        for w4 in range(4):
            for tq in range(w4 * 4, w4 * 4 + 4):
                emit_transposes(tq)
            for h in range(8):
                for r in range(4):
                    fq.append(ffn_emit_A(idx, h, w4, r))
                    idx += 1
                    if len(fq) > 4:
                        gq.append(ffn_emit_B1(fq.pop(0)))
                        if len(gq) > 2:
                            ffn_emit_B2(gq.pop(0))
        while fq:
            gq.append(ffn_emit_B1(fq.pop(0)))
            if len(gq) > 2:
                ffn_emit_B2(gq.pop(0))
        while gq:
            ffn_emit_B2(gq.pop(0))
        free_sync = qkf_tile[(0, NTT - 1)] + qkf_tile[(1, NTT - 1)]
        for nm in (f"q_tok{l}", f"k_tok{l}", f"ss{l}", f"ssb{l}", f"sqf{l}",
                   f"r1_{l}", f"r2_{l}", f"r3_{l}",
                   f"cqs{l}", f"sqs{l}", f"cks{l}", f"sks{l}"):
            ar.free(nm)
        tok_lng, tok_lnb, tok_b2, tok_b1a, tok_b1g = stage_param_tiles(free_sync)
        rowbuf_free_tok[0] = tok_b2[-1]
        tok_onrm = [o_evac[0], o_evac[1]]
        if dbg_stop == "attn":
            return ("DBG", [("dbg_o", o_nrm)], tok_onrm)
        for nm in tuple(f"pn{l}_{jj}" for jj in range(3)) + \
                  tuple(f"pz{l}_{jj}" for jj in range(6)) + \
                  tuple(f"ptf{l}_{jj}" for jj in range(3)) + (f"rowbuf2_{l}", f"rowbuf3_{l}") + (
                   f"q_feat{l}", f"k_feat{l}", f"v_tok{l}"):
            ar.free(nm)
        p.barrier()

        wo_sb = ar.alloc(f"wo_sb{l}", [128, 4 * 1024], BF16)
        tok_wo = p.gdma(lambda e: e.dma_start(
            out=vap(wo_sb, 0, [(1024, wo_sb.shape[1] // 1024), (1, 1024)]),
            in_=W["wo"][l].rearrange("(c p) m -> p c m", p=128)), chain=f"wo{l}")
        tok_res1 = []
        for m in range(8):
            for sp in range(4):
                ev = mm_group(
                    lambda k, m=m: wo_sb[:, k * 1024 + m * 128: k * 1024 + m * 128 + 128],
                    lambda k, sp=sp: o_nrm[:, k * T + sp * 512: k * T + sp * 512 + 512],
                    4, [tok_wo] + tok_onrm,
                    lambda e, ps, m=m, sp=sp: e.tensor_tensor(
                        out=ych(m, sp * 512, 512), in0=ps, in1=ych(m, sp * 512, 512),
                        op=ALU.add),
                    evac_eng="v", evac_waits=y_ready)
                tok_res1.append(ev)
        if dbg_stop == "wores":
            return ("DBG", [("dbg_y", y)], tok_res1)
        ar.free(f"wo_sb{l}")
        ar.free(f"o_nrm{l}")
        p.barrier()

        hln = ar.alloc(f"hln{l}", [128, NCH * T], BF16, side="right")
        hsc = ar.alloc(f"hsc{l}", [128, 1024], F32, side="right")
        srow = ar.alloc(f"srow{l}", [1, 2 * 1024], F32, side="right")
        sqs2 = ar.alloc(f"sqs2_{l}", [128, 1024], F32, side="right")
        rmr = ar.alloc(f"rmr{l}", [128, 2 * 1024], F32, side="right")
        tok_y2 = tok_res1

        tok_hln_all = []
        for hh in range(2):
            hlo = hh * 1024
            sum_mm, sq_mm = [], []
            prev_sqa = None
            prev_m2 = []
            for c in range(NCH):
                ysrc = (lambda lo, n, c=c, hlo=hlo: ych_sh(c, hlo + lo, n)) if c >= 4 else \
                       (lambda lo, n, c=c, hlo=hlo: ych(c, hlo + lo, n))
                prev_sqa = p.s(lambda e, ysrc=ysrc: e.activation(
                    out=sqs2[:, :], in_=ysrc(0, 1024), func=AF.Square),
                    waits=tok_y2 + [prev_sqa] + prev_m2)
                prev_m2 = []
                for sp in range(2):
                    b1_ = bank_of(P2, sp * 512)
                    m1 = p.t(lambda e, c=c, sp=sp, ysrc=ysrc: e.matmul(
                        P2[0:1, sp * 512: sp * 512 + 512], ones1[:, 0:1],
                        ysrc(sp * 512, 512), start=(c == 0), stop=(c == NCH - 1)),
                        waits=((pw_waits([b1_]) if c == 0 else []) + tok_y2 + [tok_consts]))
                    b2_ = bank_of(P3, sp * 512)
                    m2 = p.t(lambda e, sp=sp, c=c: e.matmul(
                        P3[0:1, sp * 512: sp * 512 + 512], ones1[:, 0:1],
                        sqs2[:, sp * 512: sp * 512 + 512], start=(c == 0), stop=(c == NCH - 1)),
                        waits=([prev_sqa] + (pw_waits([b2_]) if c == 0 else [])))
                    if c == 0:
                        pw_done([b1_], m1)
                        pw_done([b2_], m2)
                    else:
                        psum_wr[b1_] = m1
                        psum_wr[b2_] = m2
                    prev_m2.append(m2)
                    if c == NCH - 1:
                        sum_mm.append(m1)
                        sq_mm.append(m2)
            ev_rows = []
            for sp in range(2):
                e1 = p.v(lambda e, sp=sp: e.tensor_copy(
                    out=srow[0:1, sp * 512: sp * 512 + 512],
                    in_=P2[0:1, sp * 512: sp * 512 + 512]), waits=[sum_mm[sp]])
                pr_done([bank_of(P2, sp * 512)], e1)
                e2 = p.v(lambda e, sp=sp: e.tensor_copy(
                    out=srow[0:1, 1024 + sp * 512: 1024 + sp * 512 + 512],
                    in_=P3[0:1, sp * 512: sp * 512 + 512]), waits=[sq_mm[sp]])
                pr_done([bank_of(P3, sp * 512)], e2)
                ev_rows += [e1, e2]
            t_mu = p.v(lambda e: e.tensor_scalar(out=srow[0:1, 0:1024], in0=srow[0:1, 0:1024],
                                                 scalar1=1.0 / D, scalar2=None, op0=ALU.mult),
                       waits=ev_rows)
            t_msq = p.v(lambda e: e.tensor_tensor(out=hsc[0:1, 0:1024], in0=srow[0:1, 0:1024],
                                                  in1=srow[0:1, 0:1024], op=ALU.mult),
                        waits=[t_mu] + tok_hln_all[-1:])
            t_vv = p.v(lambda e: e.scalar_tensor_tensor(out=srow[0:1, 1024:2048],
                                                        in0=srow[0:1, 1024:2048],
                                                        scalar=1.0 / D, in1=hsc[0:1, 0:1024],
                                                        op0=ALU.mult, op1=ALU.subtract),
                       waits=[t_msq])
            t_lnv = p.s(lambda e: e.activation(out=srow[0:1, 1024:2048], in_=srow[0:1, 1024:2048],
                                               func=AF.Ln, bias=misc[0:1, 40:41], scale=1.0),
                        waits=[t_vv] + tok_eps)
            t_rsd = p.s(lambda e: e.activation(out=srow[0:1, 1024:2048], in_=srow[0:1, 1024:2048],
                                               func=AF.Exp, bias=0.0, scale=-0.5), waits=[t_lnv])
            t_nmu = p.v(lambda e: e.scalar_tensor_tensor(out=srow[0:1, 0:1024],
                                                         in0=srow[0:1, 0:1024], scalar=-1.0,
                                                         in1=srow[0:1, 1024:2048],
                                                         op0=ALU.mult, op1=ALU.mult),
                        waits=[t_rsd])
            tok_bc = []
            for roff, dcol in [(1024, 0), (0, 1024)]:
                for sp in range(2):
                    b = bank_of(P2, sp * 512)
                    bm = p.t(lambda e, roff=roff, sp=sp: e.matmul(
                        P2[:, sp * 512: sp * 512 + 512], ones_row[0:1, :],
                        srow[0:1, roff + sp * 512: roff + sp * 512 + 512],
                        start=True, stop=True), waits=[t_nmu] + pw_waits([b]))
                    pw_done([b], bm)
                    ev = p.v(lambda e, dcol=dcol, sp=sp: e.tensor_copy(
                        out=rmr[:, dcol + sp * 512: dcol + sp * 512 + 512],
                        in_=P2[:, sp * 512: sp * 512 + 512]), waits=[bm])
                    pr_done([b], ev)
                    tok_bc.append(ev)
            tok_hln = []
            prev_h = None
            for c in range(NCH):
                ysrc = ych_sh(c, hlo, 1024) if c >= 4 else ych(c, hlo, 1024)
                a1 = p.v(lambda e, ysrc=ysrc: e.tensor_tensor(out=hsc[:, :], in0=ysrc,
                                                              in1=rmr[:, 0:1024], op=ALU.mult),
                         waits=tok_bc + tok_y2 + [prev_h])
                a2 = p.v(lambda e: e.tensor_tensor(out=hsc[:, :], in0=hsc[:, :],
                                                   in1=rmr[:, 1024:2048], op=ALU.add),
                         waits=[a1])
                prev_h = p.v(lambda e, c=c, hlo=hlo: e.tensor_scalar(
                    out=hln[:, c * T + hlo: c * T + hlo + 1024], in0=hsc[:, :],
                    scalar1=lt[:, c:c + 1], scalar2=lt[:, 8 + c:9 + c],
                    op0=ALU.mult, op1=ALU.add), waits=[a2] + tok_lng + tok_lnb)
                tok_hln.append(prev_h)
            if dbg_stop == "hln" + str(hh):
                return ("DBG", [("dbg_h", hln)], tok_hln)
            tok_hln_all += tok_hln
        ar.free(f"srow{l}")
        ar.free(f"sqs2_{l}")
        ar.free(f"rmr{l}")
        ar.free(f"hsc{l}")
        p.barrier()

        for hh in range(2):
            hlo = hh * 1024
            tok_hln = tok_hln_all

            mid = ar.alloc(f"mid{l}_{hh}", [128, NHC * 1024], BF16)
            w2a_sb = ar.alloc(f"w2a_{l}{hh}", [128, 11 * 1024], BF16)
            w1a = [ar.alloc(f"w1a{l}{hh}{jj}", [128, 8 * 512], BF16) for jj in range(2)]
            w1g = [ar.alloc(f"w1g{l}{hh}{jj}", [128, 8 * 512], BF16) for jj in range(2)]
            w1_rd = [None, None]
            pre_w1 = {}
            for gi2 in (0, 1):
                g0b, gwb = (0, 128) if gi2 == 0 else (128, 128)
                ta2 = p.gdma(lambda e, g0b=g0b, gwb=gwb, gi2=gi2, w1a=w1a: e.dma_start(
                    out=vap(w1a[gi2], 0, [(gwb, 8), (1, gwb)]),
                    in_=W["w1"][l, :, g0b:g0b + gwb].rearrange("(c p) m -> p c m", p=128)),
                    chain=f"w1a{gi2}")
                tg2 = p.gdma(lambda e, g0b=g0b, gwb=gwb, gi2=gi2, w1g=w1g: e.dma_start(
                    out=vap(w1g[gi2], 0, [(gwb, 8), (1, gwb)]),
                    in_=W["w1"][l, :, HID + g0b: HID + g0b + gwb].rearrange("(c p) m -> p c m", p=128)),
                    chain=f"w1g{gi2}")
                pre_w1[gi2] = (ta2, tg2)
            p.gdma(lambda e, w2a_sb=w2a_sb: e.dma_start(
                out=vap(w2a_sb, 0, [(1024, 11), (1, 1024)]),
                in_=W["w2"][l, 0:1408, :].rearrange("(c p) m -> p c m", p=128)),
                chain="w2")
            glu_scr = ar.alloc(f"glu{l}{hh}", [128, 512], BF16, side="right")
            asc_scr = ar.alloc(f"asc{l}{hh}", [128, 2 * 512], BF16, side="right")
            asc_free = [None, None]
            groups = [(0, 128), (128, 128), (256, 256), (512, 512), (1024, 512),
                      (1536, 512), (2048, 512), (2560, 170)]
            gl_ps = [(P2, 0), (P2, 512), (P3, 0), (P3, 512)]
            gl_i = [0]
            tok_mid = []
            prev_glu = None
            for gi, (g0, gw) in enumerate(groups):
                jj = gi % 2
                if gi in pre_w1:
                    tok_wa, tok_wg = pre_w1[gi]
                else:
                    tok_wa = None
                tok_wa = tok_wa or p.gdma(lambda e, g0=g0, gw=gw, jj=jj, w1a=w1a: e.dma_start(
                    out=vap(w1a[jj], 0, [(gw, 8), (1, gw)]),
                    in_=W["w1"][l, :, g0:g0 + gw].rearrange("(c p) m -> p c m", p=128)),
                    waits=[w1_rd[jj]], chain=f"w1a{jj}")
                if gi not in pre_w1:
                    tok_wg = p.gdma(lambda e, g0=g0, gw=gw, jj=jj, w1g=w1g: e.dma_start(
                        out=vap(w1g[jj], 0, [(gw, 8), (1, gw)]),
                        in_=W["w1"][l, :, HID + g0: HID + g0 + gw].rearrange("(c p) m -> p c m", p=128)),
                        waits=[w1_rd[jj]], chain=f"w1g{jj}")
                mw = []
                off = 0
                while off < gw:
                    mw.append(min(128, gw - off))
                    off += 128
                last_mm = None
                for mi, wdt in enumerate(mw):
                    mcg = (g0 + mi * 128) // 128
                    for sp in range(2):
                        sla = gl_ps[gl_i[0] % 4]
                        slg = gl_ps[(gl_i[0] + 1) % 4]
                        gl_i[0] += 2
                        ba = [bank_of(*sla)]
                        bg = [bank_of(*slg)]
                        am = gm = None
                        for k in range(8):
                            am = p.t(lambda e, k=k, mi=mi, wdt=wdt, sp=sp, jj=jj, gw=gw,
                                     sla=sla, hlo=hlo, w1a=w1a: e.matmul(
                                sla[0][0:wdt, sla[1]:sla[1] + 512],
                                w1a[jj][:, k * gw + mi * 128: k * gw + mi * 128 + wdt],
                                hln[:, k * T + hlo + sp * 512: k * T + hlo + sp * 512 + 512],
                                start=(k == 0), stop=(k == 7)),
                                waits=([tok_wa] + tok_hln + pw_waits(ba)) if k == 0 else [])
                        for k in range(8):
                            gm = p.t(lambda e, k=k, mi=mi, wdt=wdt, sp=sp, jj=jj, gw=gw,
                                     slg=slg, hlo=hlo, w1g=w1g: e.matmul(
                                slg[0][0:wdt, slg[1]:slg[1] + 512],
                                w1g[jj][:, k * gw + mi * 128: k * gw + mi * 128 + wdt],
                                hln[:, k * T + hlo + sp * 512: k * T + hlo + sp * 512 + 512],
                                start=(k == 0), stop=(k == 7)),
                                waits=([tok_wg] + pw_waits(bg)) if k == 0 else [])
                        pw_done(ba, am)
                        pw_done(bg, gm)
                        last_mm = gm
                        jj2 = (gl_i[0] // 2) % 2
                        t_ga = p.s(lambda e, sla=sla, wdt=wdt, mcg=mcg, jj2=jj2,
                                   asc_scr=asc_scr: e.activation(
                            out=asc_scr[0:wdt, jj2 * 512:(jj2 + 1) * 512],
                            in_=sla[0][0:wdt, sla[1]:sla[1] + 512],
                            func=AF.Identity, bias=lt[0:wdt, 16 + mcg: 17 + mcg],
                            scale=1.0), waits=[am, asc_free[jj2]] + tok_b1a)
                        pr_done(ba, t_ga)
                        t_gl = p.s(lambda e, slg=slg, wdt=wdt, mcg=mcg,
                                   glu_scr=glu_scr: e.activation(
                            out=glu_scr[0:wdt, :], in_=slg[0][0:wdt, slg[1]:slg[1] + 512],
                            func=AF.Gelu, bias=lt[0:wdt, 16 + NHC + mcg: 17 + NHC + mcg],
                            scale=1.0), waits=[gm, prev_glu] + tok_b1g + tok_mid[-1:])
                        prev_glu = t_gl
                        pr_done(bg, t_gl)
                        t_md = p.v(lambda e, wdt=wdt, mcg=mcg, sp=sp, jj2=jj2,
                                   mid=mid, asc_scr=asc_scr, glu_scr=glu_scr: e.tensor_tensor(
                            out=mid[0:wdt, mcg * 1024 + sp * 512: mcg * 1024 + sp * 512 + 512],
                            in0=asc_scr[0:wdt, jj2 * 512:(jj2 + 1) * 512],
                            in1=glu_scr[0:wdt, :], op=ALU.mult), waits=[t_gl, t_ga])
                        asc_free[jj2] = t_md
                        tok_mid.append(t_md)
                w1_rd[jj] = last_mm
            for nm in (f"w1a{l}{hh}0", f"w1a{l}{hh}1", f"w1g{l}{hh}0", f"w1g{l}{hh}1",
                       f"glu{l}{hh}", f"asc{l}{hh}"):
                ar.free(nm)
            p.barrier()

            w2b_sb = ar.alloc(f"w2b_{l}{hh}", [128, 11 * 1024], BF16)
            p.gdma(lambda e, w2b_sb=w2b_sb: e.dma_start(
                out=vap(w2b_sb, 0, [(1024, 10), (1, 1024)]),
                in_=W["w2"][l, 1408:2688, :].rearrange("(c p) m -> p c m", p=128)),
                chain="w2")
            p.gdma(lambda e, w2b_sb=w2b_sb: e.dma_start(
                out=w2b_sb[0:42, 10 * 1024:11 * 1024], in_=W["w2"][l, 2688:2730, :]),
                chain="w2")
            tok_w2 = [p.chain_total("w2")]
            new_y2 = []
            for sp in range(2):
                for m in range(8):
                    sl = gl_ps[gl_i[0] % 4]
                    gl_i[0] += 1
                    bb2 = [bank_of(*sl)]
                    mmt = None
                    for kc in range(NHC):
                        wk = 128 if kc < 21 else 42
                        wt = w2a_sb if kc < 11 else w2b_sb
                        ko = kc if kc < 11 else kc - 11
                        mmt = p.t(lambda e, kc=kc, ko=ko, m=m, sp=sp, sl=sl, wk=wk,
                                  wt=wt, mid=mid: e.matmul(
                            sl[0][:, sl[1]:sl[1] + 512],
                            wt[0:wk, ko * 1024 + m * 128: ko * 1024 + m * 128 + 128],
                            mid[0:wk, kc * 1024 + sp * 512: kc * 1024 + sp * 512 + 512],
                            start=(kc == 0), stop=(kc == NHC - 1)),
                            waits=(tok_w2 + tok_mid + pw_waits(bb2)) if kc == 0 else [])
                    pw_done(bb2, mmt)
                    ev = p.v(lambda e, m=m, sp=sp, hlo=hlo, sl=sl: e.scalar_tensor_tensor(
                        out=ych(m, hlo + sp * 512, 512), in0=sl[0][:, sl[1]:sl[1] + 512],
                        scalar=b2_t[:, m:m + 1], in1=ych(m, hlo + sp * 512, 512),
                        op0=ALU.add, op1=ALU.add), waits=[mmt] + tok_b2 + tok_y2)
                    pr_done(bb2, ev)
                    new_y2.append(ev)
            tok_y2 = tok_y2 + new_y2
            if dbg_stop == "glu" + str(hh):
                return ("DBG", [("dbg_y", y)], tok_y2)
            ar.free(f"w2a_{l}{hh}")
            ar.free(f"w2b_{l}{hh}")
            ar.free(f"mid{l}_{hh}")
            p.barrier()

        ar.free(f"hln{l}")
        p.barrier()
        return tok_y2

    if upto.startswith("ffn1_"):
        r = ffn_layer(0, tok_p0, dbg_stop=upto.split("_")[1])
        _, tens, toks = r
        dbg_toks = []
        for nm, h in tens:
            d = nc.dram_tensor(nm, [128, h.shape[1]], h.dtype, kind="ExternalOutput").ap()
            p.ydma(lambda e, d=d, h=h: e.dma_start(out=d, in_=h[:, :]),
                   waits=toks, chain="dbg")
        p.emit("sync", lambda e: e.nop(), waits=[p.chain_total("dbg")], inc=None)
        with nc.Block() as block:
            p.replay(nc, block)
        return nc

    y_ready = ffn_layer(0, tok_p0)
    if upto == "ffn1":
        dbg = nc.dram_tensor("dbg_y1", [128, NCH * (T + 1)], F32, kind="ExternalOutput").ap()
        tk = p.ydma(lambda e: e.dma_start(out=dbg, in_=y[:, :]), waits=y_ready, chain="dbg")
        p.emit("sync", lambda e: e.nop(), waits=[tk], inc=None)
        with nc.Block() as block:
            p.replay(nc, block)
        return nc
    y_ready = ffn_layer(1, y_ready)
    ar.free("rowbuf_g")
    p.barrier()

    sg_scr = ar.alloc("sg_scr", [128, 2 * T], F32)
    s_ld = ar.alloc("s_ld", [128, 4 * T], F32)
    x_ld = ar.alloc("x_ld", [128, 4 * D], F32)
    o_st = ar.alloc("o_st", [128, 4 * D], F32)

    s_rd = [None] * 4
    sg_rd = [None, None]
    tok_sg = []
    for c in range(NCH):
        par4 = c % 4
        par = c % 2
        t_s = p.ydma(lambda e, c=c, par4=par4: e.dma_start(
            out=s_ld[:, par4 * T:(par4 + 1) * T], in_=s_hbm[c * 128:(c + 1) * 128, :]),
            waits=[s_rd[par4]] + tok_s_done, chain=f"sr{par4}")
        t_si = p.s(lambda e, c=c, par=par: e.activation(
            out=sg_scr[:, par * T:(par + 1) * T], in_=ych(c), func=AF.Silu,
            bias=0.0, scale=1.0),
            waits=y_ready + [sg_rd[par]])
        t_m = (p.g if c % 2 == 0 else p.v)(lambda e, c=c, par=par, par4=par4: e.tensor_tensor(
            out=ych(c), in0=sg_scr[:, par * T:(par + 1) * T],
            in1=s_ld[:, par4 * T:(par4 + 1) * T], op=ALU.mult),
            waits=[t_si, t_s])
        s_rd[par4] = t_m
        sg_rd[par] = t_m
        tok_sg.append(t_m)

    x_rd = [None] * 4
    o_wr = [None] * 4
    out_toks = []
    for tt in range(NTT):
        par = tt % 4
        t_x = p.ydma(lambda e, tt=tt, par=par: e.dma_start(
            out=x_ld[:, par * D:(par + 1) * D], in_=x_in[tt * 128:(tt + 1) * 128, :]),
            waits=[x_rd[par]], chain=f"xs{par % 2}")
        x_rd[par] = t_x
        ev_toks = []
        for c in range(NCH):
            bk = c % 4
            b = bank_of(P1, bk * 512)
            pt = P1[:, bk * 512: bk * 512 + 128]
            t_tr = p.t(lambda e, c=c, tt=tt, pt=pt: e.transpose(
                pt, ych(c, tt * 128, 128), id_f32[:, :]),
                waits=[tok_sg[c]] + pw_waits([b]) + [o_wr[par]])
            pw_done([b], t_tr)
            eng = p.v if c % 2 == 0 else p.v
            ev = eng(lambda e, c=c, par=par, pt=pt: e.tensor_tensor(
                out=o_st[:, par * D + c * 128: par * D + c * 128 + 128], in0=pt,
                in1=x_ld[:, par * D + c * 128: par * D + c * 128 + 128], op=ALU.add),
                waits=[t_tr, t_x])
            pr_done([b], ev)
            ev_toks.append(ev)
        o_wr[par] = p.ydma(lambda e, tt=tt, par=par: e.dma_start(
            out=out_t[tt * 128:(tt + 1) * 128, :], in_=o_st[:, par * D:(par + 1) * D]),
            waits=ev_toks, chain=f"ow{par}")
        out_toks.append(o_wr[par])
    p.emit("sync", lambda e: e.nop(), waits=out_toks, inc=None)

    with nc.Block() as block:
        p.replay(nc, block)
    return nc



_REN = {
    "W_kv": "L0_attn_W_kv", "g_kv": "L0_attn_g_kv", "W_kdec": "L0_attn_W_kdec",
    "W_vdec": "L0_attn_W_vdec", "W_q": "L0_attn_W_q", "W_qdec": "L0_attn_W_qdec",
    "W_krope": "L0_attn_W_krope", "W_qrope": "L0_attn_W_qrope", "W_o": "L0_attn_W_o",
    "wqkv": "L0_ffn_wqkv", "q_scale": "L0_ffn_q_scale", "k_scale": "L0_ffn_k_scale",
    "wo": "L0_ffn_wo", "ln_g": "L0_ffn_ln_g", "ln_b": "L0_ffn_ln_b",
    "w1": "L0_ffn_w1", "b1": "L0_ffn_b1", "w2": "L0_ffn_w2", "b2": "L0_ffn_b2",
}


def kernel(**inputs):
    consts = _host_constants()
    nc = build_program(_DEV_UPTO)
    in_maps = []
    for core in range(8):
        m = {"x": np.ascontiguousarray(np.asarray(inputs["L0_attn_x"][core], dtype=np.float32))}
        for k, src in _REN.items():
            a = np.asarray(inputs[src], dtype=np.float32)
            if k == "g_kv":
                a = a.reshape(1, -1)
            m[k] = np.ascontiguousarray(a)
        m.update(consts)
        in_maps.append(m)
    res = run_bass_kernel_spmd(nc, in_maps, core_ids=list(range(8)))
    return np.stack([np.asarray(res.results[c]["out"]) for c in range(8)], axis=0)
